# version 08
# baseline (speedup 1.0000x reference)
import numpy as np
import concourse.bass as bass
import concourse.mybir as mybir
from concourse.bass_utils import run_bass_kernel_spmd
from concourse.alu_op_type import AluOpType as op
from contextlib import ExitStack

F32 = mybir.dt.float32
BF16 = mybir.dt.bfloat16
AF = mybir.ActivationFunctionType
AX = mybir.AxisListType.X

P = 128
D = 2048
H = 16
T = 2048
NT = 16
NDC = 16
NH = 8192
NHC = 64
QKV = 6144
SCALE = float(128) ** -0.5
NCORES = 8
KB = 1024


def build_program():
    nc = bass.Bass("TRN2", target_bir_lowering=False)

    x_d = nc.dram_tensor("x", [T, D], F32, kind="ExternalInput").ap()
    wqkv_d = nc.dram_tensor("wqkv_t", [D, QKV], F32, kind="ExternalInput").ap()
    wproj_d = nc.dram_tensor("wproj_t", [D, D], F32, kind="ExternalInput").ap()
    wg_d = nc.dram_tensor("wg_t", [D, NH], F32, kind="ExternalInput").ap()
    wu_d = nc.dram_tensor("wu_t", [D, NH], F32, kind="ExternalInput").ap()
    wd_d = nc.dram_tensor("wd_t", [NH, D], F32, kind="ExternalInput").ap()
    vrows_d = nc.dram_tensor("vrows", [1, 3 * D], F32, kind="ExternalInput").ap()
    srow_d = nc.dram_tensor("srow", [1, H], F32, kind="ExternalInput").ap()
    cid_d = nc.dram_tensor("cid", [P, P], BF16, kind="ExternalInput").ap()
    conesrow_d = nc.dram_tensor("conesrow", [1, P], F32, kind="ExternalInput").ap()
    conescol_d = nc.dram_tensor("conescol", [P, 1], F32, kind="ExternalInput").ap()
    ceps_d = nc.dram_tensor("ceps", [P, 4], F32, kind="ExternalInput").ap()
    csel_d = nc.dram_tensor("csel", [P, NCORES], F32, kind="ExternalInput").ap()
    y_d = nc.dram_tensor("y", [T, D], F32, kind="ExternalOutput").ap()

    xnfm_s = nc.dram_tensor("xnfm_s", [NDC, P, T], BF16, kind="Internal")
    qf_s = nc.dram_tensor("qf_s", [H, P, T], BF16, kind="Internal")
    kf_s = nc.dram_tensor("kf_s", [H, P, T], BF16, kind="Internal")
    kl_s = nc.dram_tensor("kl_s", [H, P, T], BF16, kind="Internal")
    vv_s = nc.dram_tensor("vv_s", [H, P, T], BF16, kind="Internal")
    o_s = nc.dram_tensor("o_s", [NT, P, D], F32, kind="Internal")
    part_s = nc.dram_tensor("part_s", [NT, P, D], F32, kind="Internal")
    gated_s = nc.dram_tensor("gated_s", [NHC, P, T], BF16, kind="Internal")
    cc_in = nc.dram_tensor("cc_in", [H * P, P], F32, kind="Internal")
    cc_out = nc.dram_tensor("cc_out", [NCORES * H * P, P], F32, kind="Internal")

    stck = ExitStack()
    e = stck.enter_context
    arena = e(nc.sbuf_tensor("arena", [P, 207 * KB], mybir.dt.uint8))
    abf_h = arena.bitcast(BF16)
    af_h = arena.bitcast(F32)

    def f32at(off_b, cols):
        assert off_b % 4 == 0 and off_b + 4 * cols <= 207 * KB
        return af_h[:, off_b // 4: off_b // 4 + cols]

    def bfat(off_b, cols):
        assert off_b % 2 == 0 and off_b + 2 * cols <= 207 * KB
        return abf_h[:, off_b // 2: off_b // 2 + cols]

    c_eps = f32at(0, 4)
    sel_rep = f32at(32, 8)
    iota_col = f32at(64, 1)
    ones_col = f32at(96, 1)
    qs3 = f32at(128, 3)
    ps3 = f32at(160, 3)
    stat = f32at(192, 16)
    lamT_col = f32at(480, 16)
    lam_rev_col = f32at(544, 16)
    neg_s_rep = f32at(608, 16)
    qabs_strip = f32at(672, 48)
    pabs_strip = f32at(864, 16)
    row3 = f32at(928, 3)
    s_row = f32at(960, 16)
    s_neg_row = f32at(1024, 16)
    ones_row = f32at(1088, 128)
    rev_row = f32at(1600, 128)
    idty = bfat(2112, 128)
    iota_rep = f32at(2432, 128)
    mtmp = f32at(2944, 128)
    diffc = f32at(3456, 128)
    tri_s = f32at(3968, 128)
    vrow_buf = f32at(4480, D)
    strips = f32at(12800, 128)
    MISC_END = 13312

    A_ = MISC_END
    mask_sc = f32at(A_, D)
    corrfacA = f32at(A_, D)
    factor_fm = f32at(A_ + 8 * KB, D)
    corrfacB = f32at(A_ + 8 * KB, D)
    lamT_fm = f32at(A_ + 16 * KB, D)
    S_t = f32at(A_ + 24 * KB, D)
    ofull = f32at(A_ + 24 * KB, D)
    S_bf = bfat(A_ + 32 * KB, D)
    S_init_bf = bfat(A_ + 32 * KB, D)
    wqkv_rep = f32at(A_ + 36 * KB, D)
    pq_a = bfat(A_ + 36 * KB, D)
    pq_b = bfat(A_ + 40 * KB, D)
    S_init = f32at(A_ + 36 * KB, D)
    wnorm_rep = f32at(A_ + 44 * KB, D)
    Stmp = f32at(A_ + 44 * KB, D)
    wprojn_rep = f32at(A_ + 52 * KB, D)
    scores_bf = bfat(A_ + 52 * KB, D)
    qlam_bf = bfat(A_ + 56 * KB, D)
    PERS_END = A_ + 60 * KB

    BIG = PERS_END
    h_fm = bfat(BIG, NDC * T)
    wp_bf = bfat(BIG, NDC * D)
    xn_fm = bfat(BIG, NDC * T)
    wd_bf = bfat(BIG, NHC * 512)
    wd_bf2 = bfat(9216, NHC * 512)
    TA = BIG + 64 * KB
    assert TA + 57 * KB <= 207 * KB

    xbuf = [f32at(TA, D), f32at(TA + 8 * KB, D)]
    xn_t = [f32at(TA + 16 * KB, D), f32at(TA + 24 * KB, D)]
    hbf_t = [bfat(TA + 32 * KB, D), bfat(TA + 36 * KB, D)]
    xnbf_t = [bfat(TA + 40 * KB, D), bfat(TA + 44 * KB, D)]
    xnfm_stage = bfat(TA + 48 * KB, D)
    qabs_buf = [f32at(TA + 52 * KB, D), f32at(TA + 60 * KB, D)]
    wcol = [f32at(TA, D), f32at(TA + 8 * KB, D)]
    wq_bf = [bfat(TA + 16 * KB, D), bfat(TA + 20 * KB, D)]
    qkv_stage = [bfat(TA + 24 * KB, D), bfat(TA + 28 * KB, D)]
    t_stage = [bfat(TA + 32 * KB, D), bfat(TA + 36 * KB, D)]
    a_bf = bfat(TA + 40 * KB, D)
    b_bf = bfat(TA + 44 * KB, D)
    pabs_p2 = f32at(TA + 48 * KB, D)
    qb = [bfat(TA, D), bfat(TA + 4 * KB, D)]
    kb = [bfat(TA + 8 * KB, D), bfat(TA + 12 * KB, D)]
    klb = [bfat(TA + 16 * KB, D), bfat(TA + 20 * KB, D)]
    vb = [bfat(TA + 24 * KB, D), bfat(TA + 28 * KB, D)]
    o_stage = [f32at(TA + 32 * KB, D), f32at(TA + 40 * KB, D)]
    wpcol = f32at(TA + 48 * KB, D)
    gthr = [f32at(TA, D), f32at(TA + 8 * KB, D)]
    obuf = [f32at(TA, D), f32at(TA + 8 * KB, D)]
    qbuf4 = [bfat(TA + 16 * KB, D), bfat(TA + 20 * KB, D)]
    xbuf4 = f32at(TA + 24 * KB, D)
    pstage = [f32at(TA + 32 * KB, D), f32at(TA + 40 * KB, D)]
    qcorr = bfat(TA + 48 * KB, D)
    o2stage = bfat(TA + 52 * KB, D)
    o2stage2 = bfat(TA + 60 * KB, D)
    o2fm = bfat(TA + 64 * KB, D)
    wgb = [bfat(TA, D), bfat(TA + 4 * KB, D)]
    wub = [bfat(TA + 8 * KB, D), bfat(TA + 12 * KB, D)]
    silu_t = [f32at(TA + 16 * KB, 512), f32at(TA + 18 * KB, 512)]
    gstage = [bfat(TA + 20 * KB, D), bfat(TA + 24 * KB, D)]
    gbuf = [bfat(TA, NHC * 128), bfat(TA + 16 * KB, NHC * 128),
            bfat(TA + 32 * KB, NHC * 128)]
    partbuf = [f32at(TA + 48 * KB, 512), f32at(TA + 50 * KB, 512)]
    outstage = [f32at(TA + 52 * KB, 512), f32at(TA + 54 * KB, 512)]

    pb = [e(nc.psum_tensor(f"pb{i}", [P, 512], F32)) for i in range(8)]
    pbbf = [t.bitcast(BF16) for t in pb]

    csem = {n: e(nc.semaphore(f"s_{n}")) for n in ["pe", "ve", "se", "po"]}
    ccnt = {n: 0 for n in csem}
    E = {k: [] for k in ["sync", "tensor", "vector", "scalar", "gpsimd"]}
    PHASE = ["P0"]

    def set_phase(name):
        PHASE[0] = name

    def emit(eng, fn):
        E[eng].append((PHASE[0], fn))

    class Stream:

        def __init__(self, name):
            self.sem = e(nc.semaphore(name))
            self.count = 0

        def dma(self, eng, fn):
            self.count += 16
            emit(eng, lambda en, fn=fn, s=self.sem: fn(en).then_inc(s, 16))
            return (self, self.count)

        def bump(self, eng, fn, amount):
            self.count += amount
            emit(eng, lambda en, fn=fn, s=self.sem, a=amount: fn(en).then_inc(s, a))
            return (self, self.count)

    def waitt(eng, tok):
        if tok is None:
            return
        st2, v = tok
        if v > 0:
            emit(eng, lambda en, s=st2.sem, v=v: en.wait_ge(s, v))

    def wait(eng, sm, val):
        if val and val > 0:
            emit(eng, lambda en, s=csem[sm], v=val: en.wait_ge(s, v))

    def drn(eng):
        emit(eng, lambda en: en.drain())

    def run(eng, sm, fn):
        ccnt[sm] += 1
        v = ccnt[sm]
        emit(eng, lambda en, fn=fn, s=csem[sm]: fn(en).then_inc(s, 1))
        return v

    def run_q(eng, fn):
        emit(eng, lambda en, fn=fn: fn(en))

    def bar(eng):
        for sm in ("pe", "ve", "se", "po"):
            wait(eng, sm, ccnt[sm])

    S_misc = Stream("d_misc")
    S_qabs = [Stream("d_qabs0"), Stream("d_qabs1")]
    S_x1 = [Stream("d_x10"), Stream("d_x11")]
    S_wq = [Stream("d_wq0"), Stream("d_wq1")]
    S_pab = Stream("d_pab")
    S_scan = [Stream("d_scan0"), Stream("d_scan1")]
    S_pq = Stream("d_pq")
    S_gth = [Stream("d_gth0"), Stream("d_gth1")]
    S_row = Stream("d_row")
    S_o4 = [Stream("d_o40"), Stream("d_o41")]
    S_q4 = [Stream("d_q40"), Stream("d_q41")]
    S_x4 = Stream("d_x4")
    S_gb = [Stream("d_gb0"), Stream("d_gb1"), Stream("d_gb2")]
    S_pp = [Stream("d_pp0"), Stream("d_pp1")]
    S_out = [Stream("d_out0"), Stream("d_out1")]
    S_xn = Stream("d_xn")
    G_xnf = Stream("g_xnf")
    G_st = [Stream("g_st0"), Stream("g_st1")]
    G_ts = [Stream("g_ts0"), Stream("g_ts1")]
    G_ow = [Stream("g_ow0"), Stream("g_ow1")]
    G_cc = Stream("g_cc")
    G_ccd = Stream("g_ccd")
    G_pw = [Stream("g_pw0"), Stream("g_pw1")]
    G_wgu = [Stream("g_wgu0"), Stream("g_wgu1")]
    G_gw = [Stream("g_gw0"), Stream("g_gw1")]
    G_wd = [Stream("g_wd0"), Stream("g_wd1")]

    S_misc.dma("sync", lambda s: s.dma_start(out=c_eps[:], in_=ceps_d))
    S_misc.dma("sync", lambda s: s.dma_start(out=sel_rep[:], in_=csel_d))
    S_misc.dma("sync", lambda s: s.dma_start(out=ones_col[:], in_=conescol_d))
    S_misc.dma("sync", lambda s: s.dma_start(out=s_row[0:1, :], in_=srow_d))
    S_misc.dma("sync", lambda s: s.dma_start(out=ones_row[0:1, :], in_=conesrow_d))
    S_misc.dma("sync", lambda s: s.dma_start(out=idty[:], in_=cid_d))
    misc_tok = S_misc.dma("sync", lambda s: s.dma_start(
        out=vrow_buf[0:1, :], in_=vrows_d[0:1, 0:D]))

    io_c = run("gpsimd", "po", lambda g: g.iota(
        iota_col[:], [[1, 1]], base=0, channel_multiplier=1,
        allow_small_or_imprecise_dtypes=True))
    io_r = run("gpsimd", "po", lambda g: g.iota(
        iota_rep[:], [[1, 128]], base=0, channel_multiplier=0,
        allow_small_or_imprecise_dtypes=True))
    io_rev = run("gpsimd", "po", lambda g: g.iota(
        rev_row[0:1, :], [[-1, 128]], base=128, channel_multiplier=0,
        allow_small_or_imprecise_dtypes=True))
    ms_S = run("gpsimd", "po", lambda g: g.memset(S_t[:], 0.0))
    ms_Sbf = run("gpsimd", "po", lambda g: g.memset(S_bf[:], 0.0))

    waitt("vector", misc_tok)
    v_sneg = run("vector", "ve", lambda v: v.tensor_scalar(
        s_neg_row[0:1, :], s_row[0:1, :], -1.0, None, op.mult))

    waitt("tensor", misc_tok)
    wait("tensor", "ve", v_sneg)
    pe1 = run("tensor", "pe", lambda t: t.matmul(
        pb[7][:, 0:16], ones_row[0:1, :], s_neg_row[0:1, :], start=True, stop=True))
    wait("vector", "pe", pe1)
    v_negs = run("vector", "ve", lambda v: v.tensor_copy(neg_s_rep[:], pb[7][:, 0:16]))
    wait("tensor", "ve", v_negs)
    wait("tensor", "po", io_rev)
    pe2 = run("tensor", "pe", lambda t: t.matmul(
        pb[7][:, 0:16], rev_row[0:1, :], s_neg_row[0:1, :], start=True, stop=True))
    wait("scalar", "pe", pe2)
    se_lrev = run("scalar", "se", lambda s: s.activation(
        lam_rev_col[:], pb[7][:, 0:16], AF.Exp))
    se_lamT = run("scalar", "se", lambda s: s.activation(
        lamT_col[:], neg_s_rep[:], AF.Exp, scale=128.0))
    wait("tensor", "se", se_lrev)
    for j in range(4):
        pv = run("tensor", "pe", lambda t, j=j: t.matmul(
            pb[7][:, 0:512], ones_row[0:1, :], vrow_buf[0:1, j * 512:(j + 1) * 512],
            start=True, stop=True))
        wait("vector", "pe", pv)
        vv = run("vector", "ve", lambda v, j=j: v.tensor_copy(
            wqkv_rep[:, j * 512:(j + 1) * 512], pb[7][:, 0:512]))
        wait("tensor", "ve", vv)

    wait("vector", "po", io_r)
    wait("vector", "po", io_c)
    v_dif = run("vector", "ve", lambda v: v.tensor_scalar(
        diffc[:], iota_rep[:], iota_col[:], 0.0, op.subtract, op.max))
    run("vector", "ve", lambda v: v.tensor_scalar(
        tri_s[:], iota_rep[:], iota_col[:], 0.0, op.subtract, op.is_ge))
    drn("vector")
    v_tris = run("vector", "ve", lambda v: v.tensor_scalar(
        tri_s[:], tri_s[:], SCALE, None, op.mult))
    wait("scalar", "ve", v_dif)
    for h in range(H):
        se_m = run("scalar", "se", lambda s, h=h: s.activation(
            mtmp[:], diffc[:], AF.Exp, scale=neg_s_rep[:, h:h + 1]))
        wait("vector", "se", se_m)
        if h == 0:
            wait("vector", "ve", v_tris)
        vm = run("vector", "ve", lambda v, h=h: v.tensor_tensor(
            mask_sc[:, h * 128:(h + 1) * 128], mtmp[:], tri_s[:], op.mult))
        wait("scalar", "ve", vm)
    for h in range(H):
        se_f = run("scalar", "se", lambda s, h=h: s.activation(
            mtmp[:], iota_rep[:], AF.Exp, scale=neg_s_rep[:, h:h + 1]))
        wait("vector", "se", se_f)
        vf = run("vector", "ve", lambda v, h=h: v.tensor_scalar(
            factor_fm[:, h * 128:(h + 1) * 128], mtmp[:], SCALE, None, op.mult))
        wait("scalar", "ve", vf)
    wait("vector", "se", se_lamT)
    for h in range(H):
        run("vector", "ve", lambda v, h=h: v.tensor_scalar(
            lamT_fm[:, h * 128:(h + 1) * 128], iota_rep[:], 0.0,
            lamT_col[:, h:h + 1], op.mult, op.add))

    set_phase("P1")
    wq_view = wqkv_d.rearrange("(dc p) o -> p dc o", p=P)
    abs_red = []

    def emit_abs(i):
        par = i % 2
        buf = qabs_buf[par]
        if i >= 2:
            wait("sync", abs_red[i - 2][0], abs_red[i - 2][1])
        tok = S_qabs[par].dma("sync", lambda s, i=i, buf=buf: s.dma_start(
            out=buf[:], in_=wq_view[:, :, i * 128:(i + 1) * 128]))
        if par == 0:
            waitt("scalar", tok)
            abs_red.append(("se", run("scalar", "se", lambda sc, i=i, buf=buf: sc.activation(
                buf[:], buf[:], AF.Abs, accum_out=qabs_strip[:, i:i + 1]))))
        else:
            waitt("vector", tok)
            abs_red.append(("ve", run("vector", "ve", lambda v, i=i, buf=buf: v.reduce_sum(
                qabs_strip[:, i:i + 1], buf[:], axis=AX, apply_absolute_value=True))))

    x_view = x_d.rearrange("(tc p) d -> tc p d", p=P)
    xnfm_view = xnfm_s.ap().rearrange("dc p t -> p dc t")

    def hfm_dst(tc, dc0):
        base = BIG // 2 + dc0 * T + tc * 128
        return bass.AP(abf_h, base, [[(207 * KB) // 2, P], [T, 8], [1, 128]])

    waitt("scalar", misc_tok)
    sumst = strips[:, 0:16]
    sqst = strips[:, 16:32]
    mst = strips[:, 32:48]
    vast = strips[:, 48:64]
    rst = strips[:, 64:80]
    dst_ = strips[:, 80:96]
    m2st = strips[:, 96:112]
    r2st = strips[:, 112:128]
    p1_sq = [0, 0]
    p1_sum = [0, 0]
    for tc in range(NT):
        par = tc % 2
        if tc >= 2:
            wait("sync", "ve", p1_sum[par])
            wait("sync", "se", p1_sq[par])
        x_tok = S_x1[par].dma("sync", lambda s, tc=tc, par=par: s.dma_start(
            out=xbuf[par][:], in_=x_view[tc, :, :]))
        waitt("vector", x_tok)
        p1_sum[par] = run("vector", "ve", lambda v, tc=tc, par=par: v.reduce_sum(
            sumst[:, tc:tc + 1], xbuf[par][:], axis=AX))
        waitt("scalar", x_tok)
        p1_sq[par] = run("scalar", "se", lambda s, tc=tc, par=par: s.activation(
            xn_t[par][:], xbuf[par][:], AF.Square, accum_out=sqst[:, tc:tc + 1]))
        emit_abs(tc)
    wait("vector", "se", p1_sq[0])
    wait("vector", "se", p1_sq[1])
    drn("vector")
    run("vector", "ve", lambda v: v.tensor_scalar(
        mst[:], sumst[:], 1.0 / D, None, op.mult))
    drn("vector")
    vch = run("vector", "ve", lambda v: v.tensor_scalar(
        m2st[:], mst[:], None, None, op.bypass) if False else v.tensor_tensor(
        m2st[:], mst[:], mst[:], op.mult))
    drn("vector")
    run("vector", "ve", lambda v: v.scalar_tensor_tensor(
        vast[:], sqst[:], 1.0 / D, m2st[:], op.mult, op.subtract))
    drn("vector")
    run("vector", "ve", lambda v: v.tensor_scalar(
        dst_[:], vast[:], 1e-5, None, op.add))
    drn("vector")
    run("vector", "ve", lambda v: v.reciprocal(m2st[:], dst_[:]))
    drn("vector")
    v_mean2 = run("vector", "ve", lambda v: v.tensor_tensor(
        r2st[:], vast[:], m2st[:], op.mult))
    wait("scalar", "ve", v_mean2)
    run("scalar", "se", lambda s: s.activation(
        rst[:], dst_[:], AF.Ln))
    drn("scalar")
    run("scalar", "se", lambda s: s.activation(
        rst[:], rst[:], AF.Exp, scale=-0.5))
    drn("scalar")
    run("scalar", "se", lambda s: s.activation(
        r2st[:], r2st[:], AF.Ln, bias=c_eps[:, 1:2]))
    drn("scalar")
    se_chain = run("scalar", "se", lambda s: s.activation(
        r2st[:], r2st[:], AF.Exp, scale=-0.5))
    p1_xfree = [0, 0]
    p1_tp_last = [0, 0]
    p1_banklast = 0
    p1_selast = 0
    p1_dma = []
    wait("vector", "se", se_chain)
    for tc in range(NT):
        par = tc % 2
        if tc < 2:
            wait("sync", "ve", p1_sum[par])
            wait("sync", "se", p1_sq[par])
        else:
            wait("sync", "ve", p1_xfree[par])
        x_tok = S_x1[par].dma("sync", lambda s, tc=tc, par=par: s.dma_start(
            out=xbuf[par][:], in_=x_view[tc, :, :]))
        waitt("vector", x_tok)
        if p1_tp_last[par]:
            wait("vector", "pe", p1_tp_last[par])
        v_xn = run("vector", "ve", lambda v, tc=tc, par=par: v.tensor_scalar(
            xn_t[par][:], xbuf[par][:], mst[:, tc:tc + 1], rst[:, tc:tc + 1],
            op.subtract, op.mult))
        p1_xfree[par] = v_xn
        drn("vector")
        v_h = run("vector", "ve", lambda v, tc=tc, par=par: v.scalar_tensor_tensor(
            hbf_t[par][:], xn_t[par][:], r2st[:, tc:tc + 1], wqkv_rep[:],
            op.mult, op.mult))
        drn("vector")
        v_xnb = run("vector", "ve", lambda v, par=par: v.tensor_copy(
            xnbf_t[par][:], xn_t[par][:]))
        wait("tensor", "ve", v_xnb)
        if tc == 0:
            waitt("tensor", misc_tok)
        if p1_banklast:
            wait("tensor", "se", p1_banklast)
        if p1_selast:
            wait("tensor", "se", p1_selast)
        for i in range(8):
            run_q("tensor", lambda t, i=i, par=par: t.transpose(
                pbbf[0][:, i * 128:(i + 1) * 128], xnbf_t[par][:, i * 128:(i + 1) * 128], idty[:]))
        for i in range(8):
            run_q("tensor", lambda t, i=i, par=par: t.transpose(
                pbbf[1][:, i * 128:(i + 1) * 128], xnbf_t[par][:, (8 + i) * 128:(9 + i) * 128], idty[:]))
        for i in range(8):
            run_q("tensor", lambda t, i=i, par=par: t.transpose(
                pbbf[2][:, i * 128:(i + 1) * 128], hbf_t[par][:, i * 128:(i + 1) * 128], idty[:]))
        for i in range(8):
            if i < 7:
                run_q("tensor", lambda t, i=i, par=par: t.transpose(
                    pbbf[3][:, i * 128:(i + 1) * 128], hbf_t[par][:, (8 + i) * 128:(9 + i) * 128], idty[:]))
            else:
                p1_tp_last[par] = run("tensor", "pe", lambda t, i=i, par=par: t.transpose(
                    pbbf[3][:, i * 128:(i + 1) * 128], hbf_t[par][:, (8 + i) * 128:(9 + i) * 128], idty[:]))
        wait("scalar", "pe", p1_tp_last[par])
        if tc >= 1:
            waitt("scalar", p1_dma[tc - 1])
        run("scalar", "se", lambda sc: sc.copy(xnfm_stage[:, 0:1024], pbbf[0][:, 0:1024]))
        p1_se_ev = run("scalar", "se", lambda sc: sc.copy(
            xnfm_stage[:, 1024:2048], pbbf[1][:, 0:1024]))
        p1_selast = p1_se_ev
        run("scalar", "se", lambda sc, tc=tc: sc.copy(hfm_dst(tc, 0), pbbf[2][:, 0:1024]))
        p1_banklast = run("scalar", "se", lambda sc, tc=tc: sc.copy(
            hfm_dst(tc, 8), pbbf[3][:, 0:1024]))
        wait("gpsimd", "se", p1_banklast)
        p1_dma.append(G_xnf.dma("gpsimd", lambda g, tc=tc: g.dma_start(
            out=xnfm_view[:, :, tc * 128:(tc + 1) * 128], in_=xnfm_stage[:])))
        emit_abs(16 + 2 * tc)
        emit_abs(17 + 2 * tc)

    wait("vector", abs_red[-1][0], abs_red[-1][1])
    wait("vector", abs_red[-2][0], abs_red[-2][1])
    drn("vector")
    v_qa = run("vector", "ve", lambda v: v.reduce_sum(
        stat[:, 12:13], qabs_strip[:], axis=AX))
    wait("tensor", "ve", v_qa)
    pe_qa = run("tensor", "pe", lambda t: t.matmul(
        pb[6][0:1, 0:1], stat[:, 12:13], ones_col[:], start=True, stop=True))
    wait("vector", "pe", pe_qa)
    run("vector", "ve", lambda v: v.tensor_scalar(
        row3[0:1, 0:1], pb[6][0:1, 0:1], 1.0 / (QKV * D), 1e-5, op.mult, op.max))
    drn("vector")
    run("vector", "ve", lambda v: v.reciprocal(row3[0:1, 1:2], row3[0:1, 0:1]))
    drn("vector")
    v_r3 = run("vector", "ve", lambda v: v.tensor_scalar(
        row3[0:1, 2:3], row3[0:1, 1:2], -1.0, None, op.mult))
    wait("tensor", "ve", v_r3)
    pe_q3 = run("tensor", "pe", lambda t: t.matmul(
        pb[6][:, 0:3], ones_row[0:1, :], row3[0:1, 0:3], start=True, stop=True))
    wait("vector", "pe", pe_q3)
    run("vector", "ve", lambda v: v.tensor_copy(qs3[:, 0:1], pb[6][:, 1:2]))
    run("vector", "ve", lambda v: v.tensor_copy(qs3[:, 1:2], pb[6][:, 2:3]))
    v_qs3 = run("vector", "ve", lambda v: v.tensor_copy(qs3[:, 2:3], pb[6][:, 0:1]))

    set_phase("P2")
    bar("sync")
    waitt("sync", (S_x1[0], S_x1[0].count))
    waitt("sync", (S_x1[1], S_x1[1].count))
    waitt("sync", (S_qabs[0], S_qabs[0].count))
    waitt("sync", (S_qabs[1], S_qabs[1].count))
    waitt("sync", (G_xnf, G_xnf.count))
    waitt("vector", (G_xnf, G_xnf.count))
    waitt("scalar", (G_xnf, G_xnf.count))
    wpv = wproj_d.rearrange("(dc p) o -> p dc o", p=P)
    evac_done = [0] * 4
    quant_done = [0, 0]
    wq_consumed = [0, 0]
    stage_dma = [None, None]
    stage_pe = [0, 0]
    tstage_done = 0
    tstage_eng = "ve"
    tstage_dma = [None, None]
    pabs_red = [0] * NDC
    w_toks = [None] * 48

    def p2_load(oc):
        buf = oc % 2
        if quant_done[buf]:
            wait("sync", "ve", quant_done[buf])
        w_toks[oc] = S_wq[buf].dma("sync", lambda s, oc=oc, buf=buf: s.dma_start(
            out=wcol[buf][:], in_=wq_view[:, :, oc * 128:(oc + 1) * 128]))

    def p2_quant(oc):
        buf = oc % 2
        waitt("vector", w_toks[oc])
        if oc == 0:
            wait("vector", "ve", v_qs3)
        if wq_consumed[buf]:
            wait("vector", "pe", wq_consumed[buf])
        run("vector", "ve", lambda v, buf=buf: v.tensor_scalar(
            a_bf[:], wcol[buf][:], qs3[:, 0:1], 0.5, op.mult, op.is_gt))
        run("vector", "ve", lambda v, buf=buf: v.tensor_scalar(
            b_bf[:], wcol[buf][:], qs3[:, 1:2], 0.5, op.mult, op.is_gt))
        drn("vector")
        quant_done[buf] = run("vector", "ve", lambda v, buf=buf: v.tensor_tensor(
            wq_bf[buf][:], a_bf[:], b_bf[:], op.subtract))

    p2_load(0)
    p2_load(1)
    p2_quant(0)
    for oc in range(48):
        buf = oc % 2
        if oc < NDC:
            if oc >= 1:
                wait("sync", "se", pabs_red[oc - 1])
            pa_tok = S_pab.dma("sync", lambda s, oc=oc: s.dma_start(
                out=pabs_p2[:], in_=wpv[:, oc, :]))
            waitt("scalar", pa_tok)
            pabs_red[oc] = run("scalar", "se", lambda sc, oc=oc: sc.activation(
                pabs_p2[:], pabs_p2[:], AF.Abs, accum_out=pabs_strip[:, oc:oc + 1]))
        wait("tensor", "ve", quant_done[buf])
        pe_tt = [0] * 4
        for tt in range(4):
            if evac_done[tt]:
                wait("tensor", "ve", evac_done[tt])
            for dc in range(NDC):
                fin = (dc == NDC - 1)
                mmf = lambda t, tt=tt, dc=dc, buf=buf, fin=fin: t.matmul(
                    pb[tt][:, 0:512], wq_bf[buf][:, dc * 128:(dc + 1) * 128],
                    h_fm[:, dc * T + tt * 512: dc * T + (tt + 1) * 512],
                    start=(dc == 0), stop=fin)
                if fin:
                    pe_tt[tt] = run("tensor", "pe", mmf)
                else:
                    run_q("tensor", mmf)
        wq_consumed[buf] = pe_tt[3]
        if oc + 2 < 48:
            p2_load(oc + 2)
        if stage_dma[buf]:
            waitt("vector", stage_dma[buf])
        if stage_pe[buf]:
            wait("vector", "pe", stage_pe[buf])
        wait("vector", "pe", pe_tt[0])
        ev_last = run("vector", "ve", lambda v, buf=buf: v.tensor_scalar(
            qkv_stage[buf][:, 0:512], pb[0][:, 0:512], qs3[:, 2:3], None, op.mult))
        evac_done[0] = ev_last
        if oc + 1 < 48:
            p2_quant(oc + 1)
        for tt in range(1, 4):
            wait("vector", "pe", pe_tt[tt])
            ev_last = run("vector", "ve", lambda v, tt=tt, buf=buf: v.tensor_scalar(
                qkv_stage[buf][:, tt * 512:(tt + 1) * 512], pb[tt][:, 0:512],
                qs3[:, 2:3], None, op.mult))
            evac_done[tt] = ev_last
        if oc < 32:
            tgt = qf_s.ap()[oc] if oc < 16 else kf_s.ap()[oc - 16]
            wait("gpsimd", "ve", ev_last)
            stage_dma[buf] = G_st[buf].dma("gpsimd", lambda g, tgt=tgt, buf=buf: g.dma_start(
                out=tgt, in_=qkv_stage[buf][:]))
        if oc >= 16:
            h_idx = oc - 16 if oc < 32 else oc - 32
            wait("tensor", "ve", ev_last)
            if tstage_done:
                wait("tensor", tstage_eng, tstage_done)
            tp_last = 0
            for i in range(8):
                run_q("tensor", lambda t, i=i, buf=buf: t.transpose(
                    pbbf[4][:, i * 128:(i + 1) * 128],
                    qkv_stage[buf][:, i * 128:(i + 1) * 128], idty[:]))
            for i in range(8):
                if i < 7:
                    run_q("tensor", lambda t, i=i, buf=buf: t.transpose(
                        pbbf[5][:, i * 128:(i + 1) * 128],
                        qkv_stage[buf][:, (8 + i) * 128:(9 + i) * 128], idty[:]))
                else:
                    tp_last = run("tensor", "pe", lambda t, i=i, buf=buf: t.transpose(
                        pbbf[5][:, i * 128:(i + 1) * 128],
                        qkv_stage[buf][:, (8 + i) * 128:(9 + i) * 128], idty[:]))
            stage_pe[buf] = tp_last
            ev_eng = "vector" if oc < 32 else "scalar"
            if tstage_dma[buf]:
                waitt(ev_eng, tstage_dma[buf])
            if oc < 32:
                wait("vector", "pe", tp_last)
                run("vector", "ve", lambda v, buf=buf, h_idx=h_idx: v.tensor_scalar(
                    t_stage[buf][:, 0:1024], pbbf[4][:, 0:1024],
                    lam_rev_col[:, h_idx:h_idx + 1], None, op.mult))
                tv = run("vector", "ve", lambda v, buf=buf, h_idx=h_idx: v.tensor_scalar(
                    t_stage[buf][:, 1024:2048], pbbf[5][:, 0:1024],
                    lam_rev_col[:, h_idx:h_idx + 1], None, op.mult))
                tstage_done, tstage_eng = tv, "ve"
                tgt2 = kl_s.ap()[h_idx]
                tv_eng = "ve"
            else:
                wait("scalar", "pe", tp_last)
                run("scalar", "se", lambda s, buf=buf: s.copy(
                    t_stage[buf][:, 0:1024], pbbf[4][:, 0:1024]))
                tv = run("scalar", "se", lambda s, buf=buf: s.copy(
                    t_stage[buf][:, 1024:2048], pbbf[5][:, 0:1024]))
                tstage_done, tstage_eng = tv, "se"
                tgt2 = vv_s.ap()[h_idx]
                tv_eng = "se"
            wait("gpsimd", tv_eng, tv)
            tstage_dma[buf] = G_ts[buf].dma("gpsimd", lambda g, tgt2=tgt2, buf=buf: g.dma_start(
                out=tgt2, in_=t_stage[buf][:]))

    wait("vector", "se", pabs_red[NDC - 1])
    v_pa = run("vector", "ve", lambda v: v.reduce_sum(
        stat[:, 13:14], pabs_strip[:], axis=AX))
    wait("tensor", "ve", v_pa)
    pe_pa = run("tensor", "pe", lambda t: t.matmul(
        pb[6][0:1, 0:1], stat[:, 13:14], ones_col[:], start=True, stop=True))
    wait("vector", "pe", pe_pa)
    run("vector", "ve", lambda v: v.tensor_scalar(
        row3[0:1, 0:1], pb[6][0:1, 0:1], 1.0 / (D * D), 1e-5, op.mult, op.max))
    drn("vector")
    run("vector", "ve", lambda v: v.reciprocal(row3[0:1, 1:2], row3[0:1, 0:1]))
    drn("vector")
    v_p3 = run("vector", "ve", lambda v: v.tensor_scalar(
        row3[0:1, 2:3], row3[0:1, 1:2], -1.0, None, op.mult))
    wait("tensor", "ve", v_p3)
    pe_p3 = run("tensor", "pe", lambda t: t.matmul(
        pb[6][:, 0:3], ones_row[0:1, :], row3[0:1, 0:3], start=True, stop=True))
    wait("vector", "pe", pe_p3)
    run("vector", "ve", lambda v: v.tensor_copy(ps3[:, 0:1], pb[6][:, 1:2]))
    run("vector", "ve", lambda v: v.tensor_copy(ps3[:, 1:2], pb[6][:, 2:3]))
    v_ps3 = run("vector", "ve", lambda v: v.tensor_copy(ps3[:, 2:3], pb[6][:, 0:1]))

    def strided_read(src, c):
        return src.ap().rearrange("h p t -> p h t")[:, :, c * 128:(c + 1) * 128]

    set_phase("SCAN")
    bar("sync")
    for s2 in (G_st[0], G_st[1], G_ts[0], G_ts[1]):
        waitt("sync", (s2, s2.count))
    bar("scalar")
    waitt("scalar", (G_ts[0], G_ts[0].count))
    waitt("scalar", (G_ts[1], G_ts[1].count))
    waitt("scalar", (G_st[0], G_st[0].count))
    waitt("scalar", (G_st[1], G_st[1].count))
    waitt("scalar", (S_wq[0], S_wq[0].count))
    waitt("scalar", (S_wq[1], S_wq[1].count))
    waitt("scalar", (S_pab, S_pab.count))
    waitt("sync", (S_wq[0], S_wq[0].count))
    waitt("sync", (S_wq[1], S_wq[1].count))
    waitt("sync", (S_pab, S_pab.count))
    scan_pe_done = []
    scan_qlam = []
    scan_supd = []
    scan_cast = []
    scan_oevac = []
    scan_kevac = []
    scan_odma = []
    pq_done = [0] * NDC
    pq_tok = [None] * NDC
    kv_sb = Stmp
    for c in range(NT):
        par = c % 2
        slo = 0 if par == 0 else 4
        obk = 4 if par == 0 else 0
        if c >= 2:
            wait("sync", "pe", scan_pe_done[c - 2])
            wait("sync", "ve", scan_qlam[c - 2])
            wait("scalar", "pe", scan_pe_done[c - 2])
        S_scan[par].dma("sync", lambda s, c=c, par=par: s.dma_start(
            out=qb[par][:], in_=strided_read(qf_s, c)))
        S_scan[par].dma("sync", lambda s, c=c, par=par: s.dma_start(
            out=kb[par][:], in_=strided_read(kf_s, c)))
        S_scan[par].dma("scalar", lambda s, c=c, par=par: s.dma_start(
            out=klb[par][:], in_=strided_read(kl_s, c)))
        ld_tok = S_scan[par].dma("scalar", lambda s, c=c, par=par: s.dma_start(
            out=vb[par][:], in_=strided_read(vv_s, c)))
        if c < NDC:
            if c >= 1:
                wait("sync", "ve", pq_done[c - 1])
            pq_tok[c] = S_pq.dma("sync", lambda s, c=c: s.dma_start(
                out=wpcol[:], in_=wproj_d[c * P:(c + 1) * P, :]))
        waitt("vector", ld_tok)
        scan_qlam.append(run("vector", "ve", lambda v, par=par: v.tensor_tensor(
            qlam_bf[:], qb[par][:], factor_fm[:], op.mult)))
        waitt("tensor", ld_tok)
        if c == 0:
            wait("tensor", "po", ms_Sbf)
        if c >= 1:
            wait("tensor", "se", scan_oevac[c - 1])
        if c >= 2:
            wait("tensor", "po", scan_supd[c - 2])
        pesc = 0
        for h in range(H):
            bank, sl = slo + h // 4, (h % 4) * 128
            fin = (h == H - 1)
            mmf = lambda t, h=h, bank=bank, sl=sl, par=par: t.matmul(
                pb[bank][:, sl:sl + 128], kb[par][:, h * 128:(h + 1) * 128],
                qb[par][:, h * 128:(h + 1) * 128], start=True, stop=True)
            if fin:
                pesc = run("tensor", "pe", mmf)
            else:
                run_q("tensor", mmf)
        wait("vector", "pe", pesc)
        vme = 0
        for bank in range(4):
            vme = run("vector", "ve", lambda v, bank=bank, slo=slo: v.tensor_tensor(
                scores_bf[:, bank * 512:(bank + 1) * 512], pb[slo + bank][:, 0:512],
                mask_sc[:, bank * 512:(bank + 1) * 512], op.mult))
        wait("tensor", "ve", vme)
        wait("tensor", "ve", scan_qlam[c])
        if c >= 1:
            wait("tensor", "ve", scan_cast[c - 1])
            wait("tensor", "po", scan_supd[c - 1])
        if c >= 2:
            wait("tensor", "se", scan_oevac[c - 2] if False else 0)
        peoc = 0
        for h in range(H):
            bank, sl = obk + h // 4, (h % 4) * 128
            run_q("tensor", lambda t, h=h, bank=bank, sl=sl, par=par: t.matmul(
                pb[bank][:, sl:sl + 128], scores_bf[:, h * 128:(h + 1) * 128],
                vb[par][:, h * 128:(h + 1) * 128], start=True, stop=False))
            fin = (h == H - 1)
            mmf = lambda t, h=h, bank=bank, sl=sl: t.matmul(
                pb[bank][:, sl:sl + 128], qlam_bf[:, h * 128:(h + 1) * 128],
                S_bf[:, h * 128:(h + 1) * 128], start=False, stop=True)
            if fin:
                peoc = run("tensor", "pe", mmf)
            else:
                run_q("tensor", mmf)
        pekv = 0
        for h in range(H):
            bank, sl = slo + h // 4, (h % 4) * 128
            fin = (h == H - 1)
            mmf = lambda t, h=h, bank=bank, sl=sl, par=par: t.matmul(
                pb[bank][:, sl:sl + 128], klb[par][:, h * 128:(h + 1) * 128],
                vb[par][:, h * 128:(h + 1) * 128], start=True, stop=True)
            if fin:
                pekv = run("tensor", "pe", mmf)
            else:
                run_q("tensor", mmf)
        scan_pe_done.append(pekv)
        wait("scalar", "pe", peoc)
        if c >= 2:
            waitt("scalar", scan_odma[c - 2])
        soe = 0
        for bank in range(4):
            soe = run("scalar", "se", lambda s, bank=bank, obk=obk, par=par: s.copy(
                o_stage[par][:, bank * 512:(bank + 1) * 512], pb[obk + bank][:, 0:512]))
        scan_oevac.append(soe)
        wait("gpsimd", "se", soe)
        scan_odma.append(G_ow[par].dma("gpsimd", lambda g, c=c, par=par: g.dma_start(
            out=o_s.ap()[c], in_=o_stage[par][:])))
        wait("scalar", "pe", pekv)
        if c >= 1:
            wait("scalar", "po", scan_supd[c - 1])
        ske = 0
        for bank in range(4):
            ske = run("scalar", "se", lambda s, bank=bank, slo=slo: s.copy(
                kv_sb[:, bank * 512:(bank + 1) * 512], pb[slo + bank][:, 0:512]))
        scan_kevac.append(ske)
        if c == 0:
            wait("gpsimd", "po", ms_S)
        if c >= 1:
            wait("gpsimd", "ve", scan_cast[c - 1])
        po1 = run("gpsimd", "po", lambda g: g.tensor_tensor(
            S_t[:], S_t[:], lamT_fm[:], op.mult))
        wait("gpsimd", "po", po1)
        wait("gpsimd", "se", ske)
        po2 = run("gpsimd", "po", lambda g: g.tensor_tensor(
            S_t[:], S_t[:], kv_sb[:], op.add))
        scan_supd.append(po2)
        wait("vector", "po", po2)
        scan_cast.append(run("vector", "ve", lambda v: v.tensor_copy(S_bf[:], S_t[:])))
        if c < NDC:
            waitt("vector", pq_tok[c])
            if c == 0:
                wait("vector", "ve", v_ps3)
            run("vector", "ve", lambda v: v.tensor_scalar(
                pq_a[:], wpcol[:], ps3[:, 0:1], 0.5, op.mult, op.is_gt))
            run("vector", "ve", lambda v: v.tensor_scalar(
                pq_b[:], wpcol[:], ps3[:, 1:2], 0.5, op.mult, op.is_gt))
            drn("vector")
            pq_done[c] = run("vector", "ve", lambda v, c=c: v.tensor_tensor(
                wp_bf[:, c * D:(c + 1) * D], pq_a[:], pq_b[:], op.subtract))

    set_phase("CC")
    wait("gpsimd", "po", scan_supd[-1])
    cc_tok = G_cc.dma("gpsimd", lambda g: g.dma_start(
        out=cc_in.ap().rearrange("(h d) e -> d h e", d=P), in_=S_t[:]))
    waitt("gpsimd", cc_tok)
    cc_done = G_ccd.bump("gpsimd", lambda g: g.collective_compute(
        "AllGather", op.bypass, replica_groups=[list(range(NCORES))],
        ins=[cc_in.ap().opt()], outs=[cc_out.ap().opt()]), 1)

    wait("tensor", "se", scan_oevac[-1])
    wait("tensor", "po", scan_supd[-1])
    rep_tgts = [wnorm_rep, wprojn_rep]
    pv = 0
    for r in range(2):
        if pv:
            wait("sync", "pe", pv)
        row_tok = S_row.dma("sync", lambda s, r=r: s.dma_start(
            out=vrow_buf[0:1, :], in_=vrows_d[0:1, (r + 1) * D:(r + 2) * D]))
        waitt("tensor", row_tok)
        for j in range(4):
            pv = run("tensor", "pe", lambda t, r=r, j=j: t.matmul(
                pb[7][:, 0:512], ones_row[0:1, :],
                vrow_buf[0:1, j * 512:(j + 1) * 512], start=True, stop=True))
            wait("vector", "pe", pv)
            if r == 0 and j == 0:
                wait("vector", "po", scan_supd[-1])
                wait("vector", "se", scan_kevac[-1])
                wait("vector", "ve", pq_done[-1])
            vv = run("vector", "ve", lambda v, r=r, j=j: v.tensor_copy(
                rep_tgts[r][:, j * 512:(j + 1) * 512], pb[7][:, 0:512]))
            wait("tensor", "ve", vv)

    waitt("vector", cc_done)
    run("vector", "ve", lambda v: v.memset(S_init[:], 0.0))
    waitt("sync", cc_done)
    bar("sync")
    waitt("sync", (S_scan[0], S_scan[0].count))
    waitt("sync", (S_scan[1], S_scan[1].count))
    waitt("sync", (S_pq, S_pq.count))
    si_done = []
    src = S_init
    for r in range(NCORES):
        if r >= 2:
            wait("sync", "ve", si_done[r - 2])
        g_tok = S_gth[r % 2].dma("sync", lambda s, r=r: s.dma_start(
            out=gthr[r % 2][:],
            in_=cc_out.ap()[r * H * P:(r + 1) * H * P, :].rearrange(
                "(h d) e -> d h e", d=P)))
        waitt("vector", g_tok)
        drn("vector")
        dst = S_t if r % 2 == 0 else S_init
        si_done.append(run("vector", "ve", lambda v, r=r, dst=dst, src=src: v.scalar_tensor_tensor(
            dst[:], gthr[r % 2][:], sel_rep[:, r:r + 1], src[:], op.mult, op.add)))
        src = dst
    drn("vector")
    v_sib = run("vector", "ve", lambda v: v.tensor_copy(S_init_bf[:], S_init[:]))

    set_phase("P4")
    v_cfi = run("vector", "ve", lambda v: v.tensor_copy(corrfacA[:], factor_fm[:]))
    wait("sync", "ve", v_sib)
    wait("scalar", "ve", v_sib)
    waitt("scalar", scan_odma[-1])
    waitt("scalar", scan_odma[-2])
    waitt("sync", scan_odma[-1])
    waitt("sync", scan_odma[-2])
    bar("sync")
    waitt("sync", (S_gth[0], S_gth[0].count))
    waitt("sync", (S_gth[1], S_gth[1].count))
    bar("scalar")
    waitt("scalar", (S_gth[0], S_gth[0].count))
    waitt("scalar", (S_gth[1], S_gth[1].count))
    cf_cur, cf_nxt = corrfacA, corrfacB
    p4_o2done = [0] * NT
    p4_qcwr = [0] * NT
    p4_projlast = [0] * NT
    p4_partlast = [0] * NT
    p4_o2evac = [0] * NT
    p4_corr = [0] * NT
    p4_vof = [0] * NT
    p4_tp = [0] * NT
    p4_pdma = [None] * NT
    ser3v = [0] * NT
    x_free = 0
    o2s = [o2stage, o2stage2]

    def p4_loads(tc):
        nonlocal_x = None
        par = tc % 2
        if tc >= 2:
            wait("sync", "ve", p4_o2done[tc - 2])
        lo = S_o4[par].dma("sync", lambda s, tc=tc, par=par: s.dma_start(
            out=obuf[par][:], in_=o_s.ap()[tc]))
        if tc >= 2:
            wait("scalar", "ve", p4_qcwr[tc - 2])
        lq = S_q4[par].dma("scalar", lambda s, tc=tc, par=par: s.dma_start(
            out=qbuf4[par][:], in_=strided_read(qf_s, tc)))
        return lo, lq

    def p4_qcorr(tc, toks):
        nonlocal cf_cur, cf_nxt
        par = tc % 2
        lo, lq = toks
        waitt("vector", lq)
        if tc == 0:
            wait("vector", "ve", v_sib)
            wait("vector", "ve", v_cfi)
        if tc >= 1:
            wait("vector", "pe", p4_corr[tc - 1])
        p4_qcwr[tc] = run("vector", "ve", lambda v, par=par, cf=cf_cur: v.tensor_tensor(
            qcorr[:], qbuf4[par][:], cf[:], op.mult))
        run("vector", "ve", lambda v, cf=cf_cur, cn=cf_nxt: v.tensor_tensor(
            cn[:], cf[:], lamT_fm[:], op.mult))
        cf_cur, cf_nxt = cf_nxt, cf_cur

    def p4_corr_mms(tc):
        wait("tensor", "ve", p4_qcwr[tc])
        if tc >= 1:
            wait("tensor", "ve", p4_vof[tc - 1])
        pec = 0
        for h in range(H):
            bank, sl = 4 + h // 4, (h % 4) * 128
            fin = (h == H - 1)
            mmf = lambda t, h=h, bank=bank, sl=sl: t.matmul(
                pb[bank][:, sl:sl + 128], qcorr[:, h * 128:(h + 1) * 128],
                S_init_bf[:, h * 128:(h + 1) * 128], start=True, stop=True)
            if fin:
                pec = run("tensor", "pe", mmf)
            else:
                run_q("tensor", mmf)
        p4_corr[tc] = pec

    def p4_chain(tc, toks):
        par = tc % 2
        lo, lq = toks
        waitt("vector", lo)
        wait("vector", "pe", p4_corr[tc])
        if tc >= 1:
            wait("vector", "se", ser3v[tc - 1])
        vof = 0
        for bank in range(4):
            vof = run("vector", "ve", lambda v, bank=bank, par=par: v.tensor_tensor(
                ofull[:, bank * 512:(bank + 1) * 512],
                obuf[par][:, bank * 512:(bank + 1) * 512], pb[4 + bank][:, 0:512], op.add))
        p4_vof[tc] = vof
        wait("scalar", "ve", vof)
        if tc >= 2:
            waitt("scalar", p4_pdma[tc - 2])
        run("scalar", "se", lambda s, par=par: s.activation(
            pstage[par][:], ofull[:], AF.Square, accum_out=stat[:, 14:15]))
        drn("scalar")
        run("scalar", "se", lambda s: s.activation(
            stat[:, 14:15], stat[:, 14:15], AF.Ln, scale=1.0 / D, bias=c_eps[:, 2:3]))
        drn("scalar")
        ser3 = run("scalar", "se", lambda s: s.activation(
            stat[:, 14:15], stat[:, 14:15], AF.Exp, scale=-0.5))
        ser3v[tc] = ser3
        wait("vector", "se", ser3)
        vo1 = run("vector", "ve", lambda v, par=par: v.scalar_tensor_tensor(
            obuf[par][:], ofull[:], stat[:, 14:15], wnorm_rep[:], op.mult, op.mult))
        wait("scalar", "ve", vo1)
        run("scalar", "se", lambda s, par=par: s.activation(
            pstage[par][:], obuf[par][:], AF.Square, accum_out=stat[:, 15:16]))
        drn("scalar")
        run("scalar", "se", lambda s: s.activation(
            stat[:, 15:16], stat[:, 15:16], AF.Ln, scale=1.0 / D, bias=c_eps[:, 1:2]))
        drn("scalar")
        ser4 = run("scalar", "se", lambda s: s.activation(
            stat[:, 15:16], stat[:, 15:16], AF.Exp, scale=-0.5))
        wait("vector", "se", ser4)
        if tc >= 2:
            wait("vector", "pe", p4_tp[tc - 2])
        vo2 = run("vector", "ve", lambda v, par=par: v.scalar_tensor_tensor(
            o2s[par][:], obuf[par][:], stat[:, 15:16], wprojn_rep[:], op.mult, op.mult))
        p4_o2done[tc] = vo2

    def p4_tp_proj(tc):
        par = tc % 2
        wait("tensor", "ve", p4_o2done[tc])
        if tc >= 1:
            wait("tensor", "ve", p4_partlast[tc - 1])
        tpl = 0
        for i in range(8):
            run_q("tensor", lambda t, i=i, par=par: t.transpose(
                pbbf[0][:, i * 128:(i + 1) * 128], o2s[par][:, i * 128:(i + 1) * 128], idty[:]))
        for i in range(8):
            if i < 7:
                run_q("tensor", lambda t, i=i, par=par: t.transpose(
                    pbbf[1][:, i * 128:(i + 1) * 128],
                    o2s[par][:, (8 + i) * 128:(9 + i) * 128], idty[:]))
            else:
                tpl = run("tensor", "pe", lambda t, i=i, par=par: t.transpose(
                    pbbf[1][:, i * 128:(i + 1) * 128],
                    o2s[par][:, (8 + i) * 128:(9 + i) * 128], idty[:]))
        p4_tp[tc] = tpl
        wait("vector", "pe", tpl)
        run("vector", "ve", lambda v: v.tensor_copy(o2fm[:, 0:1024], pbbf[0][:, 0:1024]))
        ev = run("vector", "ve", lambda v: v.tensor_copy(o2fm[:, 1024:2048], pbbf[1][:, 0:1024]))
        p4_o2evac[tc] = ev
        wait("tensor", "ve", ev)
        if tc == 0:
            wait("tensor", "ve", pq_done[-1])
        pep = [0] * 4
        for eg in range(4):
            for dc in range(NDC):
                fin = (dc == NDC - 1)
                mmf = lambda t, eg=eg, dc=dc, fin=fin: t.matmul(
                    pb[eg][:, 0:512], o2fm[:, dc * 128:(dc + 1) * 128],
                    wp_bf[:, dc * D + eg * 512: dc * D + (eg + 1) * 512],
                    start=(dc == 0), stop=fin)
                if fin:
                    pep[eg] = run("tensor", "pe", mmf)
                else:
                    run_q("tensor", mmf)
        p4_projlast[tc] = pep[3]
        return pep

    def p4_partial(tc, pep, lx):
        par = tc % 2
        waitt("vector", lx)
        if tc >= 2:
            waitt("vector", p4_pdma[tc - 2])
        vpp = 0
        for eg in range(4):
            wait("vector", "pe", pep[eg])
            vpp = run("vector", "ve", lambda v, eg=eg, par=par: v.scalar_tensor_tensor(
                pstage[par][:, eg * 512:(eg + 1) * 512], pb[eg][:, 0:512],
                ps3[:, 2:3], xbuf4[:, eg * 512:(eg + 1) * 512], op.mult, op.add))
        p4_partlast[tc] = vpp
        wait("gpsimd", "ve", vpp)
        p4_pdma[tc] = G_pw[par].dma("gpsimd", lambda g, tc=tc, par=par: g.dma_start(
            out=part_s.ap()[tc], in_=pstage[par][:]))
        return vpp

    toks = [None] * (NT + 1)
    toks[0] = p4_loads(0)
    p4_qcorr(0, toks[0])
    p4_corr_mms(0)
    p4_chain(0, toks[0])
    for tc in range(NT):
        if tc + 1 < NT:
            toks[tc + 1] = p4_loads(tc + 1)
            p4_qcorr(tc + 1, toks[tc + 1])
            p4_corr_mms(tc + 1)
        if x_free:
            wait("sync", "ve", x_free)
        lx = S_x4.dma("sync", lambda s, tc=tc: s.dma_start(
            out=xbuf4[:], in_=x_view[tc, :, :]))
        pep = p4_tp_proj(tc)
        if tc + 1 < NT:
            p4_chain(tc + 1, toks[tc + 1])
        vpp = p4_partial(tc, pep, lx)
        x_free = vpp

    set_phase("P3A")
    wait("sync", "pe", p4_projlast[-1])
    xn_tok = S_xn.dma("sync", lambda s: s.dma_start(
        out=xn_fm[:], in_=xnfm_s.ap().rearrange("d p t -> p d t")))
    wgv = wg_d.rearrange("(dc p) h -> p dc h", p=P)
    wuv = wu_d.rearrange("(dc p) h -> p dc h", p=P)
    gu_evac = [0] * 8
    silu_guard = [0, 0]
    gu_eng = ["se"] * 4 + ["ve"] * 4
    g_dma = [None, None]
    w_free = [0, 0]
    wait("gpsimd", "ve", p4_partlast[-1])
    waitt("gpsimd", (S_o4[0], S_o4[0].count))
    waitt("gpsimd", (S_o4[1], S_o4[1].count))
    waitt("gpsimd", (S_q4[0], S_q4[0].count))
    waitt("gpsimd", (S_q4[1], S_q4[1].count))
    waitt("gpsimd", (S_x4, S_x4.count))
    for hc in range(NHC):
        buf = hc % 2
        if w_free[buf]:
            wait("gpsimd", "pe", w_free[buf])
        G_wgu[buf].dma("gpsimd", lambda g, hc=hc, buf=buf: g.dma_start(
            out=wgb[buf][:], in_=wgv[:, :, hc * 128:(hc + 1) * 128]))
        w_tok = G_wgu[buf].dma("gpsimd", lambda g, hc=hc, buf=buf: g.dma_start(
            out=wub[buf][:], in_=wuv[:, :, hc * 128:(hc + 1) * 128]))
        waitt("tensor", w_tok)
        if hc == 0:
            waitt("tensor", xn_tok)
        pe_g = [0] * 4
        pe_u = [0] * 4
        for tt in range(4):
            if gu_evac[tt]:
                wait("tensor", gu_eng[tt], gu_evac[tt])
            for dc in range(NDC):
                fin = (dc == NDC - 1)
                mmf = lambda t, tt=tt, dc=dc, buf=buf, fin=fin: t.matmul(
                    pb[tt][:, 0:512], wgb[buf][:, dc * 128:(dc + 1) * 128],
                    xn_fm[:, dc * T + tt * 512: dc * T + (tt + 1) * 512],
                    start=(dc == 0), stop=fin)
                if fin:
                    pe_g[tt] = run("tensor", "pe", mmf)
                else:
                    run_q("tensor", mmf)
        for tt in range(4):
            if gu_evac[4 + tt]:
                wait("tensor", gu_eng[4 + tt], gu_evac[4 + tt])
            for dc in range(NDC):
                fin = (dc == NDC - 1)
                mmf = lambda t, tt=tt, dc=dc, buf=buf, fin=fin: t.matmul(
                    pb[4 + tt][:, 0:512], wub[buf][:, dc * 128:(dc + 1) * 128],
                    xn_fm[:, dc * T + tt * 512: dc * T + (tt + 1) * 512],
                    start=(dc == 0), stop=fin)
                if fin:
                    pe_u[tt] = run("tensor", "pe", mmf)
                else:
                    run_q("tensor", mmf)
        w_free[buf] = pe_u[3]
        if g_dma[buf]:
            waitt("vector", g_dma[buf])
        vg = 0
        for tt in range(4):
            spar = tt % 2
            wait("scalar", "pe", pe_g[tt])
            if silu_guard[spar]:
                wait("scalar", "ve", silu_guard[spar])
            sse = run("scalar", "se", lambda s, tt=tt, spar=spar: s.activation(
                silu_t[spar][:], pb[tt][:, 0:512], AF.Silu))
            gu_evac[tt] = sse
            wait("vector", "se", sse)
            wait("vector", "pe", pe_u[tt])
            vg = run("vector", "ve", lambda v, tt=tt, spar=spar, buf=buf: v.tensor_tensor(
                gstage[buf][:, tt * 512:(tt + 1) * 512], silu_t[spar][:],
                pb[4 + tt][:, 0:512], op.mult))
            gu_evac[4 + tt] = vg
            silu_guard[spar] = vg
        wait("sync", "ve", vg)
        g_dma[buf] = G_gw[buf].dma("sync", lambda g, hc=hc, buf=buf: g.dma_start(
            out=gated_s.ap()[hc], in_=gstage[buf][:]))

    set_phase("P3B")
    wdv = wd_d.rearrange("(hc p) d -> p hc d", p=P)
    gview = gated_s.ap().rearrange("h p t -> p h t")
    y_view = y_d.rearrange("(tc p) d -> tc p d", p=P)
    down_evac = [0, 0]
    gb_free = [0, 0, 0]
    out_toks = []
    wd_bufs = [wd_bf, wd_bf2]
    wd_toks = [None] * 4
    wd_used = [0, 0]
    bar("sync")
    waitt("sync", g_dma[0])
    waitt("sync", g_dma[1])
    waitt("sync", p4_pdma[-1])
    waitt("sync", p4_pdma[-2])
    waitt("sync", (G_wgu[0], G_wgu[0].count))
    waitt("sync", (G_wgu[1], G_wgu[1].count))
    bar("scalar")
    waitt("scalar", g_dma[0])
    waitt("scalar", g_dma[1])
    waitt("scalar", (G_wgu[0], G_wgu[0].count))
    waitt("scalar", (G_wgu[1], G_wgu[1].count))
    waitt("scalar", p4_pdma[-1])
    waitt("scalar", p4_pdma[-2])
    for dg2 in range(4):
        wbuf = dg2 % 2
        if dg2 < 2:
            if wbuf == 0:
                wait("gpsimd", "pe", w_free[0])
                wait("gpsimd", "pe", w_free[1])
            else:
                wait("gpsimd", "pe", p4_projlast[-1])
                wait("gpsimd", "ve", p4_partlast[-1])
        else:
            wait("gpsimd", "pe", wd_used[wbuf])
        if dg2 < 2:
            wd_toks[dg2] = G_wd[wbuf].dma("gpsimd", lambda g, dg2=dg2, wbuf=wbuf: g.dma_start(
                out=wd_bufs[wbuf][:], in_=wdv[:, :, dg2 * 512:(dg2 + 1) * 512]))
    for dg2 in range(4):
        wbuf = dg2 % 2
        wdb = wd_bufs[wbuf]
        ld_wd = wd_toks[dg2]
        for tc in range(NT):
            it = dg2 * NT + tc
            buf = it % 3
            pbk = it % 2
            if gb_free[buf]:
                wait("scalar", "pe", gb_free[buf])
            lg = S_gb[buf].dma("scalar", lambda s, tc=tc, buf=buf: s.dma_start(
                out=gbuf[buf][:], in_=gview[:, :, tc * 128:(tc + 1) * 128]))
            if down_evac[pbk]:
                wait("sync", "ve", down_evac[pbk])
            lp = S_pp[pbk].dma("sync", lambda s, tc=tc, dg2=dg2, pbk=pbk: s.dma_start(
                out=partbuf[pbk][:], in_=part_s.ap()[tc, :, dg2 * 512:(dg2 + 1) * 512]))
            waitt("tensor", lg)
            waitt("tensor", ld_wd)
            if down_evac[pbk]:
                wait("tensor", "ve", down_evac[pbk])
            ped = 0
            for hc in range(NHC):
                fin = (hc == NHC - 1)
                mmf = lambda t, hc=hc, buf=buf, pbk=pbk, fin=fin, wdb=wdb: t.matmul(
                    pb[pbk][:, 0:512], gbuf[buf][:, hc * 128:(hc + 1) * 128],
                    wdb[:, hc * 512:(hc + 1) * 512], start=(hc == 0), stop=fin)
                if fin:
                    ped = run("tensor", "pe", mmf)
                else:
                    run_q("tensor", mmf)
            gb_free[buf] = ped
            wd_used[wbuf] = ped
            wait("vector", "pe", ped)
            waitt("vector", lp)
            if len(out_toks) >= 2:
                waitt("vector", out_toks[-2])
            vde = run("vector", "ve", lambda v, pbk=pbk: v.tensor_tensor(
                outstage[pbk][:], pb[pbk][:, 0:512], partbuf[pbk][:], op.add))
            down_evac[pbk] = vde
            wait("sync", "ve", vde)
            out_toks.append(S_out[pbk].dma("sync", lambda s, tc=tc, dg2=dg2, pbk=pbk: s.dma_start(
                out=y_view[tc, :, dg2 * 512:(dg2 + 1) * 512], in_=outstage[pbk][:])))
        if dg2 + 2 < 4:
            wait("gpsimd", "pe", wd_used[wbuf])
            wd_toks[dg2 + 2] = G_wd[wbuf].dma("gpsimd", lambda g, dg2=dg2, wbuf=wbuf: g.dma_start(
                out=wd_bufs[wbuf][:], in_=wdv[:, :, (dg2 + 2) * 512:(dg2 + 3) * 512]))

    waitt("sync", (S_out[0], S_out[0].count))
    waitt("sync", (S_out[1], S_out[1].count))

    def _run_engine(en, entries):
        cur = None
        cur_id = None
        for ph, f in entries:
            if ph != cur:
                if cur is not None:
                    nc.leave_named_scope(cur, cur_id, False)
                cur_id, _ = nc.enter_named_scope(ph, False)
                cur = ph
            f(en)
        if cur is not None:
            nc.leave_named_scope(cur, cur_id, False)

    with nc.Block() as block:
        @block.sync
        def _(en):
            _run_engine(en, E["sync"])

        @block.tensor
        def _(en):
            _run_engine(en, E["tensor"])

        @block.vector
        def _(en):
            _run_engine(en, E["vector"])

        @block.scalar
        def _(en):
            _run_engine(en, E["scalar"])

        @block.gpsimd
        def _(en):
            _run_engine(en, E["gpsimd"])

    stck.close()
    return nc


_CACHE = {}


def _get_nc():
    if "nc" not in _CACHE:
        _CACHE["nc"] = build_program()
    return _CACHE["nc"]


def kernel(L0_attn_x, L0_attn_qkv_w, L0_attn_qkv_norm_w, L0_attn_proj_w,
           L0_attn_proj_norm_w, L0_attn_norm_w, L0_attn_s,
           L0_ffn_w_gate, L0_ffn_w_up, L0_ffn_w_down):
    import ml_dtypes
    x = np.ascontiguousarray(np.asarray(L0_attn_x, dtype=np.float32))
    wqkv_t = np.ascontiguousarray(np.asarray(L0_attn_qkv_w, np.float32).T)
    wproj_t = np.ascontiguousarray(np.asarray(L0_attn_proj_w, np.float32).T)
    wg_t = np.ascontiguousarray(np.asarray(L0_ffn_w_gate, np.float32).T)
    wu_t = np.ascontiguousarray(np.asarray(L0_ffn_w_up, np.float32).T)
    wd_t = np.ascontiguousarray(np.asarray(L0_ffn_w_down, np.float32).T)
    vrows = np.concatenate([
        np.asarray(L0_attn_qkv_norm_w, np.float32).reshape(-1),
        np.asarray(L0_attn_norm_w, np.float32).reshape(-1),
        np.asarray(L0_attn_proj_norm_w, np.float32).reshape(-1)]).reshape(1, 3 * D)
    srow = np.asarray(L0_attn_s, np.float32).reshape(1, H)
    cid = np.eye(P, dtype=ml_dtypes.bfloat16)
    conesrow = np.ones((1, P), np.float32)
    conescol = np.ones((P, 1), np.float32)
    ceps = np.tile(np.array([[1e-5, float(np.finfo(np.float32).eps), 1e-6, 0.0]],
                            np.float32), (P, 1))

    in_maps = []
    for c in range(NCORES):
        b, half = c // 2, c % 2
        sel = np.zeros((P, NCORES), np.float32)
        if half == 1:
            sel[:, c - 1] = 1.0
        in_maps.append({
            "x": np.ascontiguousarray(x[b, half * T:(half + 1) * T, :]),
            "wqkv_t": wqkv_t, "wproj_t": wproj_t,
            "wg_t": wg_t, "wu_t": wu_t, "wd_t": wd_t,
            "vrows": vrows, "srow": srow, "cid": cid,
            "conesrow": conesrow, "conescol": conescol,
            "ceps": ceps, "csel": sel,
        })

    nc = _get_nc()
    res = run_bass_kernel_spmd(nc, in_maps, core_ids=list(range(NCORES)))
    out = np.empty((4, 4096, D), np.float32)
    for c in range(NCORES):
        b, half = c // 2, c % 2
        out[b, half * T:(half + 1) * T, :] = res.results[c]["y"]
    return out


# version 09
# speedup vs baseline: 1.0146x; 1.0146x over previous
import numpy as np
import concourse.bass as bass
import concourse.mybir as mybir
from concourse.bass_utils import run_bass_kernel_spmd
from concourse.alu_op_type import AluOpType as op
from contextlib import ExitStack

F32 = mybir.dt.float32
BF16 = mybir.dt.bfloat16
AF = mybir.ActivationFunctionType
AX = mybir.AxisListType.X

P = 128
D = 2048
H = 16
T = 2048
NT = 16
NDC = 16
NH = 8192
NHC = 64
QKV = 6144
SCALE = float(128) ** -0.5
NCORES = 8
KB = 1024


def build_program():
    nc = bass.Bass("TRN2", target_bir_lowering=False)

    x_d = nc.dram_tensor("x", [T, D], F32, kind="ExternalInput").ap()
    wqkv_d = nc.dram_tensor("wqkv_t", [D, QKV], F32, kind="ExternalInput").ap()
    wproj_d = nc.dram_tensor("wproj_t", [D, D], F32, kind="ExternalInput").ap()
    wg_d = nc.dram_tensor("wg_t", [D, NH], F32, kind="ExternalInput").ap()
    wu_d = nc.dram_tensor("wu_t", [D, NH], F32, kind="ExternalInput").ap()
    wd_d = nc.dram_tensor("wd_t", [NH, D], F32, kind="ExternalInput").ap()
    vrows_d = nc.dram_tensor("vrows", [1, 3 * D], F32, kind="ExternalInput").ap()
    srow_d = nc.dram_tensor("srow", [1, H], F32, kind="ExternalInput").ap()
    cid_d = nc.dram_tensor("cid", [P, P], BF16, kind="ExternalInput").ap()
    conesrow_d = nc.dram_tensor("conesrow", [1, P], F32, kind="ExternalInput").ap()
    conescol_d = nc.dram_tensor("conescol", [P, 1], F32, kind="ExternalInput").ap()
    ceps_d = nc.dram_tensor("ceps", [P, 4], F32, kind="ExternalInput").ap()
    csel_d = nc.dram_tensor("csel", [P, NCORES], F32, kind="ExternalInput").ap()
    y_d = nc.dram_tensor("y", [T, D], F32, kind="ExternalOutput").ap()

    xnfm_s = nc.dram_tensor("xnfm_s", [NDC, P, T], BF16, kind="Internal")
    qf_s = nc.dram_tensor("qf_s", [H, P, T], BF16, kind="Internal")
    kf_s = nc.dram_tensor("kf_s", [H, P, T], BF16, kind="Internal")
    kl_s = nc.dram_tensor("kl_s", [H, P, T], BF16, kind="Internal")
    vv_s = nc.dram_tensor("vv_s", [H, P, T], BF16, kind="Internal")
    o_s = nc.dram_tensor("o_s", [NT, P, D], F32, kind="Internal")
    part_s = nc.dram_tensor("part_s", [NT, P, D], F32, kind="Internal")
    gated_s = nc.dram_tensor("gated_s", [NHC, P, T], BF16, kind="Internal")
    cc_in = nc.dram_tensor("cc_in", [H * P, P], F32, kind="Internal")
    cc_out = nc.dram_tensor("cc_out", [NCORES * H * P, P], F32, kind="Internal")

    stck = ExitStack()
    e = stck.enter_context
    arena = e(nc.sbuf_tensor("arena", [P, 207 * KB], mybir.dt.uint8))
    abf_h = arena.bitcast(BF16)
    af_h = arena.bitcast(F32)

    def f32at(off_b, cols):
        assert off_b % 4 == 0 and off_b + 4 * cols <= 207 * KB
        return af_h[:, off_b // 4: off_b // 4 + cols]

    def bfat(off_b, cols):
        assert off_b % 2 == 0 and off_b + 2 * cols <= 207 * KB
        return abf_h[:, off_b // 2: off_b // 2 + cols]

    c_eps = f32at(0, 4)
    sel_rep = f32at(32, 8)
    iota_col = f32at(64, 1)
    ones_col = f32at(96, 1)
    qs3 = f32at(128, 3)
    ps3 = f32at(160, 3)
    stat = f32at(192, 16)
    lamT_col = f32at(480, 16)
    lam_rev_col = f32at(544, 16)
    neg_s_rep = f32at(608, 16)
    qabs_strip = f32at(672, 48)
    pabs_strip = f32at(864, 16)
    row3 = f32at(928, 3)
    s_row = f32at(960, 16)
    s_neg_row = f32at(1024, 16)
    ones_row = f32at(1088, 128)
    rev_row = f32at(1600, 128)
    idty = bfat(2112, 128)
    iota_rep = f32at(2432, 128)
    mtmp = f32at(2944, 128)
    diffc = f32at(3456, 128)
    tri_s = f32at(3968, 128)
    vrow_buf = f32at(4480, D)
    strips = f32at(12800, 128)
    MISC_END = 13312

    A_ = MISC_END
    mask_sc = f32at(A_, D)
    corrfacA = f32at(A_, D)
    factor_fm = f32at(A_ + 8 * KB, D)
    corrfacB = f32at(A_ + 8 * KB, D)
    lamT_fm = f32at(A_ + 16 * KB, D)
    S_t = f32at(A_ + 24 * KB, D)
    ofull = f32at(A_ + 24 * KB, D)
    S_bf = bfat(A_ + 32 * KB, D)
    S_init_bf = bfat(A_ + 32 * KB, D)
    wqkv_rep = f32at(A_ + 36 * KB, D)
    pq_a = bfat(A_ + 36 * KB, D)
    pq_b = bfat(A_ + 40 * KB, D)
    S_init = f32at(A_ + 36 * KB, D)
    wnorm_rep = f32at(A_ + 44 * KB, D)
    Stmp = f32at(A_ + 44 * KB, D)
    wprojn_rep = f32at(A_ + 52 * KB, D)
    scores_bf = bfat(A_ + 52 * KB, D)
    qlam_bf = bfat(A_ + 56 * KB, D)
    PERS_END = A_ + 60 * KB

    BIG = PERS_END
    h_fm = bfat(BIG, NDC * T)
    wp_bf = bfat(BIG, NDC * D)
    xn_fm = bfat(BIG, NDC * T)
    wd_bf = bfat(BIG, NHC * 512)
    wd_bf2 = bfat(9216, NHC * 512)
    TA = BIG + 64 * KB
    assert TA + 57 * KB <= 207 * KB

    xbuf = [f32at(TA, D), f32at(TA + 8 * KB, D)]
    xn_t = [f32at(TA + 16 * KB, D), f32at(TA + 24 * KB, D)]
    hbf_t = [bfat(TA + 32 * KB, D), bfat(TA + 36 * KB, D)]
    xnbf_t = [bfat(TA + 40 * KB, D), bfat(TA + 44 * KB, D)]
    xnfm_stage = bfat(TA + 48 * KB, D)
    qabs_buf = [f32at(TA + 52 * KB, D), f32at(TA + 60 * KB, D)]
    wcol = [f32at(TA, D), f32at(TA + 8 * KB, D)]
    wq_bf = [bfat(TA + 16 * KB, D), bfat(TA + 20 * KB, D)]
    qkv_stage = [bfat(TA + 24 * KB, D), bfat(TA + 28 * KB, D)]
    t_stage = [bfat(TA + 32 * KB, D), bfat(TA + 36 * KB, D)]
    a_bf = bfat(TA + 40 * KB, D)
    b_bf = bfat(TA + 44 * KB, D)
    pabs_p2 = f32at(TA + 48 * KB, D)
    qb = [bfat(TA, D), bfat(TA + 4 * KB, D)]
    kb = [bfat(TA + 8 * KB, D), bfat(TA + 12 * KB, D)]
    klb = [bfat(TA + 16 * KB, D), bfat(TA + 20 * KB, D)]
    vb = [bfat(TA + 24 * KB, D), bfat(TA + 28 * KB, D)]
    o_stage = [f32at(TA + 32 * KB, D), f32at(TA + 40 * KB, D)]
    wpcol = f32at(TA + 48 * KB, D)
    gthr = [f32at(TA, D), f32at(TA + 8 * KB, D)]
    obuf = [f32at(TA, D), f32at(TA + 8 * KB, D)]
    qbuf4 = [bfat(TA + 16 * KB, D), bfat(TA + 20 * KB, D)]
    xbuf4 = f32at(TA + 24 * KB, D)
    pstage = [f32at(TA + 32 * KB, D), f32at(TA + 40 * KB, D)]
    qcorr = bfat(TA + 48 * KB, D)
    o2stage = bfat(TA + 52 * KB, D)
    o2stage2 = bfat(TA + 60 * KB, D)
    o2fm = bfat(TA + 64 * KB, D)
    wgb = [bfat(TA, D), bfat(TA + 4 * KB, D)]
    wub = [bfat(TA + 8 * KB, D), bfat(TA + 12 * KB, D)]
    silu_t = [f32at(TA + 16 * KB, 512), f32at(TA + 18 * KB, 512)]
    gstage = [bfat(TA + 20 * KB, D), bfat(TA + 24 * KB, D)]
    gbuf = [bfat(TA, NHC * 128), bfat(TA + 16 * KB, NHC * 128),
            bfat(TA + 32 * KB, NHC * 128)]
    partbuf = [f32at(TA + 48 * KB, 512), f32at(TA + 50 * KB, 512)]
    outstage = [f32at(TA + 52 * KB, 512), f32at(TA + 54 * KB, 512)]

    pb = [e(nc.psum_tensor(f"pb{i}", [P, 512], F32)) for i in range(8)]
    pbbf = [t.bitcast(BF16) for t in pb]

    csem = {n: e(nc.semaphore(f"s_{n}")) for n in ["pe", "ve", "se", "po"]}
    ccnt = {n: 0 for n in csem}
    E = {k: [] for k in ["sync", "tensor", "vector", "scalar", "gpsimd"]}
    PHASE = ["P0"]

    def set_phase(name):
        PHASE[0] = name

    def emit(eng, fn):
        E[eng].append((PHASE[0], fn))

    class Stream:

        def __init__(self, name):
            self.sem = e(nc.semaphore(name))
            self.count = 0

        def dma(self, eng, fn):
            self.count += 16
            emit(eng, lambda en, fn=fn, s=self.sem: fn(en).then_inc(s, 16))
            return (self, self.count)

        def bump(self, eng, fn, amount):
            self.count += amount
            emit(eng, lambda en, fn=fn, s=self.sem, a=amount: fn(en).then_inc(s, a))
            return (self, self.count)

    def waitt(eng, tok):
        if tok is None:
            return
        st2, v = tok
        if v > 0:
            emit(eng, lambda en, s=st2.sem, v=v: en.wait_ge(s, v))

    def wait(eng, sm, val):
        if val and val > 0:
            emit(eng, lambda en, s=csem[sm], v=val: en.wait_ge(s, v))

    def drn(eng):
        emit(eng, lambda en: en.drain())

    def run(eng, sm, fn):
        ccnt[sm] += 1
        v = ccnt[sm]
        emit(eng, lambda en, fn=fn, s=csem[sm]: fn(en).then_inc(s, 1))
        return v

    def run_q(eng, fn):
        emit(eng, lambda en, fn=fn: fn(en))

    def bar(eng):
        for sm in ("pe", "ve", "se", "po"):
            wait(eng, sm, ccnt[sm])

    S_misc = Stream("d_misc")
    S_qabs = [Stream("d_qabs0"), Stream("d_qabs1")]
    S_x1 = [Stream("d_x10"), Stream("d_x11")]
    S_wq = [Stream("d_wq0"), Stream("d_wq1")]
    S_pab = Stream("d_pab")
    S_scan = [Stream("d_scan0"), Stream("d_scan1")]
    S_pq = Stream("d_pq")
    S_gth = [Stream("d_gth0"), Stream("d_gth1")]
    S_row = Stream("d_row")
    S_o4 = [Stream("d_o40"), Stream("d_o41")]
    S_q4 = [Stream("d_q40"), Stream("d_q41")]
    S_x4 = Stream("d_x4")
    S_gb = [Stream("d_gb0"), Stream("d_gb1"), Stream("d_gb2")]
    S_pp = [Stream("d_pp0"), Stream("d_pp1")]
    S_out = [Stream("d_out0"), Stream("d_out1")]
    S_xn = Stream("d_xn")
    G_xnf = Stream("g_xnf")
    G_st = [Stream("g_st0"), Stream("g_st1")]
    G_ts = [Stream("g_ts0"), Stream("g_ts1")]
    G_ow = [Stream("g_ow0"), Stream("g_ow1")]
    G_cc = Stream("g_cc")
    G_ccd = Stream("g_ccd")
    G_pw = [Stream("g_pw0"), Stream("g_pw1")]
    G_wgu = [Stream("g_wgu0"), Stream("g_wgu1")]
    G_gw = [Stream("g_gw0"), Stream("g_gw1")]
    G_wd = [Stream("g_wd0"), Stream("g_wd1")]

    S_misc.dma("sync", lambda s: s.dma_start(out=c_eps[:], in_=ceps_d))
    S_misc.dma("sync", lambda s: s.dma_start(out=sel_rep[:], in_=csel_d))
    S_misc.dma("sync", lambda s: s.dma_start(out=ones_col[:], in_=conescol_d))
    S_misc.dma("sync", lambda s: s.dma_start(out=s_row[0:1, :], in_=srow_d))
    S_misc.dma("sync", lambda s: s.dma_start(out=ones_row[0:1, :], in_=conesrow_d))
    S_misc.dma("sync", lambda s: s.dma_start(out=idty[:], in_=cid_d))
    misc_tok = S_misc.dma("sync", lambda s: s.dma_start(
        out=vrow_buf[0:1, :], in_=vrows_d[0:1, 0:D]))

    io_c = run("gpsimd", "po", lambda g: g.iota(
        iota_col[:], [[1, 1]], base=0, channel_multiplier=1,
        allow_small_or_imprecise_dtypes=True))
    io_r = run("gpsimd", "po", lambda g: g.iota(
        iota_rep[:], [[1, 128]], base=0, channel_multiplier=0,
        allow_small_or_imprecise_dtypes=True))
    io_rev = run("gpsimd", "po", lambda g: g.iota(
        rev_row[0:1, :], [[-1, 128]], base=128, channel_multiplier=0,
        allow_small_or_imprecise_dtypes=True))
    ms_S = run("gpsimd", "po", lambda g: g.memset(S_t[:], 0.0))
    ms_Sbf = run("gpsimd", "po", lambda g: g.memset(S_bf[:], 0.0))

    waitt("vector", misc_tok)
    v_sneg = run("vector", "ve", lambda v: v.tensor_scalar(
        s_neg_row[0:1, :], s_row[0:1, :], -1.0, None, op.mult))

    waitt("tensor", misc_tok)
    wait("tensor", "ve", v_sneg)
    pe1 = run("tensor", "pe", lambda t: t.matmul(
        pb[7][:, 0:16], ones_row[0:1, :], s_neg_row[0:1, :], start=True, stop=True))
    wait("vector", "pe", pe1)
    v_negs = run("vector", "ve", lambda v: v.tensor_copy(neg_s_rep[:], pb[7][:, 0:16]))
    wait("tensor", "ve", v_negs)
    wait("tensor", "po", io_rev)
    pe2 = run("tensor", "pe", lambda t: t.matmul(
        pb[7][:, 0:16], rev_row[0:1, :], s_neg_row[0:1, :], start=True, stop=True))
    wait("scalar", "pe", pe2)
    se_lrev = run("scalar", "se", lambda s: s.activation(
        lam_rev_col[:], pb[7][:, 0:16], AF.Exp))
    se_lamT = run("scalar", "se", lambda s: s.activation(
        lamT_col[:], neg_s_rep[:], AF.Exp, scale=128.0))
    wait("tensor", "se", se_lrev)
    for j in range(4):
        pv = run("tensor", "pe", lambda t, j=j: t.matmul(
            pb[7][:, 0:512], ones_row[0:1, :], vrow_buf[0:1, j * 512:(j + 1) * 512],
            start=True, stop=True))
        wait("vector", "pe", pv)
        vv = run("vector", "ve", lambda v, j=j: v.tensor_copy(
            wqkv_rep[:, j * 512:(j + 1) * 512], pb[7][:, 0:512]))
        wait("tensor", "ve", vv)

    wait("vector", "po", io_r)
    wait("vector", "po", io_c)
    v_dif = run("vector", "ve", lambda v: v.tensor_scalar(
        diffc[:], iota_rep[:], iota_col[:], 0.0, op.subtract, op.max))
    run("vector", "ve", lambda v: v.tensor_scalar(
        tri_s[:], iota_rep[:], iota_col[:], 0.0, op.subtract, op.is_ge))
    drn("vector")
    v_tris = run("vector", "ve", lambda v: v.tensor_scalar(
        tri_s[:], tri_s[:], SCALE, None, op.mult))
    wait("scalar", "ve", v_dif)
    for h in range(H):
        se_m = run("scalar", "se", lambda s, h=h: s.activation(
            mtmp[:], diffc[:], AF.Exp, scale=neg_s_rep[:, h:h + 1]))
        wait("vector", "se", se_m)
        if h == 0:
            wait("vector", "ve", v_tris)
        vm = run("vector", "ve", lambda v, h=h: v.tensor_tensor(
            mask_sc[:, h * 128:(h + 1) * 128], mtmp[:], tri_s[:], op.mult))
        wait("scalar", "ve", vm)
    for h in range(H):
        se_f = run("scalar", "se", lambda s, h=h: s.activation(
            mtmp[:], iota_rep[:], AF.Exp, scale=neg_s_rep[:, h:h + 1]))
        wait("vector", "se", se_f)
        vf = run("vector", "ve", lambda v, h=h: v.tensor_scalar(
            factor_fm[:, h * 128:(h + 1) * 128], mtmp[:], SCALE, None, op.mult))
        wait("scalar", "ve", vf)
    wait("vector", "se", se_lamT)
    for h in range(H):
        run("vector", "ve", lambda v, h=h: v.tensor_scalar(
            lamT_fm[:, h * 128:(h + 1) * 128], iota_rep[:], 0.0,
            lamT_col[:, h:h + 1], op.mult, op.add))

    set_phase("P1")
    wq_view = wqkv_d.rearrange("(dc p) o -> p dc o", p=P)
    abs_red = []

    def emit_abs(i):
        par = i % 2
        buf = qabs_buf[par]
        if i >= 2:
            wait("sync", abs_red[i - 2][0], abs_red[i - 2][1])
        tok = S_qabs[par].dma("sync", lambda s, i=i, buf=buf: s.dma_start(
            out=buf[:], in_=wq_view[:, :, i * 128:(i + 1) * 128]))
        if par == 0:
            waitt("scalar", tok)
            abs_red.append(("se", run("scalar", "se", lambda sc, i=i, buf=buf: sc.activation(
                buf[:], buf[:], AF.Abs, accum_out=qabs_strip[:, i:i + 1]))))
        else:
            waitt("vector", tok)
            abs_red.append(("ve", run("vector", "ve", lambda v, i=i, buf=buf: v.reduce_sum(
                qabs_strip[:, i:i + 1], buf[:], axis=AX, apply_absolute_value=True))))

    x_view = x_d.rearrange("(tc p) d -> tc p d", p=P)
    xnfm_view = xnfm_s.ap().rearrange("dc p t -> p dc t")

    def hfm_dst(tc, dc0):
        base = BIG // 2 + dc0 * T + tc * 128
        return bass.AP(abf_h, base, [[(207 * KB) // 2, P], [T, 8], [1, 128]])

    waitt("scalar", misc_tok)
    sumst = strips[:, 0:16]
    sqst = strips[:, 16:32]
    mst = strips[:, 32:48]
    vast = strips[:, 48:64]
    rst = strips[:, 64:80]
    dst_ = strips[:, 80:96]
    m2st = strips[:, 96:112]
    r2st = strips[:, 112:128]
    p1_sq = [0, 0]
    p1_sum = [0, 0]
    for tc in range(NT):
        par = tc % 2
        if tc >= 2:
            wait("sync", "ve", p1_sum[par])
            wait("sync", "se", p1_sq[par])
        x_tok = S_x1[par].dma("sync", lambda s, tc=tc, par=par: s.dma_start(
            out=xbuf[par][:], in_=x_view[tc, :, :]))
        waitt("vector", x_tok)
        p1_sum[par] = run("vector", "ve", lambda v, tc=tc, par=par: v.reduce_sum(
            sumst[:, tc:tc + 1], xbuf[par][:], axis=AX))
        waitt("scalar", x_tok)
        p1_sq[par] = run("scalar", "se", lambda s, tc=tc, par=par: s.activation(
            xn_t[par][:], xbuf[par][:], AF.Square, accum_out=sqst[:, tc:tc + 1]))
        emit_abs(tc)
    wait("vector", "se", p1_sq[0])
    wait("vector", "se", p1_sq[1])
    drn("vector")
    run("vector", "ve", lambda v: v.tensor_scalar(
        mst[:], sumst[:], 1.0 / D, None, op.mult))
    drn("vector")
    vch = run("vector", "ve", lambda v: v.tensor_scalar(
        m2st[:], mst[:], None, None, op.bypass) if False else v.tensor_tensor(
        m2st[:], mst[:], mst[:], op.mult))
    drn("vector")
    run("vector", "ve", lambda v: v.scalar_tensor_tensor(
        vast[:], sqst[:], 1.0 / D, m2st[:], op.mult, op.subtract))
    drn("vector")
    run("vector", "ve", lambda v: v.tensor_scalar(
        dst_[:], vast[:], 1e-5, None, op.add))
    drn("vector")
    run("vector", "ve", lambda v: v.reciprocal(m2st[:], dst_[:]))
    drn("vector")
    v_mean2 = run("vector", "ve", lambda v: v.tensor_tensor(
        r2st[:], vast[:], m2st[:], op.mult))
    wait("scalar", "ve", v_mean2)
    run("scalar", "se", lambda s: s.activation(
        rst[:], dst_[:], AF.Ln))
    drn("scalar")
    run("scalar", "se", lambda s: s.activation(
        rst[:], rst[:], AF.Exp, scale=-0.5))
    drn("scalar")
    run("scalar", "se", lambda s: s.activation(
        r2st[:], r2st[:], AF.Ln, bias=c_eps[:, 1:2]))
    drn("scalar")
    se_chain = run("scalar", "se", lambda s: s.activation(
        r2st[:], r2st[:], AF.Exp, scale=-0.5))
    p1_xfree = [0, 0]
    p1_tp_last = [0, 0]
    p1_banklast = 0
    p1_selast = 0
    p1_dma = []
    wait("vector", "se", se_chain)
    for tc in range(NT):
        par = tc % 2
        if tc < 2:
            wait("sync", "ve", p1_sum[par])
            wait("sync", "se", p1_sq[par])
        else:
            wait("sync", "ve", p1_xfree[par])
        x_tok = S_x1[par].dma("sync", lambda s, tc=tc, par=par: s.dma_start(
            out=xbuf[par][:], in_=x_view[tc, :, :]))
        waitt("vector", x_tok)
        if p1_tp_last[par]:
            wait("vector", "pe", p1_tp_last[par])
        v_xn = run("vector", "ve", lambda v, tc=tc, par=par: v.tensor_scalar(
            xn_t[par][:], xbuf[par][:], mst[:, tc:tc + 1], rst[:, tc:tc + 1],
            op.subtract, op.mult))
        p1_xfree[par] = v_xn
        drn("vector")
        v_h = run("vector", "ve", lambda v, tc=tc, par=par: v.scalar_tensor_tensor(
            hbf_t[par][:], xn_t[par][:], r2st[:, tc:tc + 1], wqkv_rep[:],
            op.mult, op.mult))
        drn("vector")
        v_xnb = run("vector", "ve", lambda v, par=par: v.tensor_copy(
            xnbf_t[par][:], xn_t[par][:]))
        wait("tensor", "ve", v_xnb)
        if tc == 0:
            waitt("tensor", misc_tok)
        if p1_banklast:
            wait("tensor", "se", p1_banklast)
        if p1_selast:
            wait("tensor", "se", p1_selast)
        for i in range(8):
            run_q("tensor", lambda t, i=i, par=par: t.transpose(
                pbbf[0][:, i * 128:(i + 1) * 128], xnbf_t[par][:, i * 128:(i + 1) * 128], idty[:]))
        for i in range(8):
            run_q("tensor", lambda t, i=i, par=par: t.transpose(
                pbbf[1][:, i * 128:(i + 1) * 128], xnbf_t[par][:, (8 + i) * 128:(9 + i) * 128], idty[:]))
        for i in range(8):
            run_q("tensor", lambda t, i=i, par=par: t.transpose(
                pbbf[2][:, i * 128:(i + 1) * 128], hbf_t[par][:, i * 128:(i + 1) * 128], idty[:]))
        for i in range(8):
            if i < 7:
                run_q("tensor", lambda t, i=i, par=par: t.transpose(
                    pbbf[3][:, i * 128:(i + 1) * 128], hbf_t[par][:, (8 + i) * 128:(9 + i) * 128], idty[:]))
            else:
                p1_tp_last[par] = run("tensor", "pe", lambda t, i=i, par=par: t.transpose(
                    pbbf[3][:, i * 128:(i + 1) * 128], hbf_t[par][:, (8 + i) * 128:(9 + i) * 128], idty[:]))
        wait("scalar", "pe", p1_tp_last[par])
        if tc >= 1:
            waitt("scalar", p1_dma[tc - 1])
        run("scalar", "se", lambda sc: sc.copy(xnfm_stage[:, 0:1024], pbbf[0][:, 0:1024]))
        p1_se_ev = run("scalar", "se", lambda sc: sc.copy(
            xnfm_stage[:, 1024:2048], pbbf[1][:, 0:1024]))
        p1_selast = p1_se_ev
        run("scalar", "se", lambda sc, tc=tc: sc.copy(hfm_dst(tc, 0), pbbf[2][:, 0:1024]))
        p1_banklast = run("scalar", "se", lambda sc, tc=tc: sc.copy(
            hfm_dst(tc, 8), pbbf[3][:, 0:1024]))
        wait("gpsimd", "se", p1_banklast)
        p1_dma.append(G_xnf.dma("gpsimd", lambda g, tc=tc: g.dma_start(
            out=xnfm_view[:, :, tc * 128:(tc + 1) * 128], in_=xnfm_stage[:])))
        emit_abs(16 + 2 * tc)
        emit_abs(17 + 2 * tc)

    wait("vector", abs_red[-1][0], abs_red[-1][1])
    wait("vector", abs_red[-2][0], abs_red[-2][1])
    drn("vector")
    v_qa = run("vector", "ve", lambda v: v.reduce_sum(
        stat[:, 12:13], qabs_strip[:], axis=AX))
    wait("tensor", "ve", v_qa)
    pe_qa = run("tensor", "pe", lambda t: t.matmul(
        pb[6][0:1, 0:1], stat[:, 12:13], ones_col[:], start=True, stop=True))
    wait("vector", "pe", pe_qa)
    run("vector", "ve", lambda v: v.tensor_scalar(
        row3[0:1, 0:1], pb[6][0:1, 0:1], 1.0 / (QKV * D), 1e-5, op.mult, op.max))
    drn("vector")
    run("vector", "ve", lambda v: v.reciprocal(row3[0:1, 1:2], row3[0:1, 0:1]))
    drn("vector")
    v_r3 = run("vector", "ve", lambda v: v.tensor_scalar(
        row3[0:1, 2:3], row3[0:1, 1:2], -1.0, None, op.mult))
    wait("tensor", "ve", v_r3)
    pe_q3 = run("tensor", "pe", lambda t: t.matmul(
        pb[6][:, 0:3], ones_row[0:1, :], row3[0:1, 0:3], start=True, stop=True))
    wait("vector", "pe", pe_q3)
    run("vector", "ve", lambda v: v.tensor_copy(qs3[:, 0:1], pb[6][:, 1:2]))
    run("vector", "ve", lambda v: v.tensor_copy(qs3[:, 1:2], pb[6][:, 2:3]))
    v_qs3 = run("vector", "ve", lambda v: v.tensor_copy(qs3[:, 2:3], pb[6][:, 0:1]))

    set_phase("P2")
    bar("sync")
    waitt("sync", (S_x1[0], S_x1[0].count))
    waitt("sync", (S_x1[1], S_x1[1].count))
    waitt("sync", (S_qabs[0], S_qabs[0].count))
    waitt("sync", (S_qabs[1], S_qabs[1].count))
    waitt("sync", (G_xnf, G_xnf.count))
    waitt("vector", (G_xnf, G_xnf.count))
    waitt("scalar", (G_xnf, G_xnf.count))
    wpv = wproj_d.rearrange("(dc p) o -> p dc o", p=P)
    evac_done = [0] * 4
    quant_done = [0, 0]
    wq_consumed = [0, 0]
    stage_dma = [None, None]
    stage_pe = [0, 0]
    tstage_done = 0
    tstage_eng = "ve"
    tstage_dma = [None, None]
    pabs_red = [0] * NDC
    w_toks = [None] * 48

    def p2_load(oc):
        buf = oc % 2
        if quant_done[buf]:
            wait("sync", "ve", quant_done[buf])
        w_toks[oc] = S_wq[buf].dma("sync", lambda s, oc=oc, buf=buf: s.dma_start(
            out=wcol[buf][:], in_=wq_view[:, :, oc * 128:(oc + 1) * 128]))

    def p2_quant(oc):
        buf = oc % 2
        waitt("vector", w_toks[oc])
        if oc == 0:
            wait("vector", "ve", v_qs3)
        if wq_consumed[buf]:
            wait("vector", "pe", wq_consumed[buf])
        run("vector", "ve", lambda v, buf=buf: v.tensor_scalar(
            a_bf[:], wcol[buf][:], qs3[:, 0:1], 0.5, op.mult, op.is_gt))
        run("vector", "ve", lambda v, buf=buf: v.tensor_scalar(
            b_bf[:], wcol[buf][:], qs3[:, 1:2], 0.5, op.mult, op.is_gt))
        drn("vector")
        quant_done[buf] = run("vector", "ve", lambda v, buf=buf: v.tensor_tensor(
            wq_bf[buf][:], a_bf[:], b_bf[:], op.subtract))

    p2_load(0)
    p2_load(1)
    p2_quant(0)
    for oc in range(48):
        buf = oc % 2
        if oc < NDC:
            if oc >= 1:
                wait("sync", "se", pabs_red[oc - 1])
            pa_tok = S_pab.dma("sync", lambda s, oc=oc: s.dma_start(
                out=pabs_p2[:], in_=wpv[:, oc, :]))
            waitt("scalar", pa_tok)
            pabs_red[oc] = run("scalar", "se", lambda sc, oc=oc: sc.activation(
                pabs_p2[:], pabs_p2[:], AF.Abs, accum_out=pabs_strip[:, oc:oc + 1]))
        wait("tensor", "ve", quant_done[buf])
        pe_tt = [0] * 4
        for tt in range(4):
            if evac_done[tt]:
                wait("tensor", "ve", evac_done[tt])
            for dc in range(NDC):
                fin = (dc == NDC - 1)
                mmf = lambda t, tt=tt, dc=dc, buf=buf, fin=fin: t.matmul(
                    pb[tt][:, 0:512], wq_bf[buf][:, dc * 128:(dc + 1) * 128],
                    h_fm[:, dc * T + tt * 512: dc * T + (tt + 1) * 512],
                    start=(dc == 0), stop=fin)
                if fin:
                    pe_tt[tt] = run("tensor", "pe", mmf)
                else:
                    run_q("tensor", mmf)
        wq_consumed[buf] = pe_tt[3]
        if oc + 2 < 48:
            p2_load(oc + 2)
        if stage_dma[buf]:
            waitt("vector", stage_dma[buf])
        if stage_pe[buf]:
            wait("vector", "pe", stage_pe[buf])
        wait("vector", "pe", pe_tt[0])
        ev_last = run("vector", "ve", lambda v, buf=buf: v.tensor_scalar(
            qkv_stage[buf][:, 0:512], pb[0][:, 0:512], qs3[:, 2:3], None, op.mult))
        evac_done[0] = ev_last
        if oc + 1 < 48:
            p2_quant(oc + 1)
        for tt in range(1, 4):
            wait("vector", "pe", pe_tt[tt])
            ev_last = run("vector", "ve", lambda v, tt=tt, buf=buf: v.tensor_scalar(
                qkv_stage[buf][:, tt * 512:(tt + 1) * 512], pb[tt][:, 0:512],
                qs3[:, 2:3], None, op.mult))
            evac_done[tt] = ev_last
        if oc < 32:
            tgt = qf_s.ap()[oc] if oc < 16 else kf_s.ap()[oc - 16]
            wait("gpsimd", "ve", ev_last)
            stage_dma[buf] = G_st[buf].dma("gpsimd", lambda g, tgt=tgt, buf=buf: g.dma_start(
                out=tgt, in_=qkv_stage[buf][:]))
        if oc >= 16:
            h_idx = oc - 16 if oc < 32 else oc - 32
            wait("tensor", "ve", ev_last)
            if tstage_done:
                wait("tensor", tstage_eng, tstage_done)
            tp_last = 0
            for i in range(8):
                run_q("tensor", lambda t, i=i, buf=buf: t.transpose(
                    pbbf[4][:, i * 128:(i + 1) * 128],
                    qkv_stage[buf][:, i * 128:(i + 1) * 128], idty[:]))
            for i in range(8):
                if i < 7:
                    run_q("tensor", lambda t, i=i, buf=buf: t.transpose(
                        pbbf[5][:, i * 128:(i + 1) * 128],
                        qkv_stage[buf][:, (8 + i) * 128:(9 + i) * 128], idty[:]))
                else:
                    tp_last = run("tensor", "pe", lambda t, i=i, buf=buf: t.transpose(
                        pbbf[5][:, i * 128:(i + 1) * 128],
                        qkv_stage[buf][:, (8 + i) * 128:(9 + i) * 128], idty[:]))
            stage_pe[buf] = tp_last
            ev_eng = "vector" if oc < 32 else "scalar"
            if tstage_dma[buf]:
                waitt(ev_eng, tstage_dma[buf])
            if oc < 32:
                wait("vector", "pe", tp_last)
                run("vector", "ve", lambda v, buf=buf, h_idx=h_idx: v.tensor_scalar(
                    t_stage[buf][:, 0:1024], pbbf[4][:, 0:1024],
                    lam_rev_col[:, h_idx:h_idx + 1], None, op.mult))
                tv = run("vector", "ve", lambda v, buf=buf, h_idx=h_idx: v.tensor_scalar(
                    t_stage[buf][:, 1024:2048], pbbf[5][:, 0:1024],
                    lam_rev_col[:, h_idx:h_idx + 1], None, op.mult))
                tstage_done, tstage_eng = tv, "ve"
                tgt2 = kl_s.ap()[h_idx]
                tv_eng = "ve"
            else:
                wait("scalar", "pe", tp_last)
                run("scalar", "se", lambda s, buf=buf: s.copy(
                    t_stage[buf][:, 0:1024], pbbf[4][:, 0:1024]))
                tv = run("scalar", "se", lambda s, buf=buf: s.copy(
                    t_stage[buf][:, 1024:2048], pbbf[5][:, 0:1024]))
                tstage_done, tstage_eng = tv, "se"
                tgt2 = vv_s.ap()[h_idx]
                tv_eng = "se"
            wait("gpsimd", tv_eng, tv)
            tstage_dma[buf] = G_ts[buf].dma("gpsimd", lambda g, tgt2=tgt2, buf=buf: g.dma_start(
                out=tgt2, in_=t_stage[buf][:]))

    wait("vector", "se", pabs_red[NDC - 1])
    v_pa = run("vector", "ve", lambda v: v.reduce_sum(
        stat[:, 13:14], pabs_strip[:], axis=AX))
    wait("tensor", "ve", v_pa)
    pe_pa = run("tensor", "pe", lambda t: t.matmul(
        pb[6][0:1, 0:1], stat[:, 13:14], ones_col[:], start=True, stop=True))
    wait("vector", "pe", pe_pa)
    run("vector", "ve", lambda v: v.tensor_scalar(
        row3[0:1, 0:1], pb[6][0:1, 0:1], 1.0 / (D * D), 1e-5, op.mult, op.max))
    drn("vector")
    run("vector", "ve", lambda v: v.reciprocal(row3[0:1, 1:2], row3[0:1, 0:1]))
    drn("vector")
    v_p3 = run("vector", "ve", lambda v: v.tensor_scalar(
        row3[0:1, 2:3], row3[0:1, 1:2], -1.0, None, op.mult))
    wait("tensor", "ve", v_p3)
    pe_p3 = run("tensor", "pe", lambda t: t.matmul(
        pb[6][:, 0:3], ones_row[0:1, :], row3[0:1, 0:3], start=True, stop=True))
    wait("vector", "pe", pe_p3)
    run("vector", "ve", lambda v: v.tensor_copy(ps3[:, 0:1], pb[6][:, 1:2]))
    run("vector", "ve", lambda v: v.tensor_copy(ps3[:, 1:2], pb[6][:, 2:3]))
    v_ps3 = run("vector", "ve", lambda v: v.tensor_copy(ps3[:, 2:3], pb[6][:, 0:1]))

    def strided_read(src, c):
        return src.ap().rearrange("h p t -> p h t")[:, :, c * 128:(c + 1) * 128]

    set_phase("SCAN")
    bar("sync")
    for s2 in (G_st[0], G_st[1], G_ts[0], G_ts[1]):
        waitt("sync", (s2, s2.count))
    bar("scalar")
    waitt("scalar", (G_ts[0], G_ts[0].count))
    waitt("scalar", (G_ts[1], G_ts[1].count))
    waitt("scalar", (G_st[0], G_st[0].count))
    waitt("scalar", (G_st[1], G_st[1].count))
    waitt("scalar", (S_wq[0], S_wq[0].count))
    waitt("scalar", (S_wq[1], S_wq[1].count))
    waitt("scalar", (S_pab, S_pab.count))
    waitt("sync", (S_wq[0], S_wq[0].count))
    waitt("sync", (S_wq[1], S_wq[1].count))
    waitt("sync", (S_pab, S_pab.count))
    scan_pe_done = []
    scan_qlam = []
    scan_supd = []
    scan_cast = []
    scan_oevac = []
    scan_kevac = []
    scan_odma = []
    pq_done = [0] * NDC
    pq_tok = [None] * NDC
    kv_sb = Stmp
    for c in range(NT):
        par = c % 2
        slo = 0 if par == 0 else 4
        obk = 4 if par == 0 else 0
        if c >= 2:
            wait("sync", "pe", scan_pe_done[c - 2])
            wait("sync", "ve", scan_qlam[c - 2])
            wait("scalar", "pe", scan_pe_done[c - 2])
        S_scan[par].dma("sync", lambda s, c=c, par=par: s.dma_start(
            out=qb[par][:], in_=strided_read(qf_s, c)))
        S_scan[par].dma("sync", lambda s, c=c, par=par: s.dma_start(
            out=kb[par][:], in_=strided_read(kf_s, c)))
        S_scan[par].dma("scalar", lambda s, c=c, par=par: s.dma_start(
            out=klb[par][:], in_=strided_read(kl_s, c)))
        ld_tok = S_scan[par].dma("scalar", lambda s, c=c, par=par: s.dma_start(
            out=vb[par][:], in_=strided_read(vv_s, c)))
        if c < NDC:
            if c >= 1:
                wait("sync", "ve", pq_done[c - 1])
            pq_tok[c] = S_pq.dma("sync", lambda s, c=c: s.dma_start(
                out=wpcol[:], in_=wproj_d[c * P:(c + 1) * P, :]))
        waitt("vector", ld_tok)
        scan_qlam.append(run("vector", "ve", lambda v, par=par: v.tensor_tensor(
            qlam_bf[:], qb[par][:], factor_fm[:], op.mult)))
        waitt("tensor", ld_tok)
        if c == 0:
            wait("tensor", "po", ms_Sbf)
        if c >= 1:
            wait("tensor", "se", scan_oevac[c - 1])
        if c >= 2:
            wait("tensor", "ve", scan_supd[c - 2])
        pesc = 0
        for h in range(H):
            bank, sl = slo + h // 4, (h % 4) * 128
            fin = (h == H - 1)
            mmf = lambda t, h=h, bank=bank, sl=sl, par=par: t.matmul(
                pb[bank][:, sl:sl + 128], kb[par][:, h * 128:(h + 1) * 128],
                qb[par][:, h * 128:(h + 1) * 128], start=True, stop=True)
            if fin:
                pesc = run("tensor", "pe", mmf)
            else:
                run_q("tensor", mmf)
        wait("vector", "pe", pesc)
        vme = 0
        for bank in range(4):
            vme = run("vector", "ve", lambda v, bank=bank, slo=slo: v.tensor_tensor(
                scores_bf[:, bank * 512:(bank + 1) * 512], pb[slo + bank][:, 0:512],
                mask_sc[:, bank * 512:(bank + 1) * 512], op.mult))
        wait("tensor", "ve", vme)
        wait("tensor", "ve", scan_qlam[c])
        if c >= 1:
            wait("tensor", "ve", scan_supd[c - 1])
        if c >= 2:
            wait("tensor", "se", scan_oevac[c - 2] if False else 0)
        peoc = 0
        for h in range(H):
            bank, sl = obk + h // 4, (h % 4) * 128
            run_q("tensor", lambda t, h=h, bank=bank, sl=sl, par=par: t.matmul(
                pb[bank][:, sl:sl + 128], scores_bf[:, h * 128:(h + 1) * 128],
                vb[par][:, h * 128:(h + 1) * 128], start=True, stop=False))
            fin = (h == H - 1)
            mmf = lambda t, h=h, bank=bank, sl=sl: t.matmul(
                pb[bank][:, sl:sl + 128], qlam_bf[:, h * 128:(h + 1) * 128],
                S_bf[:, h * 128:(h + 1) * 128], start=False, stop=True)
            if fin:
                peoc = run("tensor", "pe", mmf)
            else:
                run_q("tensor", mmf)
        pekv = 0
        for h in range(H):
            bank, sl = slo + h // 4, (h % 4) * 128
            fin = (h == H - 1)
            mmf = lambda t, h=h, bank=bank, sl=sl, par=par: t.matmul(
                pb[bank][:, sl:sl + 128], klb[par][:, h * 128:(h + 1) * 128],
                vb[par][:, h * 128:(h + 1) * 128], start=True, stop=True)
            if fin:
                pekv = run("tensor", "pe", mmf)
            else:
                run_q("tensor", mmf)
        scan_pe_done.append(pekv)
        wait("scalar", "pe", peoc)
        if c >= 2:
            waitt("scalar", scan_odma[c - 2])
        soe = 0
        for bank in range(4):
            soe = run("scalar", "se", lambda s, bank=bank, obk=obk, par=par: s.copy(
                o_stage[par][:, bank * 512:(bank + 1) * 512], pb[obk + bank][:, 0:512]))
        scan_oevac.append(soe)
        wait("gpsimd", "se", soe)
        scan_odma.append(G_ow[par].dma("gpsimd", lambda g, c=c, par=par: g.dma_start(
            out=o_s.ap()[c], in_=o_stage[par][:])))
        wait("vector", "pe", pekv)
        if c == 0:
            wait("vector", "po", ms_S)
        for bank in range(4):
            run_q("vector", lambda v, bank=bank: v.tensor_tensor(
                Stmp[:, bank * 512:(bank + 1) * 512], S_t[:, bank * 512:(bank + 1) * 512],
                lamT_fm[:, bank * 512:(bank + 1) * 512], op.mult))
        drn("vector")
        for bank in range(4):
            run_q("vector", lambda v, bank=bank, slo=slo: v.tensor_tensor(
                S_t[:, bank * 512:(bank + 1) * 512], Stmp[:, bank * 512:(bank + 1) * 512],
                pb[slo + bank][:, 0:512], op.add))
        drn("vector")
        po2 = run("vector", "ve", lambda v: v.tensor_copy(S_bf[:], S_t[:]))
        scan_supd.append(po2)
        scan_kevac.append(po2)
        scan_cast.append(po2)
        if c < NDC:
            waitt("vector", pq_tok[c])
            if c == 0:
                wait("vector", "ve", v_ps3)
            run("vector", "ve", lambda v: v.tensor_scalar(
                pq_a[:], wpcol[:], ps3[:, 0:1], 0.5, op.mult, op.is_gt))
            run("vector", "ve", lambda v: v.tensor_scalar(
                pq_b[:], wpcol[:], ps3[:, 1:2], 0.5, op.mult, op.is_gt))
            drn("vector")
            pq_done[c] = run("vector", "ve", lambda v, c=c: v.tensor_tensor(
                wp_bf[:, c * D:(c + 1) * D], pq_a[:], pq_b[:], op.subtract))

    set_phase("CC")
    wait("gpsimd", "ve", scan_supd[-1])
    cc_tok = G_cc.dma("gpsimd", lambda g: g.dma_start(
        out=cc_in.ap().rearrange("(h d) e -> d h e", d=P), in_=S_t[:]))
    waitt("gpsimd", cc_tok)
    cc_done = G_ccd.bump("gpsimd", lambda g: g.collective_compute(
        "AllGather", op.bypass, replica_groups=[list(range(NCORES))],
        ins=[cc_in.ap().opt()], outs=[cc_out.ap().opt()]), 1)

    wait("tensor", "se", scan_oevac[-1])
    wait("tensor", "ve", scan_supd[-1])
    rep_tgts = [wnorm_rep, wprojn_rep]
    pv = 0
    for r in range(2):
        if pv:
            wait("sync", "pe", pv)
        row_tok = S_row.dma("sync", lambda s, r=r: s.dma_start(
            out=vrow_buf[0:1, :], in_=vrows_d[0:1, (r + 1) * D:(r + 2) * D]))
        waitt("tensor", row_tok)
        for j in range(4):
            pv = run("tensor", "pe", lambda t, r=r, j=j: t.matmul(
                pb[7][:, 0:512], ones_row[0:1, :],
                vrow_buf[0:1, j * 512:(j + 1) * 512], start=True, stop=True))
            wait("vector", "pe", pv)
            if r == 0 and j == 0:
                wait("vector", "ve", scan_supd[-1])
                wait("vector", "ve", pq_done[-1])
            vv = run("vector", "ve", lambda v, r=r, j=j: v.tensor_copy(
                rep_tgts[r][:, j * 512:(j + 1) * 512], pb[7][:, 0:512]))
            wait("tensor", "ve", vv)

    waitt("vector", cc_done)
    run("vector", "ve", lambda v: v.memset(S_init[:], 0.0))
    waitt("sync", cc_done)
    bar("sync")
    waitt("sync", (S_scan[0], S_scan[0].count))
    waitt("sync", (S_scan[1], S_scan[1].count))
    waitt("sync", (S_pq, S_pq.count))
    si_done = []
    src = S_init
    for r in range(NCORES):
        if r >= 2:
            wait("sync", "ve", si_done[r - 2])
        g_tok = S_gth[r % 2].dma("sync", lambda s, r=r: s.dma_start(
            out=gthr[r % 2][:],
            in_=cc_out.ap()[r * H * P:(r + 1) * H * P, :].rearrange(
                "(h d) e -> d h e", d=P)))
        waitt("vector", g_tok)
        drn("vector")
        dst = S_t if r % 2 == 0 else S_init
        si_done.append(run("vector", "ve", lambda v, r=r, dst=dst, src=src: v.scalar_tensor_tensor(
            dst[:], gthr[r % 2][:], sel_rep[:, r:r + 1], src[:], op.mult, op.add)))
        src = dst
    drn("vector")
    v_sib = run("vector", "ve", lambda v: v.tensor_copy(S_init_bf[:], S_init[:]))

    set_phase("P4")
    v_cfi = run("vector", "ve", lambda v: v.tensor_copy(corrfacA[:], factor_fm[:]))
    wait("sync", "ve", v_sib)
    wait("scalar", "ve", v_sib)
    waitt("scalar", scan_odma[-1])
    waitt("scalar", scan_odma[-2])
    waitt("sync", scan_odma[-1])
    waitt("sync", scan_odma[-2])
    bar("sync")
    waitt("sync", (S_gth[0], S_gth[0].count))
    waitt("sync", (S_gth[1], S_gth[1].count))
    bar("scalar")
    waitt("scalar", (S_gth[0], S_gth[0].count))
    waitt("scalar", (S_gth[1], S_gth[1].count))
    cf_cur, cf_nxt = corrfacA, corrfacB
    p4_o2done = [0] * NT
    p4_qcwr = [0] * NT
    p4_projlast = [0] * NT
    p4_partlast = [0] * NT
    p4_o2evac = [0] * NT
    p4_corr = [0] * NT
    p4_vof = [0] * NT
    p4_tp = [0] * NT
    p4_pdma = [None] * NT
    ser3v = [0] * NT
    x_free = 0
    o2s = [o2stage, o2stage2]

    def p4_loads(tc):
        nonlocal_x = None
        par = tc % 2
        if tc >= 2:
            wait("sync", "ve", p4_o2done[tc - 2])
        lo = S_o4[par].dma("sync", lambda s, tc=tc, par=par: s.dma_start(
            out=obuf[par][:], in_=o_s.ap()[tc]))
        if tc >= 2:
            wait("scalar", "ve", p4_qcwr[tc - 2])
        lq = S_q4[par].dma("scalar", lambda s, tc=tc, par=par: s.dma_start(
            out=qbuf4[par][:], in_=strided_read(qf_s, tc)))
        return lo, lq

    def p4_qcorr(tc, toks):
        nonlocal cf_cur, cf_nxt
        par = tc % 2
        lo, lq = toks
        waitt("vector", lq)
        if tc == 0:
            wait("vector", "ve", v_sib)
            wait("vector", "ve", v_cfi)
        if tc >= 1:
            wait("vector", "pe", p4_corr[tc - 1])
        p4_qcwr[tc] = run("vector", "ve", lambda v, par=par, cf=cf_cur: v.tensor_tensor(
            qcorr[:], qbuf4[par][:], cf[:], op.mult))
        run("vector", "ve", lambda v, cf=cf_cur, cn=cf_nxt: v.tensor_tensor(
            cn[:], cf[:], lamT_fm[:], op.mult))
        cf_cur, cf_nxt = cf_nxt, cf_cur

    def p4_corr_mms(tc):
        wait("tensor", "ve", p4_qcwr[tc])
        if tc >= 1:
            wait("tensor", "ve", p4_vof[tc - 1])
        pec = 0
        for h in range(H):
            bank, sl = 4 + h // 4, (h % 4) * 128
            fin = (h == H - 1)
            mmf = lambda t, h=h, bank=bank, sl=sl: t.matmul(
                pb[bank][:, sl:sl + 128], qcorr[:, h * 128:(h + 1) * 128],
                S_init_bf[:, h * 128:(h + 1) * 128], start=True, stop=True)
            if fin:
                pec = run("tensor", "pe", mmf)
            else:
                run_q("tensor", mmf)
        p4_corr[tc] = pec

    def p4_chain(tc, toks):
        par = tc % 2
        lo, lq = toks
        waitt("vector", lo)
        wait("vector", "pe", p4_corr[tc])
        if tc >= 1:
            wait("vector", "se", ser3v[tc - 1])
        vof = 0
        for bank in range(4):
            vof = run("vector", "ve", lambda v, bank=bank, par=par: v.tensor_tensor(
                ofull[:, bank * 512:(bank + 1) * 512],
                obuf[par][:, bank * 512:(bank + 1) * 512], pb[4 + bank][:, 0:512], op.add))
        p4_vof[tc] = vof
        wait("scalar", "ve", vof)
        if tc >= 2:
            waitt("scalar", p4_pdma[tc - 2])
        run("scalar", "se", lambda s, par=par: s.activation(
            pstage[par][:], ofull[:], AF.Square, accum_out=stat[:, 14:15]))
        drn("scalar")
        run("scalar", "se", lambda s: s.activation(
            stat[:, 14:15], stat[:, 14:15], AF.Ln, scale=1.0 / D, bias=c_eps[:, 2:3]))
        drn("scalar")
        ser3 = run("scalar", "se", lambda s: s.activation(
            stat[:, 14:15], stat[:, 14:15], AF.Exp, scale=-0.5))
        ser3v[tc] = ser3
        wait("vector", "se", ser3)
        vo1 = run("vector", "ve", lambda v, par=par: v.scalar_tensor_tensor(
            obuf[par][:], ofull[:], stat[:, 14:15], wnorm_rep[:], op.mult, op.mult))
        wait("scalar", "ve", vo1)
        run("scalar", "se", lambda s, par=par: s.activation(
            pstage[par][:], obuf[par][:], AF.Square, accum_out=stat[:, 15:16]))
        drn("scalar")
        run("scalar", "se", lambda s: s.activation(
            stat[:, 15:16], stat[:, 15:16], AF.Ln, scale=1.0 / D, bias=c_eps[:, 1:2]))
        drn("scalar")
        ser4 = run("scalar", "se", lambda s: s.activation(
            stat[:, 15:16], stat[:, 15:16], AF.Exp, scale=-0.5))
        wait("vector", "se", ser4)
        if tc >= 2:
            wait("vector", "pe", p4_tp[tc - 2])
        vo2 = run("vector", "ve", lambda v, par=par: v.scalar_tensor_tensor(
            o2s[par][:], obuf[par][:], stat[:, 15:16], wprojn_rep[:], op.mult, op.mult))
        p4_o2done[tc] = vo2

    def p4_tp_proj(tc):
        par = tc % 2
        wait("tensor", "ve", p4_o2done[tc])
        if tc >= 1:
            wait("tensor", "ve", p4_partlast[tc - 1])
        tpl = 0
        for i in range(8):
            run_q("tensor", lambda t, i=i, par=par: t.transpose(
                pbbf[0][:, i * 128:(i + 1) * 128], o2s[par][:, i * 128:(i + 1) * 128], idty[:]))
        for i in range(8):
            if i < 7:
                run_q("tensor", lambda t, i=i, par=par: t.transpose(
                    pbbf[1][:, i * 128:(i + 1) * 128],
                    o2s[par][:, (8 + i) * 128:(9 + i) * 128], idty[:]))
            else:
                tpl = run("tensor", "pe", lambda t, i=i, par=par: t.transpose(
                    pbbf[1][:, i * 128:(i + 1) * 128],
                    o2s[par][:, (8 + i) * 128:(9 + i) * 128], idty[:]))
        p4_tp[tc] = tpl
        wait("vector", "pe", tpl)
        run("vector", "ve", lambda v: v.tensor_copy(o2fm[:, 0:1024], pbbf[0][:, 0:1024]))
        ev = run("vector", "ve", lambda v: v.tensor_copy(o2fm[:, 1024:2048], pbbf[1][:, 0:1024]))
        p4_o2evac[tc] = ev
        wait("tensor", "ve", ev)
        if tc == 0:
            wait("tensor", "ve", pq_done[-1])
        pep = [0] * 4
        for eg in range(4):
            for dc in range(NDC):
                fin = (dc == NDC - 1)
                mmf = lambda t, eg=eg, dc=dc, fin=fin: t.matmul(
                    pb[eg][:, 0:512], o2fm[:, dc * 128:(dc + 1) * 128],
                    wp_bf[:, dc * D + eg * 512: dc * D + (eg + 1) * 512],
                    start=(dc == 0), stop=fin)
                if fin:
                    pep[eg] = run("tensor", "pe", mmf)
                else:
                    run_q("tensor", mmf)
        p4_projlast[tc] = pep[3]
        return pep

    def p4_partial(tc, pep, lx):
        par = tc % 2
        waitt("vector", lx)
        if tc >= 2:
            waitt("vector", p4_pdma[tc - 2])
        vpp = 0
        for eg in range(4):
            wait("vector", "pe", pep[eg])
            vpp = run("vector", "ve", lambda v, eg=eg, par=par: v.scalar_tensor_tensor(
                pstage[par][:, eg * 512:(eg + 1) * 512], pb[eg][:, 0:512],
                ps3[:, 2:3], xbuf4[:, eg * 512:(eg + 1) * 512], op.mult, op.add))
        p4_partlast[tc] = vpp
        wait("gpsimd", "ve", vpp)
        p4_pdma[tc] = G_pw[par].dma("gpsimd", lambda g, tc=tc, par=par: g.dma_start(
            out=part_s.ap()[tc], in_=pstage[par][:]))
        return vpp

    toks = [None] * (NT + 1)
    toks[0] = p4_loads(0)
    p4_qcorr(0, toks[0])
    p4_corr_mms(0)
    p4_chain(0, toks[0])
    for tc in range(NT):
        if tc + 1 < NT:
            toks[tc + 1] = p4_loads(tc + 1)
            p4_qcorr(tc + 1, toks[tc + 1])
            p4_corr_mms(tc + 1)
        if x_free:
            wait("sync", "ve", x_free)
        lx = S_x4.dma("sync", lambda s, tc=tc: s.dma_start(
            out=xbuf4[:], in_=x_view[tc, :, :]))
        pep = p4_tp_proj(tc)
        if tc + 1 < NT:
            p4_chain(tc + 1, toks[tc + 1])
        vpp = p4_partial(tc, pep, lx)
        x_free = vpp

    set_phase("P3A")
    wait("sync", "pe", p4_projlast[-1])
    xn_tok = S_xn.dma("sync", lambda s: s.dma_start(
        out=xn_fm[:], in_=xnfm_s.ap().rearrange("d p t -> p d t")))
    wgv = wg_d.rearrange("(dc p) h -> p dc h", p=P)
    wuv = wu_d.rearrange("(dc p) h -> p dc h", p=P)
    gu_evac = [0] * 8
    silu_guard = [0, 0]
    gu_eng = ["se"] * 4 + ["ve"] * 4
    g_dma = [None, None]
    w_free = [0, 0]
    wait("gpsimd", "ve", p4_partlast[-1])
    waitt("gpsimd", (S_o4[0], S_o4[0].count))
    waitt("gpsimd", (S_o4[1], S_o4[1].count))
    waitt("gpsimd", (S_q4[0], S_q4[0].count))
    waitt("gpsimd", (S_q4[1], S_q4[1].count))
    waitt("gpsimd", (S_x4, S_x4.count))
    for hc in range(NHC):
        buf = hc % 2
        if w_free[buf]:
            wait("gpsimd", "pe", w_free[buf])
        G_wgu[buf].dma("gpsimd", lambda g, hc=hc, buf=buf: g.dma_start(
            out=wgb[buf][:], in_=wgv[:, :, hc * 128:(hc + 1) * 128]))
        w_tok = G_wgu[buf].dma("gpsimd", lambda g, hc=hc, buf=buf: g.dma_start(
            out=wub[buf][:], in_=wuv[:, :, hc * 128:(hc + 1) * 128]))
        waitt("tensor", w_tok)
        if hc == 0:
            waitt("tensor", xn_tok)
        pe_g = [0] * 4
        pe_u = [0] * 4
        for tt in range(4):
            if gu_evac[tt]:
                wait("tensor", gu_eng[tt], gu_evac[tt])
            for dc in range(NDC):
                fin = (dc == NDC - 1)
                mmf = lambda t, tt=tt, dc=dc, buf=buf, fin=fin: t.matmul(
                    pb[tt][:, 0:512], wgb[buf][:, dc * 128:(dc + 1) * 128],
                    xn_fm[:, dc * T + tt * 512: dc * T + (tt + 1) * 512],
                    start=(dc == 0), stop=fin)
                if fin:
                    pe_g[tt] = run("tensor", "pe", mmf)
                else:
                    run_q("tensor", mmf)
        for tt in range(4):
            if gu_evac[4 + tt]:
                wait("tensor", gu_eng[4 + tt], gu_evac[4 + tt])
            for dc in range(NDC):
                fin = (dc == NDC - 1)
                mmf = lambda t, tt=tt, dc=dc, buf=buf, fin=fin: t.matmul(
                    pb[4 + tt][:, 0:512], wub[buf][:, dc * 128:(dc + 1) * 128],
                    xn_fm[:, dc * T + tt * 512: dc * T + (tt + 1) * 512],
                    start=(dc == 0), stop=fin)
                if fin:
                    pe_u[tt] = run("tensor", "pe", mmf)
                else:
                    run_q("tensor", mmf)
        w_free[buf] = pe_u[3]
        if g_dma[buf]:
            waitt("vector", g_dma[buf])
        vg = 0
        for tt in range(4):
            spar = tt % 2
            wait("scalar", "pe", pe_g[tt])
            if silu_guard[spar]:
                wait("scalar", "ve", silu_guard[spar])
            sse = run("scalar", "se", lambda s, tt=tt, spar=spar: s.activation(
                silu_t[spar][:], pb[tt][:, 0:512], AF.Silu))
            gu_evac[tt] = sse
            wait("vector", "se", sse)
            wait("vector", "pe", pe_u[tt])
            vg = run("vector", "ve", lambda v, tt=tt, spar=spar, buf=buf: v.tensor_tensor(
                gstage[buf][:, tt * 512:(tt + 1) * 512], silu_t[spar][:],
                pb[4 + tt][:, 0:512], op.mult))
            gu_evac[4 + tt] = vg
            silu_guard[spar] = vg
        wait("sync", "ve", vg)
        g_dma[buf] = G_gw[buf].dma("sync", lambda g, hc=hc, buf=buf: g.dma_start(
            out=gated_s.ap()[hc], in_=gstage[buf][:]))

    set_phase("P3B")
    wdv = wd_d.rearrange("(hc p) d -> p hc d", p=P)
    gview = gated_s.ap().rearrange("h p t -> p h t")
    y_view = y_d.rearrange("(tc p) d -> tc p d", p=P)
    down_evac = [0, 0]
    gb_free = [0, 0, 0]
    out_toks = []
    wd_bufs = [wd_bf, wd_bf2]
    wd_toks = [None] * 4
    wd_used = [0, 0]
    bar("sync")
    waitt("sync", g_dma[0])
    waitt("sync", g_dma[1])
    waitt("sync", p4_pdma[-1])
    waitt("sync", p4_pdma[-2])
    waitt("sync", (G_wgu[0], G_wgu[0].count))
    waitt("sync", (G_wgu[1], G_wgu[1].count))
    bar("scalar")
    waitt("scalar", g_dma[0])
    waitt("scalar", g_dma[1])
    waitt("scalar", (G_wgu[0], G_wgu[0].count))
    waitt("scalar", (G_wgu[1], G_wgu[1].count))
    waitt("scalar", p4_pdma[-1])
    waitt("scalar", p4_pdma[-2])
    for dg2 in range(4):
        wbuf = dg2 % 2
        if dg2 < 2:
            if wbuf == 0:
                wait("gpsimd", "pe", w_free[0])
                wait("gpsimd", "pe", w_free[1])
            else:
                wait("gpsimd", "pe", p4_projlast[-1])
                wait("gpsimd", "ve", p4_partlast[-1])
        else:
            wait("gpsimd", "pe", wd_used[wbuf])
        if dg2 < 2:
            wd_toks[dg2] = G_wd[wbuf].dma("gpsimd", lambda g, dg2=dg2, wbuf=wbuf: g.dma_start(
                out=wd_bufs[wbuf][:], in_=wdv[:, :, dg2 * 512:(dg2 + 1) * 512]))
    for dg2 in range(4):
        wbuf = dg2 % 2
        wdb = wd_bufs[wbuf]
        ld_wd = wd_toks[dg2]
        for tc in range(NT):
            it = dg2 * NT + tc
            buf = it % 3
            pbk = it % 2
            if gb_free[buf]:
                wait("scalar", "pe", gb_free[buf])
            lg = S_gb[buf].dma("scalar", lambda s, tc=tc, buf=buf: s.dma_start(
                out=gbuf[buf][:], in_=gview[:, :, tc * 128:(tc + 1) * 128]))
            if down_evac[pbk]:
                wait("sync", "ve", down_evac[pbk])
            lp = S_pp[pbk].dma("sync", lambda s, tc=tc, dg2=dg2, pbk=pbk: s.dma_start(
                out=partbuf[pbk][:], in_=part_s.ap()[tc, :, dg2 * 512:(dg2 + 1) * 512]))
            waitt("tensor", lg)
            waitt("tensor", ld_wd)
            if down_evac[pbk]:
                wait("tensor", "ve", down_evac[pbk])
            ped = 0
            for hc in range(NHC):
                fin = (hc == NHC - 1)
                mmf = lambda t, hc=hc, buf=buf, pbk=pbk, fin=fin, wdb=wdb: t.matmul(
                    pb[pbk][:, 0:512], gbuf[buf][:, hc * 128:(hc + 1) * 128],
                    wdb[:, hc * 512:(hc + 1) * 512], start=(hc == 0), stop=fin)
                if fin:
                    ped = run("tensor", "pe", mmf)
                else:
                    run_q("tensor", mmf)
            gb_free[buf] = ped
            wd_used[wbuf] = ped
            wait("vector", "pe", ped)
            waitt("vector", lp)
            if len(out_toks) >= 2:
                waitt("vector", out_toks[-2])
            vde = run("vector", "ve", lambda v, pbk=pbk: v.tensor_tensor(
                outstage[pbk][:], pb[pbk][:, 0:512], partbuf[pbk][:], op.add))
            down_evac[pbk] = vde
            wait("sync", "ve", vde)
            out_toks.append(S_out[pbk].dma("sync", lambda s, tc=tc, dg2=dg2, pbk=pbk: s.dma_start(
                out=y_view[tc, :, dg2 * 512:(dg2 + 1) * 512], in_=outstage[pbk][:])))
        if dg2 + 2 < 4:
            wait("gpsimd", "pe", wd_used[wbuf])
            wd_toks[dg2 + 2] = G_wd[wbuf].dma("gpsimd", lambda g, dg2=dg2, wbuf=wbuf: g.dma_start(
                out=wd_bufs[wbuf][:], in_=wdv[:, :, (dg2 + 2) * 512:(dg2 + 3) * 512]))

    waitt("sync", (S_out[0], S_out[0].count))
    waitt("sync", (S_out[1], S_out[1].count))

    def _run_engine(en, entries):
        cur = None
        cur_id = None
        for ph, f in entries:
            if ph != cur:
                if cur is not None:
                    nc.leave_named_scope(cur, cur_id, False)
                cur_id, _ = nc.enter_named_scope(ph, False)
                cur = ph
            f(en)
        if cur is not None:
            nc.leave_named_scope(cur, cur_id, False)

    with nc.Block() as block:
        @block.sync
        def _(en):
            _run_engine(en, E["sync"])

        @block.tensor
        def _(en):
            _run_engine(en, E["tensor"])

        @block.vector
        def _(en):
            _run_engine(en, E["vector"])

        @block.scalar
        def _(en):
            _run_engine(en, E["scalar"])

        @block.gpsimd
        def _(en):
            _run_engine(en, E["gpsimd"])

    stck.close()
    return nc


_CACHE = {}


def _get_nc():
    if "nc" not in _CACHE:
        _CACHE["nc"] = build_program()
    return _CACHE["nc"]


def kernel(L0_attn_x, L0_attn_qkv_w, L0_attn_qkv_norm_w, L0_attn_proj_w,
           L0_attn_proj_norm_w, L0_attn_norm_w, L0_attn_s,
           L0_ffn_w_gate, L0_ffn_w_up, L0_ffn_w_down):
    import ml_dtypes
    x = np.ascontiguousarray(np.asarray(L0_attn_x, dtype=np.float32))
    wqkv_t = np.ascontiguousarray(np.asarray(L0_attn_qkv_w, np.float32).T)
    wproj_t = np.ascontiguousarray(np.asarray(L0_attn_proj_w, np.float32).T)
    wg_t = np.ascontiguousarray(np.asarray(L0_ffn_w_gate, np.float32).T)
    wu_t = np.ascontiguousarray(np.asarray(L0_ffn_w_up, np.float32).T)
    wd_t = np.ascontiguousarray(np.asarray(L0_ffn_w_down, np.float32).T)
    vrows = np.concatenate([
        np.asarray(L0_attn_qkv_norm_w, np.float32).reshape(-1),
        np.asarray(L0_attn_norm_w, np.float32).reshape(-1),
        np.asarray(L0_attn_proj_norm_w, np.float32).reshape(-1)]).reshape(1, 3 * D)
    srow = np.asarray(L0_attn_s, np.float32).reshape(1, H)
    cid = np.eye(P, dtype=ml_dtypes.bfloat16)
    conesrow = np.ones((1, P), np.float32)
    conescol = np.ones((P, 1), np.float32)
    ceps = np.tile(np.array([[1e-5, float(np.finfo(np.float32).eps), 1e-6, 0.0]],
                            np.float32), (P, 1))

    in_maps = []
    for c in range(NCORES):
        b, half = c // 2, c % 2
        sel = np.zeros((P, NCORES), np.float32)
        if half == 1:
            sel[:, c - 1] = 1.0
        in_maps.append({
            "x": np.ascontiguousarray(x[b, half * T:(half + 1) * T, :]),
            "wqkv_t": wqkv_t, "wproj_t": wproj_t,
            "wg_t": wg_t, "wu_t": wu_t, "wd_t": wd_t,
            "vrows": vrows, "srow": srow, "cid": cid,
            "conesrow": conesrow, "conescol": conescol,
            "ceps": ceps, "csel": sel,
        })

    nc = _get_nc()
    res = run_bass_kernel_spmd(nc, in_maps, core_ids=list(range(NCORES)))
    out = np.empty((4, 4096, D), np.float32)
    for c in range(NCORES):
        b, half = c // 2, c % 2
        out[b, half * T:(half + 1) * T, :] = res.results[c]["y"]
    return out


# version 10
# speedup vs baseline: 1.0167x; 1.0020x over previous
import numpy as np
import concourse.bass as bass
import concourse.mybir as mybir
from concourse.bass_utils import run_bass_kernel_spmd
from concourse.alu_op_type import AluOpType as op
from contextlib import ExitStack

F32 = mybir.dt.float32
BF16 = mybir.dt.bfloat16
AF = mybir.ActivationFunctionType
AX = mybir.AxisListType.X

P = 128
D = 2048
H = 16
T = 2048
NT = 16
NDC = 16
NH = 8192
NHC = 64
QKV = 6144
SCALE = float(128) ** -0.5
NCORES = 8
KB = 1024


def build_program():
    nc = bass.Bass("TRN2", target_bir_lowering=False)

    x_d = nc.dram_tensor("x", [T, D], F32, kind="ExternalInput").ap()
    wqkv_d = nc.dram_tensor("wqkv_t", [D, QKV], F32, kind="ExternalInput").ap()
    wproj_d = nc.dram_tensor("wproj_t", [D, D], F32, kind="ExternalInput").ap()
    wg_d = nc.dram_tensor("wg_t", [D, NH], F32, kind="ExternalInput").ap()
    wu_d = nc.dram_tensor("wu_t", [D, NH], F32, kind="ExternalInput").ap()
    wd_d = nc.dram_tensor("wd_t", [NH, D], F32, kind="ExternalInput").ap()
    vrows_d = nc.dram_tensor("vrows", [1, 3 * D], F32, kind="ExternalInput").ap()
    srow_d = nc.dram_tensor("srow", [1, H], F32, kind="ExternalInput").ap()
    cid_d = nc.dram_tensor("cid", [P, P], BF16, kind="ExternalInput").ap()
    conesrow_d = nc.dram_tensor("conesrow", [1, P], F32, kind="ExternalInput").ap()
    conescol_d = nc.dram_tensor("conescol", [P, 1], F32, kind="ExternalInput").ap()
    ceps_d = nc.dram_tensor("ceps", [P, 4], F32, kind="ExternalInput").ap()
    csel_d = nc.dram_tensor("csel", [P, NCORES], F32, kind="ExternalInput").ap()
    y_d = nc.dram_tensor("y", [T, D], F32, kind="ExternalOutput").ap()

    xnfm_s = nc.dram_tensor("xnfm_s", [NDC, P, T], BF16, kind="Internal")
    qf_s = nc.dram_tensor("qf_s", [H, P, T], BF16, kind="Internal")
    kf_s = nc.dram_tensor("kf_s", [H, P, T], BF16, kind="Internal")
    kl_s = nc.dram_tensor("kl_s", [H, P, T], BF16, kind="Internal")
    vv_s = nc.dram_tensor("vv_s", [H, P, T], BF16, kind="Internal")
    o_s = nc.dram_tensor("o_s", [NT, P, D], F32, kind="Internal")
    part_s = nc.dram_tensor("part_s", [NT, P, D], F32, kind="Internal")
    gated_s = nc.dram_tensor("gated_s", [NHC, P, T], BF16, kind="Internal")
    cc_in = nc.dram_tensor("cc_in", [H * P, P], F32, kind="Internal")
    cc_out = nc.dram_tensor("cc_out", [NCORES * H * P, P], F32, kind="Internal")

    stck = ExitStack()
    e = stck.enter_context
    arena = e(nc.sbuf_tensor("arena", [P, 207 * KB], mybir.dt.uint8))
    abf_h = arena.bitcast(BF16)
    af_h = arena.bitcast(F32)

    def f32at(off_b, cols):
        assert off_b % 4 == 0 and off_b + 4 * cols <= 207 * KB
        return af_h[:, off_b // 4: off_b // 4 + cols]

    def bfat(off_b, cols):
        assert off_b % 2 == 0 and off_b + 2 * cols <= 207 * KB
        return abf_h[:, off_b // 2: off_b // 2 + cols]

    c_eps = f32at(0, 4)
    sel_rep = f32at(32, 8)
    iota_col = f32at(64, 1)
    ones_col = f32at(96, 1)
    qs3 = f32at(128, 3)
    ps3 = f32at(160, 3)
    stat = f32at(192, 16)
    lamT_col = f32at(480, 16)
    lam_rev_col = f32at(544, 16)
    neg_s_rep = f32at(608, 16)
    qabs_strip = f32at(672, 48)
    pabs_strip = f32at(864, 16)
    row3 = f32at(928, 3)
    s_row = f32at(960, 16)
    s_neg_row = f32at(1024, 16)
    ones_row = f32at(1088, 128)
    rev_row = f32at(1600, 128)
    idty = bfat(2112, 128)
    iota_rep = f32at(2432, 128)
    mtmp = f32at(2944, 128)
    diffc = f32at(3456, 128)
    tri_s = f32at(3968, 128)
    vrow_buf = f32at(4480, D)
    strips = f32at(12800, 128)
    MISC_END = 13312

    A_ = MISC_END
    mask_sc = f32at(A_, D)
    corrfacA = f32at(A_, D)
    factor_fm = f32at(A_ + 8 * KB, D)
    corrfacB = f32at(A_ + 8 * KB, D)
    lamT_fm = f32at(A_ + 16 * KB, D)
    S_t = f32at(A_ + 24 * KB, D)
    ofull = f32at(A_ + 24 * KB, D)
    S_bf = bfat(A_ + 32 * KB, D)
    S_init_bf = bfat(A_ + 32 * KB, D)
    wqkv_rep = f32at(A_ + 36 * KB, D)
    pq_a = bfat(A_ + 36 * KB, D)
    pq_b = bfat(A_ + 40 * KB, D)
    S_init = f32at(A_ + 36 * KB, D)
    wnorm_rep = f32at(A_ + 44 * KB, D)
    Stmp = f32at(A_ + 44 * KB, D)
    wprojn_rep = f32at(A_ + 52 * KB, D)
    scores_bf = bfat(A_ + 52 * KB, D)
    qlam_bf = bfat(A_ + 56 * KB, D)
    PERS_END = A_ + 60 * KB

    BIG = PERS_END
    h_fm = bfat(BIG, NDC * T)
    wp_bf = bfat(BIG, NDC * D)
    xn_fm = bfat(BIG, NDC * T)
    wd_bf = bfat(BIG, NHC * 512)
    wd_bf2 = bfat(9216, NHC * 512)
    TA = BIG + 64 * KB
    assert TA + 57 * KB <= 207 * KB

    xbuf = [f32at(TA, D), f32at(TA + 8 * KB, D)]
    xn_t = [f32at(TA + 16 * KB, D), f32at(TA + 24 * KB, D)]
    hbf_t = [bfat(TA + 32 * KB, D), bfat(TA + 36 * KB, D)]
    xnbf_t = [bfat(TA + 40 * KB, D), bfat(TA + 44 * KB, D)]
    xnfm_stage = bfat(TA + 48 * KB, D)
    qabs_buf = [f32at(TA + 52 * KB, D), f32at(TA + 60 * KB, D),
                f32at(TA + 32 * KB, D), f32at(TA + 40 * KB, D)]
    wcol = [f32at(TA, D), f32at(TA + 8 * KB, D)]
    wq_bf = [bfat(TA + 16 * KB, D), bfat(TA + 20 * KB, D)]
    qkv_stage = [bfat(TA + 24 * KB, D), bfat(TA + 28 * KB, D)]
    t_stage = [bfat(TA + 32 * KB, D), bfat(TA + 36 * KB, D)]
    a_bf = bfat(TA + 40 * KB, D)
    b_bf = bfat(TA + 44 * KB, D)
    pabs_p2 = f32at(TA + 48 * KB, D)
    qb = [bfat(TA, D), bfat(TA + 4 * KB, D)]
    kb = [bfat(TA + 8 * KB, D), bfat(TA + 12 * KB, D)]
    klb = [bfat(TA + 16 * KB, D), bfat(TA + 20 * KB, D)]
    vb = [bfat(TA + 24 * KB, D), bfat(TA + 28 * KB, D)]
    o_stage = [f32at(TA + 32 * KB, D), f32at(TA + 40 * KB, D)]
    wpcol = f32at(TA + 48 * KB, D)
    gthr = [f32at(TA, D), f32at(TA + 8 * KB, D)]
    obuf = [f32at(TA, D), f32at(TA + 8 * KB, D)]
    qbuf4 = [bfat(TA + 16 * KB, D), bfat(TA + 20 * KB, D)]
    xbuf4 = f32at(TA + 24 * KB, D)
    pstage = [f32at(TA + 32 * KB, D), f32at(TA + 40 * KB, D)]
    qcorr = bfat(TA + 48 * KB, D)
    o2stage = bfat(TA + 52 * KB, D)
    o2stage2 = bfat(TA + 60 * KB, D)
    o2fm = bfat(TA + 64 * KB, D)
    wgb = [bfat(TA, D), bfat(TA + 4 * KB, D)]
    wub = [bfat(TA + 8 * KB, D), bfat(TA + 12 * KB, D)]
    silu_t = [f32at(TA + 16 * KB, 512), f32at(TA + 18 * KB, 512)]
    gstage = [bfat(TA + 20 * KB, D), bfat(TA + 24 * KB, D)]
    gbuf = [bfat(TA, NHC * 128), bfat(TA + 16 * KB, NHC * 128),
            bfat(TA + 32 * KB, NHC * 128)]
    partbuf = [f32at(TA + 48 * KB, 512), f32at(TA + 50 * KB, 512)]
    outstage = [f32at(TA + 52 * KB, 512), f32at(TA + 54 * KB, 512)]

    pb = [e(nc.psum_tensor(f"pb{i}", [P, 512], F32)) for i in range(8)]
    pbbf = [t.bitcast(BF16) for t in pb]

    csem = {n: e(nc.semaphore(f"s_{n}")) for n in ["pe", "ve", "se", "po"]}
    ccnt = {n: 0 for n in csem}
    E = {k: [] for k in ["sync", "tensor", "vector", "scalar", "gpsimd"]}
    PHASE = ["P0"]

    def set_phase(name):
        PHASE[0] = name

    def emit(eng, fn):
        E[eng].append((PHASE[0], fn))

    class Stream:

        def __init__(self, name):
            self.sem = e(nc.semaphore(name))
            self.count = 0

        def dma(self, eng, fn):
            self.count += 16
            emit(eng, lambda en, fn=fn, s=self.sem: fn(en).then_inc(s, 16))
            return (self, self.count)

        def bump(self, eng, fn, amount):
            self.count += amount
            emit(eng, lambda en, fn=fn, s=self.sem, a=amount: fn(en).then_inc(s, a))
            return (self, self.count)

    def waitt(eng, tok):
        if tok is None:
            return
        st2, v = tok
        if v > 0:
            emit(eng, lambda en, s=st2.sem, v=v: en.wait_ge(s, v))

    def wait(eng, sm, val):
        if val and val > 0:
            emit(eng, lambda en, s=csem[sm], v=val: en.wait_ge(s, v))

    def drn(eng):
        emit(eng, lambda en: en.drain())

    def run(eng, sm, fn):
        ccnt[sm] += 1
        v = ccnt[sm]
        emit(eng, lambda en, fn=fn, s=csem[sm]: fn(en).then_inc(s, 1))
        return v

    def run_q(eng, fn):
        emit(eng, lambda en, fn=fn: fn(en))

    def bar(eng):
        for sm in ("pe", "ve", "se", "po"):
            wait(eng, sm, ccnt[sm])

    S_misc = Stream("d_misc")
    S_qabs = [Stream("d_qabs0"), Stream("d_qabs1"), Stream("d_qabs2"), Stream("d_qabs3")]
    S_x1 = [Stream("d_x10"), Stream("d_x11")]
    S_wq = [Stream("d_wq0"), Stream("d_wq1")]
    S_pab = Stream("d_pab")
    S_scan = [Stream("d_scan0"), Stream("d_scan1")]
    S_pq = Stream("d_pq")
    S_gth = [Stream("d_gth0"), Stream("d_gth1")]
    S_row = Stream("d_row")
    S_o4 = [Stream("d_o40"), Stream("d_o41")]
    S_q4 = [Stream("d_q40"), Stream("d_q41")]
    S_x4 = Stream("d_x4")
    S_gb = [Stream("d_gb0"), Stream("d_gb1"), Stream("d_gb2")]
    S_pp = [Stream("d_pp0"), Stream("d_pp1")]
    S_out = [Stream("d_out0"), Stream("d_out1")]
    S_xn = Stream("d_xn")
    G_xnf = Stream("g_xnf")
    G_st = [Stream("g_st0"), Stream("g_st1")]
    G_ts = [Stream("g_ts0"), Stream("g_ts1")]
    G_ow = [Stream("g_ow0"), Stream("g_ow1")]
    G_cc = Stream("g_cc")
    G_ccd = Stream("g_ccd")
    G_pw = [Stream("g_pw0"), Stream("g_pw1")]
    G_wgu = [Stream("g_wgu0"), Stream("g_wgu1")]
    G_gw = [Stream("g_gw0"), Stream("g_gw1")]
    G_wd = [Stream("g_wd0"), Stream("g_wd1")]

    S_misc.dma("sync", lambda s: s.dma_start(out=c_eps[:], in_=ceps_d))
    S_misc.dma("sync", lambda s: s.dma_start(out=sel_rep[:], in_=csel_d))
    S_misc.dma("sync", lambda s: s.dma_start(out=ones_col[:], in_=conescol_d))
    S_misc.dma("sync", lambda s: s.dma_start(out=s_row[0:1, :], in_=srow_d))
    S_misc.dma("sync", lambda s: s.dma_start(out=ones_row[0:1, :], in_=conesrow_d))
    S_misc.dma("sync", lambda s: s.dma_start(out=idty[:], in_=cid_d))
    misc_tok = S_misc.dma("sync", lambda s: s.dma_start(
        out=vrow_buf[0:1, :], in_=vrows_d[0:1, 0:D]))

    io_c = run("gpsimd", "po", lambda g: g.iota(
        iota_col[:], [[1, 1]], base=0, channel_multiplier=1,
        allow_small_or_imprecise_dtypes=True))
    io_r = run("gpsimd", "po", lambda g: g.iota(
        iota_rep[:], [[1, 128]], base=0, channel_multiplier=0,
        allow_small_or_imprecise_dtypes=True))
    io_rev = run("gpsimd", "po", lambda g: g.iota(
        rev_row[0:1, :], [[-1, 128]], base=128, channel_multiplier=0,
        allow_small_or_imprecise_dtypes=True))
    ms_S = run("gpsimd", "po", lambda g: g.memset(S_t[:], 0.0))
    ms_Sbf = run("gpsimd", "po", lambda g: g.memset(S_bf[:], 0.0))

    waitt("vector", misc_tok)
    v_sneg = run("vector", "ve", lambda v: v.tensor_scalar(
        s_neg_row[0:1, :], s_row[0:1, :], -1.0, None, op.mult))

    waitt("tensor", misc_tok)
    wait("tensor", "ve", v_sneg)
    pe1 = run("tensor", "pe", lambda t: t.matmul(
        pb[7][:, 0:16], ones_row[0:1, :], s_neg_row[0:1, :], start=True, stop=True))
    wait("vector", "pe", pe1)
    v_negs = run("vector", "ve", lambda v: v.tensor_copy(neg_s_rep[:], pb[7][:, 0:16]))
    wait("tensor", "ve", v_negs)
    wait("tensor", "po", io_rev)
    pe2 = run("tensor", "pe", lambda t: t.matmul(
        pb[7][:, 0:16], rev_row[0:1, :], s_neg_row[0:1, :], start=True, stop=True))
    wait("scalar", "pe", pe2)
    se_lrev = run("scalar", "se", lambda s: s.activation(
        lam_rev_col[:], pb[7][:, 0:16], AF.Exp))
    se_lamT = run("scalar", "se", lambda s: s.activation(
        lamT_col[:], neg_s_rep[:], AF.Exp, scale=128.0))
    wait("tensor", "se", se_lrev)
    for j in range(4):
        pv = run("tensor", "pe", lambda t, j=j: t.matmul(
            pb[7][:, 0:512], ones_row[0:1, :], vrow_buf[0:1, j * 512:(j + 1) * 512],
            start=True, stop=True))
        wait("vector", "pe", pv)
        vv = run("vector", "ve", lambda v, j=j: v.tensor_copy(
            wqkv_rep[:, j * 512:(j + 1) * 512], pb[7][:, 0:512]))
        wait("tensor", "ve", vv)

    wait("vector", "po", io_r)
    wait("vector", "po", io_c)
    v_dif = run("vector", "ve", lambda v: v.tensor_scalar(
        diffc[:], iota_rep[:], iota_col[:], 0.0, op.subtract, op.max))
    run("vector", "ve", lambda v: v.tensor_scalar(
        tri_s[:], iota_rep[:], iota_col[:], 0.0, op.subtract, op.is_ge))
    drn("vector")
    v_tris = run("vector", "ve", lambda v: v.tensor_scalar(
        tri_s[:], tri_s[:], SCALE, None, op.mult))
    wait("scalar", "ve", v_dif)
    for h in range(H):
        se_m = run("scalar", "se", lambda s, h=h: s.activation(
            mtmp[:], diffc[:], AF.Exp, scale=neg_s_rep[:, h:h + 1]))
        wait("vector", "se", se_m)
        if h == 0:
            wait("vector", "ve", v_tris)
        vm = run("vector", "ve", lambda v, h=h: v.tensor_tensor(
            mask_sc[:, h * 128:(h + 1) * 128], mtmp[:], tri_s[:], op.mult))
        wait("scalar", "ve", vm)
    for h in range(H):
        se_f = run("scalar", "se", lambda s, h=h: s.activation(
            mtmp[:], iota_rep[:], AF.Exp, scale=neg_s_rep[:, h:h + 1]))
        wait("vector", "se", se_f)
        vf = run("vector", "ve", lambda v, h=h: v.tensor_scalar(
            factor_fm[:, h * 128:(h + 1) * 128], mtmp[:], SCALE, None, op.mult))
        wait("scalar", "ve", vf)
    wait("vector", "se", se_lamT)
    for h in range(H):
        run("vector", "ve", lambda v, h=h: v.tensor_scalar(
            lamT_fm[:, h * 128:(h + 1) * 128], iota_rep[:], 0.0,
            lamT_col[:, h:h + 1], op.mult, op.add))

    set_phase("P1")
    wq_view = wqkv_d.rearrange("(dc p) o -> p dc o", p=P)
    abs_red = []

    def emit_abs(i):
        bsel = (i % 4) if i < 16 else (i % 2)
        buf = qabs_buf[bsel]
        gate = i - 4 if i < 18 else i - 2
        if gate >= 0:
            wait("sync", abs_red[gate][0], abs_red[gate][1])
        tok = S_qabs[bsel].dma("sync", lambda s, i=i, buf=buf: s.dma_start(
            out=buf[:], in_=wq_view[:, :, i * 128:(i + 1) * 128]))
        if i % 2 == 0:
            waitt("scalar", tok)
            abs_red.append(("se", run("scalar", "se", lambda sc, i=i, buf=buf: sc.activation(
                buf[:], buf[:], AF.Abs, accum_out=qabs_strip[:, i:i + 1]))))
        else:
            waitt("vector", tok)
            abs_red.append(("ve", run("vector", "ve", lambda v, i=i, buf=buf: v.reduce_sum(
                qabs_strip[:, i:i + 1], buf[:], axis=AX, apply_absolute_value=True))))

    x_view = x_d.rearrange("(tc p) d -> tc p d", p=P)
    xnfm_view = xnfm_s.ap().rearrange("dc p t -> p dc t")

    def hfm_dst(tc, dc0):
        base = BIG // 2 + dc0 * T + tc * 128
        return bass.AP(abf_h, base, [[(207 * KB) // 2, P], [T, 8], [1, 128]])

    waitt("scalar", misc_tok)
    sumst = strips[:, 0:16]
    sqst = strips[:, 16:32]
    mst = strips[:, 32:48]
    vast = strips[:, 48:64]
    rst = strips[:, 64:80]
    dst_ = strips[:, 80:96]
    m2st = strips[:, 96:112]
    r2st = strips[:, 112:128]
    p1_sq = [0, 0]
    p1_sum = [0, 0]
    for tc in range(NT):
        par = tc % 2
        if tc >= 2:
            wait("sync", "ve", p1_sum[par])
            wait("sync", "se", p1_sq[par])
        x_tok = S_x1[par].dma("sync", lambda s, tc=tc, par=par: s.dma_start(
            out=xbuf[par][:], in_=x_view[tc, :, :]))
        waitt("vector", x_tok)
        p1_sum[par] = run("vector", "ve", lambda v, tc=tc, par=par: v.reduce_sum(
            sumst[:, tc:tc + 1], xbuf[par][:], axis=AX))
        waitt("scalar", x_tok)
        p1_sq[par] = run("scalar", "se", lambda s, tc=tc, par=par: s.activation(
            xn_t[par][:], xbuf[par][:], AF.Square, accum_out=sqst[:, tc:tc + 1]))
        emit_abs(tc)
    wait("vector", "se", p1_sq[0])
    wait("vector", "se", p1_sq[1])
    drn("vector")
    run("vector", "ve", lambda v: v.tensor_scalar(
        mst[:], sumst[:], 1.0 / D, None, op.mult))
    drn("vector")
    vch = run("vector", "ve", lambda v: v.tensor_scalar(
        m2st[:], mst[:], None, None, op.bypass) if False else v.tensor_tensor(
        m2st[:], mst[:], mst[:], op.mult))
    drn("vector")
    run("vector", "ve", lambda v: v.scalar_tensor_tensor(
        vast[:], sqst[:], 1.0 / D, m2st[:], op.mult, op.subtract))
    drn("vector")
    run("vector", "ve", lambda v: v.tensor_scalar(
        dst_[:], vast[:], 1e-5, None, op.add))
    drn("vector")
    run("vector", "ve", lambda v: v.reciprocal(m2st[:], dst_[:]))
    drn("vector")
    v_mean2 = run("vector", "ve", lambda v: v.tensor_tensor(
        r2st[:], vast[:], m2st[:], op.mult))
    wait("scalar", "ve", v_mean2)
    run("scalar", "se", lambda s: s.activation(
        rst[:], dst_[:], AF.Ln))
    drn("scalar")
    run("scalar", "se", lambda s: s.activation(
        rst[:], rst[:], AF.Exp, scale=-0.5))
    drn("scalar")
    run("scalar", "se", lambda s: s.activation(
        r2st[:], r2st[:], AF.Ln, bias=c_eps[:, 1:2]))
    drn("scalar")
    se_chain = run("scalar", "se", lambda s: s.activation(
        r2st[:], r2st[:], AF.Exp, scale=-0.5))
    p1_xfree = [0, 0]
    p1_tp_last = [0, 0]
    p1_banklast = 0
    p1_selast = 0
    p1_dma = []
    wait("vector", "se", se_chain)
    for tc in range(NT):
        par = tc % 2
        if tc < 2:
            wait("sync", "ve", p1_sum[par])
            wait("sync", "se", p1_sq[par])
        else:
            wait("sync", "ve", p1_xfree[par])
        x_tok = S_x1[par].dma("sync", lambda s, tc=tc, par=par: s.dma_start(
            out=xbuf[par][:], in_=x_view[tc, :, :]))
        waitt("vector", x_tok)
        if p1_tp_last[par]:
            wait("vector", "pe", p1_tp_last[par])
        v_xn = run("vector", "ve", lambda v, tc=tc, par=par: v.tensor_scalar(
            xn_t[par][:], xbuf[par][:], mst[:, tc:tc + 1], rst[:, tc:tc + 1],
            op.subtract, op.mult))
        p1_xfree[par] = v_xn
        drn("vector")
        v_h = run("vector", "ve", lambda v, tc=tc, par=par: v.scalar_tensor_tensor(
            hbf_t[par][:], xn_t[par][:], r2st[:, tc:tc + 1], wqkv_rep[:],
            op.mult, op.mult))
        drn("vector")
        v_xnb = run("vector", "ve", lambda v, par=par: v.tensor_copy(
            xnbf_t[par][:], xn_t[par][:]))
        wait("tensor", "ve", v_xnb)
        if tc == 0:
            waitt("tensor", misc_tok)
        if p1_banklast:
            wait("tensor", "se", p1_banklast)
        if p1_selast:
            wait("tensor", "se", p1_selast)
        for i in range(8):
            run_q("tensor", lambda t, i=i, par=par: t.transpose(
                pbbf[0][:, i * 128:(i + 1) * 128], xnbf_t[par][:, i * 128:(i + 1) * 128], idty[:]))
        for i in range(8):
            run_q("tensor", lambda t, i=i, par=par: t.transpose(
                pbbf[1][:, i * 128:(i + 1) * 128], xnbf_t[par][:, (8 + i) * 128:(9 + i) * 128], idty[:]))
        for i in range(8):
            run_q("tensor", lambda t, i=i, par=par: t.transpose(
                pbbf[2][:, i * 128:(i + 1) * 128], hbf_t[par][:, i * 128:(i + 1) * 128], idty[:]))
        for i in range(8):
            if i < 7:
                run_q("tensor", lambda t, i=i, par=par: t.transpose(
                    pbbf[3][:, i * 128:(i + 1) * 128], hbf_t[par][:, (8 + i) * 128:(9 + i) * 128], idty[:]))
            else:
                p1_tp_last[par] = run("tensor", "pe", lambda t, i=i, par=par: t.transpose(
                    pbbf[3][:, i * 128:(i + 1) * 128], hbf_t[par][:, (8 + i) * 128:(9 + i) * 128], idty[:]))
        wait("scalar", "pe", p1_tp_last[par])
        if tc >= 1:
            waitt("scalar", p1_dma[tc - 1])
        run("scalar", "se", lambda sc: sc.copy(xnfm_stage[:, 0:1024], pbbf[0][:, 0:1024]))
        p1_se_ev = run("scalar", "se", lambda sc: sc.copy(
            xnfm_stage[:, 1024:2048], pbbf[1][:, 0:1024]))
        p1_selast = p1_se_ev
        run("scalar", "se", lambda sc, tc=tc: sc.copy(hfm_dst(tc, 0), pbbf[2][:, 0:1024]))
        p1_banklast = run("scalar", "se", lambda sc, tc=tc: sc.copy(
            hfm_dst(tc, 8), pbbf[3][:, 0:1024]))
        wait("gpsimd", "se", p1_banklast)
        p1_dma.append(G_xnf.dma("gpsimd", lambda g, tc=tc: g.dma_start(
            out=xnfm_view[:, :, tc * 128:(tc + 1) * 128], in_=xnfm_stage[:])))
        emit_abs(16 + 2 * tc)
        emit_abs(17 + 2 * tc)

    wait("vector", abs_red[-1][0], abs_red[-1][1])
    wait("vector", abs_red[-2][0], abs_red[-2][1])
    drn("vector")
    v_qa = run("vector", "ve", lambda v: v.reduce_sum(
        stat[:, 12:13], qabs_strip[:], axis=AX))
    wait("tensor", "ve", v_qa)
    pe_qa = run("tensor", "pe", lambda t: t.matmul(
        pb[6][0:1, 0:1], stat[:, 12:13], ones_col[:], start=True, stop=True))
    wait("vector", "pe", pe_qa)
    run("vector", "ve", lambda v: v.tensor_scalar(
        row3[0:1, 0:1], pb[6][0:1, 0:1], 1.0 / (QKV * D), 1e-5, op.mult, op.max))
    drn("vector")
    run("vector", "ve", lambda v: v.reciprocal(row3[0:1, 1:2], row3[0:1, 0:1]))
    drn("vector")
    v_r3 = run("vector", "ve", lambda v: v.tensor_scalar(
        row3[0:1, 2:3], row3[0:1, 1:2], -1.0, None, op.mult))
    wait("tensor", "ve", v_r3)
    pe_q3 = run("tensor", "pe", lambda t: t.matmul(
        pb[6][:, 0:3], ones_row[0:1, :], row3[0:1, 0:3], start=True, stop=True))
    wait("vector", "pe", pe_q3)
    run("vector", "ve", lambda v: v.tensor_copy(qs3[:, 0:1], pb[6][:, 1:2]))
    run("vector", "ve", lambda v: v.tensor_copy(qs3[:, 1:2], pb[6][:, 2:3]))
    v_qs3 = run("vector", "ve", lambda v: v.tensor_copy(qs3[:, 2:3], pb[6][:, 0:1]))

    set_phase("P2")
    bar("sync")
    waitt("sync", (S_x1[0], S_x1[0].count))
    waitt("sync", (S_x1[1], S_x1[1].count))
    waitt("sync", (S_qabs[0], S_qabs[0].count))
    waitt("sync", (S_qabs[1], S_qabs[1].count))
    waitt("sync", (G_xnf, G_xnf.count))
    waitt("vector", (G_xnf, G_xnf.count))
    waitt("scalar", (G_xnf, G_xnf.count))
    wpv = wproj_d.rearrange("(dc p) o -> p dc o", p=P)
    evac_done = [0] * 4
    quant_done = [0, 0]
    wq_consumed = [0, 0]
    stage_dma = [None, None]
    stage_pe = [0, 0]
    tstage_done = 0
    tstage_eng = "ve"
    tstage_dma = [None, None]
    pabs_red = [0] * NDC
    w_toks = [None] * 48

    def p2_load(oc):
        buf = oc % 2
        if quant_done[buf]:
            wait("sync", "ve", quant_done[buf])
        w_toks[oc] = S_wq[buf].dma("sync", lambda s, oc=oc, buf=buf: s.dma_start(
            out=wcol[buf][:], in_=wq_view[:, :, oc * 128:(oc + 1) * 128]))

    def p2_quant(oc):
        buf = oc % 2
        waitt("vector", w_toks[oc])
        if oc == 0:
            wait("vector", "ve", v_qs3)
        if wq_consumed[buf]:
            wait("vector", "pe", wq_consumed[buf])
        run("vector", "ve", lambda v, buf=buf: v.tensor_scalar(
            a_bf[:], wcol[buf][:], qs3[:, 0:1], 0.5, op.mult, op.is_gt))
        run("vector", "ve", lambda v, buf=buf: v.tensor_scalar(
            b_bf[:], wcol[buf][:], qs3[:, 1:2], 0.5, op.mult, op.is_gt))
        drn("vector")
        quant_done[buf] = run("vector", "ve", lambda v, buf=buf: v.tensor_tensor(
            wq_bf[buf][:], a_bf[:], b_bf[:], op.subtract))

    p2_load(0)
    p2_load(1)
    p2_quant(0)
    for oc in range(48):
        buf = oc % 2
        if oc < NDC:
            if oc >= 1:
                wait("sync", "se", pabs_red[oc - 1])
            pa_tok = S_pab.dma("sync", lambda s, oc=oc: s.dma_start(
                out=pabs_p2[:], in_=wpv[:, oc, :]))
            waitt("scalar", pa_tok)
            pabs_red[oc] = run("scalar", "se", lambda sc, oc=oc: sc.activation(
                pabs_p2[:], pabs_p2[:], AF.Abs, accum_out=pabs_strip[:, oc:oc + 1]))
        wait("tensor", "ve", quant_done[buf])
        pe_tt = [0] * 4
        for tt in range(4):
            if evac_done[tt]:
                wait("tensor", "ve", evac_done[tt])
            for dc in range(NDC):
                fin = (dc == NDC - 1)
                mmf = lambda t, tt=tt, dc=dc, buf=buf, fin=fin: t.matmul(
                    pb[tt][:, 0:512], wq_bf[buf][:, dc * 128:(dc + 1) * 128],
                    h_fm[:, dc * T + tt * 512: dc * T + (tt + 1) * 512],
                    start=(dc == 0), stop=fin)
                if fin:
                    pe_tt[tt] = run("tensor", "pe", mmf)
                else:
                    run_q("tensor", mmf)
        wq_consumed[buf] = pe_tt[3]
        if oc + 2 < 48:
            p2_load(oc + 2)
        if stage_dma[buf]:
            waitt("vector", stage_dma[buf])
        if stage_pe[buf]:
            wait("vector", "pe", stage_pe[buf])
        wait("vector", "pe", pe_tt[0])
        ev_last = run("vector", "ve", lambda v, buf=buf: v.tensor_scalar(
            qkv_stage[buf][:, 0:512], pb[0][:, 0:512], qs3[:, 2:3], None, op.mult))
        evac_done[0] = ev_last
        if oc + 1 < 48:
            p2_quant(oc + 1)
        for tt in range(1, 4):
            wait("vector", "pe", pe_tt[tt])
            ev_last = run("vector", "ve", lambda v, tt=tt, buf=buf: v.tensor_scalar(
                qkv_stage[buf][:, tt * 512:(tt + 1) * 512], pb[tt][:, 0:512],
                qs3[:, 2:3], None, op.mult))
            evac_done[tt] = ev_last
        if oc < 32:
            tgt = qf_s.ap()[oc] if oc < 16 else kf_s.ap()[oc - 16]
            wait("gpsimd", "ve", ev_last)
            stage_dma[buf] = G_st[buf].dma("gpsimd", lambda g, tgt=tgt, buf=buf: g.dma_start(
                out=tgt, in_=qkv_stage[buf][:]))
        if oc >= 16:
            h_idx = oc - 16 if oc < 32 else oc - 32
            wait("tensor", "ve", ev_last)
            if tstage_done:
                wait("tensor", tstage_eng, tstage_done)
            tp_last = 0
            for i in range(8):
                run_q("tensor", lambda t, i=i, buf=buf: t.transpose(
                    pbbf[4][:, i * 128:(i + 1) * 128],
                    qkv_stage[buf][:, i * 128:(i + 1) * 128], idty[:]))
            for i in range(8):
                if i < 7:
                    run_q("tensor", lambda t, i=i, buf=buf: t.transpose(
                        pbbf[5][:, i * 128:(i + 1) * 128],
                        qkv_stage[buf][:, (8 + i) * 128:(9 + i) * 128], idty[:]))
                else:
                    tp_last = run("tensor", "pe", lambda t, i=i, buf=buf: t.transpose(
                        pbbf[5][:, i * 128:(i + 1) * 128],
                        qkv_stage[buf][:, (8 + i) * 128:(9 + i) * 128], idty[:]))
            stage_pe[buf] = tp_last
            ev_eng = "vector" if oc < 32 else "scalar"
            if tstage_dma[buf]:
                waitt(ev_eng, tstage_dma[buf])
            if oc < 32:
                wait("vector", "pe", tp_last)
                run("vector", "ve", lambda v, buf=buf, h_idx=h_idx: v.tensor_scalar(
                    t_stage[buf][:, 0:1024], pbbf[4][:, 0:1024],
                    lam_rev_col[:, h_idx:h_idx + 1], None, op.mult))
                tv = run("vector", "ve", lambda v, buf=buf, h_idx=h_idx: v.tensor_scalar(
                    t_stage[buf][:, 1024:2048], pbbf[5][:, 0:1024],
                    lam_rev_col[:, h_idx:h_idx + 1], None, op.mult))
                tstage_done, tstage_eng = tv, "ve"
                tgt2 = kl_s.ap()[h_idx]
                tv_eng = "ve"
            else:
                wait("scalar", "pe", tp_last)
                run("scalar", "se", lambda s, buf=buf: s.copy(
                    t_stage[buf][:, 0:1024], pbbf[4][:, 0:1024]))
                tv = run("scalar", "se", lambda s, buf=buf: s.copy(
                    t_stage[buf][:, 1024:2048], pbbf[5][:, 0:1024]))
                tstage_done, tstage_eng = tv, "se"
                tgt2 = vv_s.ap()[h_idx]
                tv_eng = "se"
            wait("gpsimd", tv_eng, tv)
            tstage_dma[buf] = G_ts[buf].dma("gpsimd", lambda g, tgt2=tgt2, buf=buf: g.dma_start(
                out=tgt2, in_=t_stage[buf][:]))

    wait("vector", "se", pabs_red[NDC - 1])
    v_pa = run("vector", "ve", lambda v: v.reduce_sum(
        stat[:, 13:14], pabs_strip[:], axis=AX))
    wait("tensor", "ve", v_pa)
    pe_pa = run("tensor", "pe", lambda t: t.matmul(
        pb[6][0:1, 0:1], stat[:, 13:14], ones_col[:], start=True, stop=True))
    wait("vector", "pe", pe_pa)
    run("vector", "ve", lambda v: v.tensor_scalar(
        row3[0:1, 0:1], pb[6][0:1, 0:1], 1.0 / (D * D), 1e-5, op.mult, op.max))
    drn("vector")
    run("vector", "ve", lambda v: v.reciprocal(row3[0:1, 1:2], row3[0:1, 0:1]))
    drn("vector")
    v_p3 = run("vector", "ve", lambda v: v.tensor_scalar(
        row3[0:1, 2:3], row3[0:1, 1:2], -1.0, None, op.mult))
    wait("tensor", "ve", v_p3)
    pe_p3 = run("tensor", "pe", lambda t: t.matmul(
        pb[6][:, 0:3], ones_row[0:1, :], row3[0:1, 0:3], start=True, stop=True))
    wait("vector", "pe", pe_p3)
    run("vector", "ve", lambda v: v.tensor_copy(ps3[:, 0:1], pb[6][:, 1:2]))
    run("vector", "ve", lambda v: v.tensor_copy(ps3[:, 1:2], pb[6][:, 2:3]))
    v_ps3 = run("vector", "ve", lambda v: v.tensor_copy(ps3[:, 2:3], pb[6][:, 0:1]))

    def strided_read(src, c):
        return src.ap().rearrange("h p t -> p h t")[:, :, c * 128:(c + 1) * 128]

    set_phase("SCAN")
    bar("sync")
    for s2 in (G_st[0], G_st[1], G_ts[0], G_ts[1]):
        waitt("sync", (s2, s2.count))
    bar("scalar")
    waitt("scalar", (G_ts[0], G_ts[0].count))
    waitt("scalar", (G_ts[1], G_ts[1].count))
    waitt("scalar", (G_st[0], G_st[0].count))
    waitt("scalar", (G_st[1], G_st[1].count))
    waitt("scalar", (S_wq[0], S_wq[0].count))
    waitt("scalar", (S_wq[1], S_wq[1].count))
    waitt("scalar", (S_pab, S_pab.count))
    waitt("sync", (S_wq[0], S_wq[0].count))
    waitt("sync", (S_wq[1], S_wq[1].count))
    waitt("sync", (S_pab, S_pab.count))
    scan_pe_done = []
    scan_qlam = []
    scan_supd = []
    scan_cast = []
    scan_oevac = []
    scan_kevac = []
    scan_odma = []
    pq_done = [0] * NDC
    pq_tok = [None] * NDC
    kv_sb = Stmp
    for c in range(NT):
        par = c % 2
        slo = 0 if par == 0 else 4
        obk = 4 if par == 0 else 0
        if c >= 2:
            wait("sync", "pe", scan_pe_done[c - 2])
            wait("sync", "ve", scan_qlam[c - 2])
            wait("scalar", "pe", scan_pe_done[c - 2])
        S_scan[par].dma("sync", lambda s, c=c, par=par: s.dma_start(
            out=qb[par][:], in_=strided_read(qf_s, c)))
        S_scan[par].dma("sync", lambda s, c=c, par=par: s.dma_start(
            out=kb[par][:], in_=strided_read(kf_s, c)))
        S_scan[par].dma("scalar", lambda s, c=c, par=par: s.dma_start(
            out=klb[par][:], in_=strided_read(kl_s, c)))
        ld_tok = S_scan[par].dma("scalar", lambda s, c=c, par=par: s.dma_start(
            out=vb[par][:], in_=strided_read(vv_s, c)))
        if c < NDC:
            if c >= 1:
                wait("sync", "ve", pq_done[c - 1])
            pq_tok[c] = S_pq.dma("sync", lambda s, c=c: s.dma_start(
                out=wpcol[:], in_=wproj_d[c * P:(c + 1) * P, :]))
        waitt("vector", ld_tok)
        scan_qlam.append(run("vector", "ve", lambda v, par=par: v.tensor_tensor(
            qlam_bf[:], qb[par][:], factor_fm[:], op.mult)))
        waitt("tensor", ld_tok)
        if c == 0:
            wait("tensor", "po", ms_Sbf)
        if c >= 1:
            wait("tensor", "se", scan_oevac[c - 1])
        if c >= 2:
            wait("tensor", "ve", scan_supd[c - 2])
        pesc = 0
        for h in range(H):
            bank, sl = slo + h // 4, (h % 4) * 128
            fin = (h == H - 1)
            mmf = lambda t, h=h, bank=bank, sl=sl, par=par: t.matmul(
                pb[bank][:, sl:sl + 128], kb[par][:, h * 128:(h + 1) * 128],
                qb[par][:, h * 128:(h + 1) * 128], start=True, stop=True)
            if fin:
                pesc = run("tensor", "pe", mmf)
            else:
                run_q("tensor", mmf)
        wait("vector", "pe", pesc)
        vme = 0
        for bank in range(4):
            vme = run("vector", "ve", lambda v, bank=bank, slo=slo: v.tensor_tensor(
                scores_bf[:, bank * 512:(bank + 1) * 512], pb[slo + bank][:, 0:512],
                mask_sc[:, bank * 512:(bank + 1) * 512], op.mult))
        wait("tensor", "ve", vme)
        wait("tensor", "ve", scan_qlam[c])
        if c >= 1:
            wait("tensor", "ve", scan_supd[c - 1])
        if c >= 2:
            wait("tensor", "se", scan_oevac[c - 2] if False else 0)
        peoc = 0
        for h in range(H):
            bank, sl = obk + h // 4, (h % 4) * 128
            run_q("tensor", lambda t, h=h, bank=bank, sl=sl, par=par: t.matmul(
                pb[bank][:, sl:sl + 128], scores_bf[:, h * 128:(h + 1) * 128],
                vb[par][:, h * 128:(h + 1) * 128], start=True, stop=False))
            fin = (h == H - 1)
            mmf = lambda t, h=h, bank=bank, sl=sl: t.matmul(
                pb[bank][:, sl:sl + 128], qlam_bf[:, h * 128:(h + 1) * 128],
                S_bf[:, h * 128:(h + 1) * 128], start=False, stop=True)
            if fin:
                peoc = run("tensor", "pe", mmf)
            else:
                run_q("tensor", mmf)
        pekv = 0
        for h in range(H):
            bank, sl = slo + h // 4, (h % 4) * 128
            fin = (h == H - 1)
            mmf = lambda t, h=h, bank=bank, sl=sl, par=par: t.matmul(
                pb[bank][:, sl:sl + 128], klb[par][:, h * 128:(h + 1) * 128],
                vb[par][:, h * 128:(h + 1) * 128], start=True, stop=True)
            if fin:
                pekv = run("tensor", "pe", mmf)
            else:
                run_q("tensor", mmf)
        scan_pe_done.append(pekv)
        wait("scalar", "pe", peoc)
        if c >= 2:
            waitt("scalar", scan_odma[c - 2])
        soe = 0
        for bank in range(4):
            soe = run("scalar", "se", lambda s, bank=bank, obk=obk, par=par: s.copy(
                o_stage[par][:, bank * 512:(bank + 1) * 512], pb[obk + bank][:, 0:512]))
        scan_oevac.append(soe)
        wait("gpsimd", "se", soe)
        scan_odma.append(G_ow[par].dma("gpsimd", lambda g, c=c, par=par: g.dma_start(
            out=o_s.ap()[c], in_=o_stage[par][:])))
        wait("vector", "pe", pekv)
        if c == 0:
            wait("vector", "po", ms_S)
        for bank in range(4):
            run_q("vector", lambda v, bank=bank: v.tensor_tensor(
                Stmp[:, bank * 512:(bank + 1) * 512], S_t[:, bank * 512:(bank + 1) * 512],
                lamT_fm[:, bank * 512:(bank + 1) * 512], op.mult))
        drn("vector")
        for bank in range(4):
            run_q("vector", lambda v, bank=bank, slo=slo: v.tensor_tensor(
                S_t[:, bank * 512:(bank + 1) * 512], Stmp[:, bank * 512:(bank + 1) * 512],
                pb[slo + bank][:, 0:512], op.add))
        drn("vector")
        po2 = run("vector", "ve", lambda v: v.tensor_copy(S_bf[:], S_t[:]))
        scan_supd.append(po2)
        scan_kevac.append(po2)
        scan_cast.append(po2)
        if c < NDC:
            waitt("vector", pq_tok[c])
            if c == 0:
                wait("vector", "ve", v_ps3)
            run("vector", "ve", lambda v: v.tensor_scalar(
                pq_a[:], wpcol[:], ps3[:, 0:1], 0.5, op.mult, op.is_gt))
            run("vector", "ve", lambda v: v.tensor_scalar(
                pq_b[:], wpcol[:], ps3[:, 1:2], 0.5, op.mult, op.is_gt))
            drn("vector")
            pq_done[c] = run("vector", "ve", lambda v, c=c: v.tensor_tensor(
                wp_bf[:, c * D:(c + 1) * D], pq_a[:], pq_b[:], op.subtract))

    set_phase("CC")
    wait("gpsimd", "ve", scan_supd[-1])
    cc_tok = G_cc.dma("gpsimd", lambda g: g.dma_start(
        out=cc_in.ap().rearrange("(h d) e -> d h e", d=P), in_=S_t[:]))
    waitt("gpsimd", cc_tok)
    cc_done = G_ccd.bump("gpsimd", lambda g: g.collective_compute(
        "AllGather", op.bypass, replica_groups=[list(range(NCORES))],
        ins=[cc_in.ap().opt()], outs=[cc_out.ap().opt()]), 1)

    wait("tensor", "se", scan_oevac[-1])
    wait("tensor", "ve", scan_supd[-1])
    rep_tgts = [wnorm_rep, wprojn_rep]
    pv = 0
    for r in range(2):
        if pv:
            wait("sync", "pe", pv)
        row_tok = S_row.dma("sync", lambda s, r=r: s.dma_start(
            out=vrow_buf[0:1, :], in_=vrows_d[0:1, (r + 1) * D:(r + 2) * D]))
        waitt("tensor", row_tok)
        for j in range(4):
            pv = run("tensor", "pe", lambda t, r=r, j=j: t.matmul(
                pb[7][:, 0:512], ones_row[0:1, :],
                vrow_buf[0:1, j * 512:(j + 1) * 512], start=True, stop=True))
            wait("vector", "pe", pv)
            if r == 0 and j == 0:
                wait("vector", "ve", scan_supd[-1])
                wait("vector", "ve", pq_done[-1])
            vv = run("vector", "ve", lambda v, r=r, j=j: v.tensor_copy(
                rep_tgts[r][:, j * 512:(j + 1) * 512], pb[7][:, 0:512]))
            wait("tensor", "ve", vv)

    waitt("vector", cc_done)
    run("vector", "ve", lambda v: v.memset(S_init[:], 0.0))
    waitt("sync", cc_done)
    bar("sync")
    waitt("sync", (S_scan[0], S_scan[0].count))
    waitt("sync", (S_scan[1], S_scan[1].count))
    waitt("sync", (S_pq, S_pq.count))
    si_done = []
    src = S_init
    for r in range(NCORES):
        if r >= 2:
            wait("sync", "ve", si_done[r - 2])
        g_tok = S_gth[r % 2].dma("sync", lambda s, r=r: s.dma_start(
            out=gthr[r % 2][:],
            in_=cc_out.ap()[r * H * P:(r + 1) * H * P, :].rearrange(
                "(h d) e -> d h e", d=P)))
        waitt("vector", g_tok)
        drn("vector")
        dst = S_t if r % 2 == 0 else S_init
        si_done.append(run("vector", "ve", lambda v, r=r, dst=dst, src=src: v.scalar_tensor_tensor(
            dst[:], gthr[r % 2][:], sel_rep[:, r:r + 1], src[:], op.mult, op.add)))
        src = dst
    drn("vector")
    v_sib = run("vector", "ve", lambda v: v.tensor_copy(S_init_bf[:], S_init[:]))

    set_phase("P4")
    v_cfi = run("vector", "ve", lambda v: v.tensor_copy(corrfacA[:], factor_fm[:]))
    wait("sync", "ve", v_sib)
    wait("scalar", "ve", v_sib)
    waitt("scalar", scan_odma[-1])
    waitt("scalar", scan_odma[-2])
    waitt("sync", scan_odma[-1])
    waitt("sync", scan_odma[-2])
    bar("sync")
    waitt("sync", (S_gth[0], S_gth[0].count))
    waitt("sync", (S_gth[1], S_gth[1].count))
    bar("scalar")
    waitt("scalar", (S_gth[0], S_gth[0].count))
    waitt("scalar", (S_gth[1], S_gth[1].count))
    cf_cur, cf_nxt = corrfacA, corrfacB
    p4_o2done = [0] * NT
    p4_qcwr = [0] * NT
    p4_projlast = [0] * NT
    p4_partlast = [0] * NT
    p4_o2evac = [0] * NT
    p4_corr = [0] * NT
    p4_vof = [0] * NT
    p4_tp = [0] * NT
    p4_pdma = [None] * NT
    ser3v = [0] * NT
    x_free = 0
    o2s = [o2stage, o2stage2]

    def p4_loads(tc):
        nonlocal_x = None
        par = tc % 2
        if tc >= 2:
            wait("sync", "ve", p4_o2done[tc - 2])
        lo = S_o4[par].dma("sync", lambda s, tc=tc, par=par: s.dma_start(
            out=obuf[par][:], in_=o_s.ap()[tc]))
        if tc >= 2:
            wait("scalar", "ve", p4_qcwr[tc - 2])
        lq = S_q4[par].dma("scalar", lambda s, tc=tc, par=par: s.dma_start(
            out=qbuf4[par][:], in_=strided_read(qf_s, tc)))
        return lo, lq

    def p4_qcorr(tc, toks):
        nonlocal cf_cur, cf_nxt
        par = tc % 2
        lo, lq = toks
        waitt("vector", lq)
        if tc == 0:
            wait("vector", "ve", v_sib)
            wait("vector", "ve", v_cfi)
        if tc >= 1:
            wait("vector", "pe", p4_corr[tc - 1])
        p4_qcwr[tc] = run("vector", "ve", lambda v, par=par, cf=cf_cur: v.tensor_tensor(
            qcorr[:], qbuf4[par][:], cf[:], op.mult))
        run("vector", "ve", lambda v, cf=cf_cur, cn=cf_nxt: v.tensor_tensor(
            cn[:], cf[:], lamT_fm[:], op.mult))
        cf_cur, cf_nxt = cf_nxt, cf_cur

    def p4_corr_mms(tc):
        wait("tensor", "ve", p4_qcwr[tc])
        if tc >= 1:
            wait("tensor", "ve", p4_vof[tc - 1])
        pec = 0
        for h in range(H):
            bank, sl = 4 + h // 4, (h % 4) * 128
            fin = (h == H - 1)
            mmf = lambda t, h=h, bank=bank, sl=sl: t.matmul(
                pb[bank][:, sl:sl + 128], qcorr[:, h * 128:(h + 1) * 128],
                S_init_bf[:, h * 128:(h + 1) * 128], start=True, stop=True)
            if fin:
                pec = run("tensor", "pe", mmf)
            else:
                run_q("tensor", mmf)
        p4_corr[tc] = pec

    def p4_chain(tc, toks):
        par = tc % 2
        lo, lq = toks
        waitt("vector", lo)
        wait("vector", "pe", p4_corr[tc])
        if tc >= 1:
            wait("vector", "se", ser3v[tc - 1])
        vof = 0
        for bank in range(4):
            vof = run("vector", "ve", lambda v, bank=bank, par=par: v.tensor_tensor(
                ofull[:, bank * 512:(bank + 1) * 512],
                obuf[par][:, bank * 512:(bank + 1) * 512], pb[4 + bank][:, 0:512], op.add))
        p4_vof[tc] = vof
        wait("scalar", "ve", vof)
        if tc >= 2:
            waitt("scalar", p4_pdma[tc - 2])
        run("scalar", "se", lambda s, par=par: s.activation(
            pstage[par][:], ofull[:], AF.Square, accum_out=stat[:, 14:15]))
        drn("scalar")
        run("scalar", "se", lambda s: s.activation(
            stat[:, 14:15], stat[:, 14:15], AF.Ln, scale=1.0 / D, bias=c_eps[:, 2:3]))
        drn("scalar")
        ser3 = run("scalar", "se", lambda s: s.activation(
            stat[:, 14:15], stat[:, 14:15], AF.Exp, scale=-0.5))
        ser3v[tc] = ser3
        wait("vector", "se", ser3)
        vo1 = run("vector", "ve", lambda v, par=par: v.scalar_tensor_tensor(
            obuf[par][:], ofull[:], stat[:, 14:15], wnorm_rep[:], op.mult, op.mult))
        wait("scalar", "ve", vo1)
        run("scalar", "se", lambda s, par=par: s.activation(
            pstage[par][:], obuf[par][:], AF.Square, accum_out=stat[:, 15:16]))
        drn("scalar")
        run("scalar", "se", lambda s: s.activation(
            stat[:, 15:16], stat[:, 15:16], AF.Ln, scale=1.0 / D, bias=c_eps[:, 1:2]))
        drn("scalar")
        ser4 = run("scalar", "se", lambda s: s.activation(
            stat[:, 15:16], stat[:, 15:16], AF.Exp, scale=-0.5))
        wait("vector", "se", ser4)
        if tc >= 2:
            wait("vector", "pe", p4_tp[tc - 2])
        vo2 = run("vector", "ve", lambda v, par=par: v.scalar_tensor_tensor(
            o2s[par][:], obuf[par][:], stat[:, 15:16], wprojn_rep[:], op.mult, op.mult))
        p4_o2done[tc] = vo2

    def p4_tp_proj(tc):
        par = tc % 2
        wait("tensor", "ve", p4_o2done[tc])
        if tc >= 1:
            wait("tensor", "ve", p4_partlast[tc - 1])
        tpl = 0
        for i in range(8):
            run_q("tensor", lambda t, i=i, par=par: t.transpose(
                pbbf[0][:, i * 128:(i + 1) * 128], o2s[par][:, i * 128:(i + 1) * 128], idty[:]))
        for i in range(8):
            if i < 7:
                run_q("tensor", lambda t, i=i, par=par: t.transpose(
                    pbbf[1][:, i * 128:(i + 1) * 128],
                    o2s[par][:, (8 + i) * 128:(9 + i) * 128], idty[:]))
            else:
                tpl = run("tensor", "pe", lambda t, i=i, par=par: t.transpose(
                    pbbf[1][:, i * 128:(i + 1) * 128],
                    o2s[par][:, (8 + i) * 128:(9 + i) * 128], idty[:]))
        p4_tp[tc] = tpl
        wait("vector", "pe", tpl)
        run("vector", "ve", lambda v: v.tensor_copy(o2fm[:, 0:1024], pbbf[0][:, 0:1024]))
        ev = run("vector", "ve", lambda v: v.tensor_copy(o2fm[:, 1024:2048], pbbf[1][:, 0:1024]))
        p4_o2evac[tc] = ev
        wait("tensor", "ve", ev)
        if tc == 0:
            wait("tensor", "ve", pq_done[-1])
        pep = [0] * 4
        for eg in range(4):
            for dc in range(NDC):
                fin = (dc == NDC - 1)
                mmf = lambda t, eg=eg, dc=dc, fin=fin: t.matmul(
                    pb[eg][:, 0:512], o2fm[:, dc * 128:(dc + 1) * 128],
                    wp_bf[:, dc * D + eg * 512: dc * D + (eg + 1) * 512],
                    start=(dc == 0), stop=fin)
                if fin:
                    pep[eg] = run("tensor", "pe", mmf)
                else:
                    run_q("tensor", mmf)
        p4_projlast[tc] = pep[3]
        return pep

    def p4_partial(tc, pep, lx):
        par = tc % 2
        waitt("vector", lx)
        if tc >= 2:
            waitt("vector", p4_pdma[tc - 2])
        vpp = 0
        for eg in range(4):
            wait("vector", "pe", pep[eg])
            vpp = run("vector", "ve", lambda v, eg=eg, par=par: v.scalar_tensor_tensor(
                pstage[par][:, eg * 512:(eg + 1) * 512], pb[eg][:, 0:512],
                ps3[:, 2:3], xbuf4[:, eg * 512:(eg + 1) * 512], op.mult, op.add))
        p4_partlast[tc] = vpp
        wait("gpsimd", "ve", vpp)
        p4_pdma[tc] = G_pw[par].dma("gpsimd", lambda g, tc=tc, par=par: g.dma_start(
            out=part_s.ap()[tc], in_=pstage[par][:]))
        return vpp

    toks = [None] * (NT + 1)
    toks[0] = p4_loads(0)
    p4_qcorr(0, toks[0])
    p4_corr_mms(0)
    p4_chain(0, toks[0])
    for tc in range(NT):
        if tc + 1 < NT:
            toks[tc + 1] = p4_loads(tc + 1)
            p4_qcorr(tc + 1, toks[tc + 1])
            p4_corr_mms(tc + 1)
        if x_free:
            wait("sync", "ve", x_free)
        lx = S_x4.dma("sync", lambda s, tc=tc: s.dma_start(
            out=xbuf4[:], in_=x_view[tc, :, :]))
        pep = p4_tp_proj(tc)
        if tc + 1 < NT:
            p4_chain(tc + 1, toks[tc + 1])
        vpp = p4_partial(tc, pep, lx)
        x_free = vpp

    set_phase("P3A")
    wait("sync", "pe", p4_projlast[-1])
    xn_tok = S_xn.dma("sync", lambda s: s.dma_start(
        out=xn_fm[:], in_=xnfm_s.ap().rearrange("d p t -> p d t")))
    wgv = wg_d.rearrange("(dc p) h -> p dc h", p=P)
    wuv = wu_d.rearrange("(dc p) h -> p dc h", p=P)
    gu_evac = [0] * 8
    silu_guard = [0, 0]
    gu_eng = ["se"] * 4 + ["ve"] * 4
    g_dma = [None, None]
    w_free = [0, 0]
    wait("gpsimd", "ve", p4_partlast[-1])
    waitt("gpsimd", (S_o4[0], S_o4[0].count))
    waitt("gpsimd", (S_o4[1], S_o4[1].count))
    waitt("gpsimd", (S_q4[0], S_q4[0].count))
    waitt("gpsimd", (S_q4[1], S_q4[1].count))
    waitt("gpsimd", (S_x4, S_x4.count))
    for hc in range(NHC):
        buf = hc % 2
        if w_free[buf]:
            wait("gpsimd", "pe", w_free[buf])
        G_wgu[buf].dma("gpsimd", lambda g, hc=hc, buf=buf: g.dma_start(
            out=wgb[buf][:], in_=wgv[:, :, hc * 128:(hc + 1) * 128]))
        w_tok = G_wgu[buf].dma("gpsimd", lambda g, hc=hc, buf=buf: g.dma_start(
            out=wub[buf][:], in_=wuv[:, :, hc * 128:(hc + 1) * 128]))
        waitt("tensor", w_tok)
        if hc == 0:
            waitt("tensor", xn_tok)
        pe_g = [0] * 4
        pe_u = [0] * 4
        for tt in range(4):
            if gu_evac[tt]:
                wait("tensor", gu_eng[tt], gu_evac[tt])
            for dc in range(NDC):
                fin = (dc == NDC - 1)
                mmf = lambda t, tt=tt, dc=dc, buf=buf, fin=fin: t.matmul(
                    pb[tt][:, 0:512], wgb[buf][:, dc * 128:(dc + 1) * 128],
                    xn_fm[:, dc * T + tt * 512: dc * T + (tt + 1) * 512],
                    start=(dc == 0), stop=fin)
                if fin:
                    pe_g[tt] = run("tensor", "pe", mmf)
                else:
                    run_q("tensor", mmf)
        for tt in range(4):
            if gu_evac[4 + tt]:
                wait("tensor", gu_eng[4 + tt], gu_evac[4 + tt])
            for dc in range(NDC):
                fin = (dc == NDC - 1)
                mmf = lambda t, tt=tt, dc=dc, buf=buf, fin=fin: t.matmul(
                    pb[4 + tt][:, 0:512], wub[buf][:, dc * 128:(dc + 1) * 128],
                    xn_fm[:, dc * T + tt * 512: dc * T + (tt + 1) * 512],
                    start=(dc == 0), stop=fin)
                if fin:
                    pe_u[tt] = run("tensor", "pe", mmf)
                else:
                    run_q("tensor", mmf)
        w_free[buf] = pe_u[3]
        if g_dma[buf]:
            waitt("vector", g_dma[buf])
        vg = 0
        for tt in range(4):
            spar = tt % 2
            wait("scalar", "pe", pe_g[tt])
            if silu_guard[spar]:
                wait("scalar", "ve", silu_guard[spar])
            sse = run("scalar", "se", lambda s, tt=tt, spar=spar: s.activation(
                silu_t[spar][:], pb[tt][:, 0:512], AF.Silu))
            gu_evac[tt] = sse
            wait("vector", "se", sse)
            wait("vector", "pe", pe_u[tt])
            vg = run("vector", "ve", lambda v, tt=tt, spar=spar, buf=buf: v.tensor_tensor(
                gstage[buf][:, tt * 512:(tt + 1) * 512], silu_t[spar][:],
                pb[4 + tt][:, 0:512], op.mult))
            gu_evac[4 + tt] = vg
            silu_guard[spar] = vg
        wait("sync", "ve", vg)
        g_dma[buf] = G_gw[buf].dma("sync", lambda g, hc=hc, buf=buf: g.dma_start(
            out=gated_s.ap()[hc], in_=gstage[buf][:]))

    set_phase("P3B")
    wdv = wd_d.rearrange("(hc p) d -> p hc d", p=P)
    gview = gated_s.ap().rearrange("h p t -> p h t")
    y_view = y_d.rearrange("(tc p) d -> tc p d", p=P)
    down_evac = [0, 0]
    gb_free = [0, 0, 0]
    out_toks = []
    wd_bufs = [wd_bf, wd_bf2]
    wd_toks = [None] * 4
    wd_used = [0, 0]
    bar("sync")
    waitt("sync", g_dma[0])
    waitt("sync", g_dma[1])
    waitt("sync", p4_pdma[-1])
    waitt("sync", p4_pdma[-2])
    waitt("sync", (G_wgu[0], G_wgu[0].count))
    waitt("sync", (G_wgu[1], G_wgu[1].count))
    bar("scalar")
    waitt("scalar", g_dma[0])
    waitt("scalar", g_dma[1])
    waitt("scalar", (G_wgu[0], G_wgu[0].count))
    waitt("scalar", (G_wgu[1], G_wgu[1].count))
    waitt("scalar", p4_pdma[-1])
    waitt("scalar", p4_pdma[-2])
    for dg2 in range(4):
        wbuf = dg2 % 2
        if dg2 < 2:
            if wbuf == 0:
                wait("gpsimd", "pe", w_free[0])
                wait("gpsimd", "pe", w_free[1])
            else:
                wait("gpsimd", "pe", p4_projlast[-1])
                wait("gpsimd", "ve", p4_partlast[-1])
        else:
            wait("gpsimd", "pe", wd_used[wbuf])
        if dg2 < 2:
            wd_toks[dg2] = G_wd[wbuf].dma("gpsimd", lambda g, dg2=dg2, wbuf=wbuf: g.dma_start(
                out=wd_bufs[wbuf][:], in_=wdv[:, :, dg2 * 512:(dg2 + 1) * 512]))
    for dg2 in range(4):
        wbuf = dg2 % 2
        wdb = wd_bufs[wbuf]
        ld_wd = wd_toks[dg2]
        for tc in range(NT):
            it = dg2 * NT + tc
            buf = it % 3
            pbk = it % 2
            if gb_free[buf]:
                wait("scalar", "pe", gb_free[buf])
            lg = S_gb[buf].dma("scalar", lambda s, tc=tc, buf=buf: s.dma_start(
                out=gbuf[buf][:], in_=gview[:, :, tc * 128:(tc + 1) * 128]))
            if down_evac[pbk]:
                wait("sync", "ve", down_evac[pbk])
            lp = S_pp[pbk].dma("sync", lambda s, tc=tc, dg2=dg2, pbk=pbk: s.dma_start(
                out=partbuf[pbk][:], in_=part_s.ap()[tc, :, dg2 * 512:(dg2 + 1) * 512]))
            waitt("tensor", lg)
            waitt("tensor", ld_wd)
            if down_evac[pbk]:
                wait("tensor", "ve", down_evac[pbk])
            ped = 0
            for hc in range(NHC):
                fin = (hc == NHC - 1)
                mmf = lambda t, hc=hc, buf=buf, pbk=pbk, fin=fin, wdb=wdb: t.matmul(
                    pb[pbk][:, 0:512], gbuf[buf][:, hc * 128:(hc + 1) * 128],
                    wdb[:, hc * 512:(hc + 1) * 512], start=(hc == 0), stop=fin)
                if fin:
                    ped = run("tensor", "pe", mmf)
                else:
                    run_q("tensor", mmf)
            gb_free[buf] = ped
            wd_used[wbuf] = ped
            wait("vector", "pe", ped)
            waitt("vector", lp)
            if len(out_toks) >= 2:
                waitt("vector", out_toks[-2])
            vde = run("vector", "ve", lambda v, pbk=pbk: v.tensor_tensor(
                outstage[pbk][:], pb[pbk][:, 0:512], partbuf[pbk][:], op.add))
            down_evac[pbk] = vde
            wait("sync", "ve", vde)
            out_toks.append(S_out[pbk].dma("sync", lambda s, tc=tc, dg2=dg2, pbk=pbk: s.dma_start(
                out=y_view[tc, :, dg2 * 512:(dg2 + 1) * 512], in_=outstage[pbk][:])))
        if dg2 + 2 < 4:
            wait("gpsimd", "pe", wd_used[wbuf])
            wd_toks[dg2 + 2] = G_wd[wbuf].dma("gpsimd", lambda g, dg2=dg2, wbuf=wbuf: g.dma_start(
                out=wd_bufs[wbuf][:], in_=wdv[:, :, (dg2 + 2) * 512:(dg2 + 3) * 512]))

    waitt("sync", (S_out[0], S_out[0].count))
    waitt("sync", (S_out[1], S_out[1].count))

    def _run_engine(en, entries):
        cur = None
        cur_id = None
        for ph, f in entries:
            if ph != cur:
                if cur is not None:
                    nc.leave_named_scope(cur, cur_id, False)
                cur_id, _ = nc.enter_named_scope(ph, False)
                cur = ph
            f(en)
        if cur is not None:
            nc.leave_named_scope(cur, cur_id, False)

    with nc.Block() as block:
        @block.sync
        def _(en):
            _run_engine(en, E["sync"])

        @block.tensor
        def _(en):
            _run_engine(en, E["tensor"])

        @block.vector
        def _(en):
            _run_engine(en, E["vector"])

        @block.scalar
        def _(en):
            _run_engine(en, E["scalar"])

        @block.gpsimd
        def _(en):
            _run_engine(en, E["gpsimd"])

    stck.close()
    return nc


_CACHE = {}


def _get_nc():
    if "nc" not in _CACHE:
        _CACHE["nc"] = build_program()
    return _CACHE["nc"]


def kernel(L0_attn_x, L0_attn_qkv_w, L0_attn_qkv_norm_w, L0_attn_proj_w,
           L0_attn_proj_norm_w, L0_attn_norm_w, L0_attn_s,
           L0_ffn_w_gate, L0_ffn_w_up, L0_ffn_w_down):
    import ml_dtypes
    x = np.ascontiguousarray(np.asarray(L0_attn_x, dtype=np.float32))
    wqkv_t = np.ascontiguousarray(np.asarray(L0_attn_qkv_w, np.float32).T)
    wproj_t = np.ascontiguousarray(np.asarray(L0_attn_proj_w, np.float32).T)
    wg_t = np.ascontiguousarray(np.asarray(L0_ffn_w_gate, np.float32).T)
    wu_t = np.ascontiguousarray(np.asarray(L0_ffn_w_up, np.float32).T)
    wd_t = np.ascontiguousarray(np.asarray(L0_ffn_w_down, np.float32).T)
    vrows = np.concatenate([
        np.asarray(L0_attn_qkv_norm_w, np.float32).reshape(-1),
        np.asarray(L0_attn_norm_w, np.float32).reshape(-1),
        np.asarray(L0_attn_proj_norm_w, np.float32).reshape(-1)]).reshape(1, 3 * D)
    srow = np.asarray(L0_attn_s, np.float32).reshape(1, H)
    cid = np.eye(P, dtype=ml_dtypes.bfloat16)
    conesrow = np.ones((1, P), np.float32)
    conescol = np.ones((P, 1), np.float32)
    ceps = np.tile(np.array([[1e-5, float(np.finfo(np.float32).eps), 1e-6, 0.0]],
                            np.float32), (P, 1))

    in_maps = []
    for c in range(NCORES):
        b, half = c // 2, c % 2
        sel = np.zeros((P, NCORES), np.float32)
        if half == 1:
            sel[:, c - 1] = 1.0
        in_maps.append({
            "x": np.ascontiguousarray(x[b, half * T:(half + 1) * T, :]),
            "wqkv_t": wqkv_t, "wproj_t": wproj_t,
            "wg_t": wg_t, "wu_t": wu_t, "wd_t": wd_t,
            "vrows": vrows, "srow": srow, "cid": cid,
            "conesrow": conesrow, "conescol": conescol,
            "ceps": ceps, "csel": sel,
        })

    nc = _get_nc()
    res = run_bass_kernel_spmd(nc, in_maps, core_ids=list(range(NCORES)))
    out = np.empty((4, 4096, D), np.float32)
    for c in range(NCORES):
        b, half = c // 2, c % 2
        out[b, half * T:(half + 1) * T, :] = res.results[c]["y"]
    return out
